# version 07
# baseline (speedup 1.0000x reference)
import math
import numpy as np
import ml_dtypes

import concourse.bass as bass
import concourse.mybir as mybir
from concourse.bass_utils import run_bass_kernel_spmd

F32 = mybir.dt.float32
BF16 = mybir.dt.bfloat16
I32 = mybir.dt.int32
AF = mybir.ActivationFunctionType
ALU = mybir.AluOpType
BF = ml_dtypes.bfloat16

NCORE = 8
TPC = 1024
TPH = TPC + 3
C = 128
NCH = TPC // C
NB = 32
DK = DV = 128
NKH, NVH = 16, 32
DM = 2048
HID = 8192
NDC = DM // 128
L2_EPS = 1e-6
RMS_EPS = 1e-6
LN_EPS = 1e-5
NEG = -1e30
MAGIC = 0x5F3759DF
T3S = [(0, 343), (343, 343), (686, 341)]


class Tok:
    __slots__ = ("eng", "idx", "dma", "sem_val", "observed")

    def __init__(self, eng, idx, dma):
        self.eng, self.idx, self.dma = eng, idx, dma
        self.sem_val = None
        self.observed = False


class Emitter:
    CENGS = ("tensor", "vector", "scalar", "gpsimd", "sync")

    def __init__(self, nc):
        self.nc = nc
        self.ops = {e: [] for e in self.CENGS}
        self.last_writer = {}
        self.readers = {}
        self.phase_deps = []

    def emit(self, eng, fn, reads=(), writes=(), deps=(), dma=False):
        t = Tok(eng, len(self.ops[eng]), dma)
        dl = list(deps) + list(self.phase_deps)
        for r in reads:
            w = self.last_writer.get(r)
            if w is not None:
                dl.append(w)
            self.readers.setdefault(r, []).append(t)
        for r in writes:
            for rd in self.readers.get(r, ()):
                dl.append(rd)
            w = self.last_writer.get(r)
            if w is not None:
                dl.append(w)
            self.readers[r] = []
            self.last_writer[r] = t
        dl2, seen = [], set()
        for d in dl:
            if d is t or id(d) in seen:
                continue
            seen.add(id(d))
            if (not d.dma) and d.eng == eng:
                continue
            dl2.append(d)
        self.ops[eng].append((t, fn, dl2))
        return t

    def phase_barrier(self):
        self.phase_deps = [ops[-1][0] for ops in self.ops.values() if ops]

    def finalize(self, block):
        for e in self.CENGS:
            for (t, fn, dl) in self.ops[e]:
                for d in dl:
                    d.observed = True
        for e in self.CENGS:
            cc, dc = 0, 0
            for (t, fn, dl) in self.ops[e]:
                if t.dma:
                    dc += 16
                    t.sem_val = dc
                    t.observed = True
                elif t.observed:
                    cc += 1
                    t.sem_val = cc
        nc = self.nc
        self.sems = {}
        for e in self.CENGS:
            self.sems[("c", e)] = nc.alloc_semaphore(f"sc_{e}")
            self.sems[("d", e)] = nc.alloc_semaphore(f"sd_{e}")
        em = self

        def run_engine(engobj, ename):
            hw = {}
            for (t, fn, dl) in em.ops[ename]:
                for d in dl:
                    key = ("d" if d.dma else "c", d.eng)
                    v = d.sem_val
                    if hw.get(key, 0) < v:
                        hw[key] = v
                        engobj.wait_ge(em.sems[key], v)
                inst = fn(engobj)
                if t.dma:
                    inst.then_inc(em.sems[("d", ename)], 16)
                elif t.observed:
                    if inst is None:
                        engobj.sem_inc(em.sems[("c", ename)], 1)
                    else:
                        inst.then_inc(em.sems[("c", ename)], 1)

        block.tensor(lambda t: run_engine(t, "tensor"))
        block.vector(lambda v: run_engine(v, "vector"))
        block.scalar(lambda s: run_engine(s, "scalar"))
        block.gpsimd(lambda g: run_engine(g, "gpsimd"))
        block.sync(lambda s: run_engine(s, "sync"))


def _quake(v, out, vin, ti, t1, t2, t3, n_newton=3, final_scale=None):
    v.tensor_scalar(ti, vin.bitcast(I32), 1, None, ALU.arith_shift_right)
    v.tensor_scalar(ti, ti, -1, None, ALU.bitwise_xor)
    v.tensor_scalar(ti, ti, MAGIC + 1, None, ALU.add)
    r_cur = ti.bitcast(F32)
    inst = None
    assert n_newton % 2 == 1
    for it in range(n_newton):
        v.tensor_tensor(t1, r_cur, r_cur, ALU.mult)
        v.tensor_tensor(t1, t1, vin, ALU.mult)
        v.tensor_scalar(t2, t1, -0.5, 1.5, ALU.mult, ALU.add)
        dst = out if it % 2 == 0 else t3
        inst = v.tensor_tensor(dst, r_cur, t2, ALU.mult)
        r_cur = dst
    if final_scale is not None:
        inst = v.tensor_scalar(out, out, final_scale, None, ALU.mult)
    return inst


def _prep_host(inputs):
    x = np.ascontiguousarray(np.asarray(inputs["L0_attn_hidden_states"],
                                        dtype=np.float32)[0])
    Wq = np.asarray(inputs["L0_attn_w_qkvz"], dtype=np.float32)
    Wba = np.asarray(inputs["L0_attn_w_ba"], dtype=np.float32)
    convw = np.asarray(inputs["L0_attn_conv_w"], dtype=np.float32)[:, 0, :]
    dtb = np.asarray(inputs["L0_attn_dt_bias"], dtype=np.float32)
    Alog = np.asarray(inputs["L0_attn_A_log"], dtype=np.float32)
    normw = np.asarray(inputs["L0_attn_norm_w"], dtype=np.float32)
    Wout = np.asarray(inputs["L0_attn_w_out"], dtype=np.float32)
    W1 = np.asarray(inputs["L0_ffn_W1"], dtype=np.float32)
    W2 = np.asarray(inputs["L0_ffn_W2"], dtype=np.float32)
    W3 = np.asarray(inputs["L0_ffn_W3"], dtype=np.float32)

    xT = np.ascontiguousarray(x.T)
    xslices = []
    for cid in range(NCORE):
        lo = TPC * cid
        sl = np.zeros((DM, TPH), np.float32)
        nh = min(3, lo)
        sl[:, 3 - nh:] = xT[:, lo - nh:lo + TPC]
        xslices.append(np.ascontiguousarray(sl))

    WQ = np.ascontiguousarray(Wq.reshape(NDC, 128, 24, 512).transpose(2, 1, 0, 3))
    bcols = np.array([4 * (h // 2) + (h % 2) for h in range(NVH)])
    Wba_p = np.ascontiguousarray(Wba[:, np.concatenate([bcols, bcols + 2])])
    convw_t = np.ascontiguousarray(convw.reshape(64, 128, 4).transpose(1, 0, 2))
    W1T = np.ascontiguousarray(W1.reshape(64, 128, NDC, 128).transpose(0, 2, 3, 1))
    W3T = np.ascontiguousarray(W3.reshape(64, 128, NDC, 128).transpose(0, 2, 3, 1))
    W2T = np.ascontiguousarray(W2.T.reshape(64, 128, NDC, 128).transpose(2, 0, 1, 3))
    WoutT = np.ascontiguousarray(Wout.reshape(NVH, 128, NDC, 128).transpose(2, 0, 1, 3))

    per_core = []
    for cid in range(NCORE):
        oh = np.zeros((128, 8), np.float32)
        oh[:, cid] = 1.0
        per_core.append(dict(
            x_sl=xslices[cid], wq=WQ, wba=Wba_p, convw=convw_t,
            dtb=np.ascontiguousarray(dtb.reshape(32, 1)),
            alog=np.ascontiguousarray(Alog.reshape(32, 1)),
            normw=np.ascontiguousarray(normw.reshape(128, 1)),
            w1t=W1T, w3t=W3T, w2t=W2T, woutt=WoutT, onehot=oh,
        ))
    return per_core


DEBUG = False


def _build(debug=False):
    nc = bass.Bass("TRN2", target_bir_lowering=False)
    em = Emitter(nc)
    E = em.emit
    dbg = {}
    if debug:
        for nm, shp, dt_ in [("dbg_xn", [128, NDC * TPH], BF16),
                             ("dbg_p0", [1, 3 * TPH], F32),
                             ("dbg_rows", [32, 2048], F32),
                             ("dbg_chst", [128, 16384], BF16),
                             ("dbg_rsq", [128, 64], F32),
                             ("dbg_cols", [128, 160], F32),
                             ("dbg_mco", [128, NVH * 128], BF16),
                             ("dbg_bco", [128, NVH * 128], BF16),
                             ("dbg_sin", [128, NVH * 128], BF16),
                             ("dbg_yst", [128, NVH * 512], BF16),
                             ("dbg_ba", [32, 2 * TPH], F32),
                             ("dbg_g", [32, 2112], F32),
                             ("dbg_lr", [32, 512], F32),
                             ("dbg_xg", [128, NVH * 128], BF16),
                             ("dbg_sz", [128, NVH * 128], BF16),
                             ("dbg_r5", [128, 64], F32)]:
            dbg[nm] = nc.dram_tensor(nm, shp, dt_, kind="ExternalOutput").ap()

    d_x = nc.dram_tensor("x_sl", [DM, TPH], F32, kind="ExternalInput").ap()
    d_wq = nc.dram_tensor("wq", [24, 128, NDC, 512], F32, kind="ExternalInput").ap()
    d_wba = nc.dram_tensor("wba", [DM, 64], F32, kind="ExternalInput").ap()
    d_convw = nc.dram_tensor("convw", [128, 64, 4], F32, kind="ExternalInput").ap()
    d_dtb = nc.dram_tensor("dtb", [32, 1], F32, kind="ExternalInput").ap()
    d_alog = nc.dram_tensor("alog", [32, 1], F32, kind="ExternalInput").ap()
    d_normw = nc.dram_tensor("normw", [128, 1], F32, kind="ExternalInput").ap()
    d_w1 = nc.dram_tensor("w1t", [64, NDC, 128, 128], F32, kind="ExternalInput").ap()
    d_w3 = nc.dram_tensor("w3t", [64, NDC, 128, 128], F32, kind="ExternalInput").ap()
    d_w2 = nc.dram_tensor("w2t", [NDC, 64, 128, 128], F32, kind="ExternalInput").ap()
    d_wo = nc.dram_tensor("woutt", [NDC, NVH, 128, 128], F32,
                          kind="ExternalInput").ap()
    d_oh = nc.dram_tensor("onehot", [128, 8], F32, kind="ExternalInput").ap()
    d_out = nc.dram_tensor("out_feat", [DM, TPC], F32, kind="ExternalOutput").ap()
    d_mixed = nc.dram_tensor("mixed_hbm", [64, 128, TPH], BF16).ap()
    d_conv = nc.dram_tensor("conv_hbm", [64, 128, TPC], BF16).ap()
    d_silz = nc.dram_tensor("silz_hbm", [32, 128, TPH], BF16).ap()
    d_chain = nc.dram_tensor("chain_hbm", [NCH, 128, 16384], BF16).ap()
    d_attn = nc.dram_tensor("attn_hbm", [NDC, 128, TPC], BF16).ap()
    d_ccin = nc.dram_tensor("cc_in", [128, 8192], BF16).ap()
    d_ccout = nc.dram_tensor("cc_out", [NCORE, 128, 8192], BF16,
                             addr_space="Shared").ap()

    ax0, ax1 = np.meshgrid(np.arange(C), np.arange(C), indexing="ij")
    c_mask_a = nc.inline_tensor(np.where(ax0 <= ax1, 0.0, NEG).astype(np.float32),
                                "c_mask_a").ap()
    c_mask_m = nc.inline_tensor(np.where(ax1 < ax0, 0.0, -NEG).astype(np.float32),
                                "c_mask_m").ap()
    c_bdneg = nc.inline_tensor((-((ax0 // NB) == (ax1 // NB)).astype(np.float32)
                                ).astype(BF), "c_bdneg").ap()
    c_bdl = nc.inline_tensor((((ax0 // NB) > (ax1 // NB)).astype(np.float32)
                              ).astype(BF), "c_bdl").ap()
    c_idb = nc.inline_tensor(np.eye(128, dtype=np.float32).astype(BF), "c_idb").ap()
    c_idf = nc.inline_tensor(np.eye(128, dtype=np.float32), "c_idf").ap()
    Hm = np.array([[1.0]], np.float32)
    while Hm.shape[0] < 128:
        Hm = np.block([[Hm, Hm], [Hm, -Hm]])
    c_h128 = nc.inline_tensor(Hm.astype(BF), "c_h128").ap()
    c_onescb = nc.inline_tensor(np.ones((128, 1), BF), "c_onescb").ap()
    c_onescf = nc.inline_tensor(np.ones((128, 1), np.float32), "c_onescf").ap()
    c_onesr = nc.inline_tensor(np.ones((32, 128), np.float32), "c_onesr").ap()
    c_onesrb = nc.inline_tensor(np.ones((32, 128), BF), "c_onesrb").ap()
    c_onesf128 = nc.inline_tensor(np.ones((128, 128), np.float32), "c_onesf128").ap()
    c_onesb128 = nc.inline_tensor(np.ones((128, 128), BF), "c_onesb128").ap()
    e127 = np.zeros((128, 1), np.float32); e127[127, 0] = 1.0
    c_e127 = nc.inline_tensor(e127, "c_e127").ap()

    A = nc.alloc_sbuf_tensor
    xn = A("xn", [128, NDC * TPH], BF16)
    s_maska = A("s_maska", [128, 128], F32)
    s_maskm = A("s_maskm", [128, 128], F32)
    s_bdneg = A("s_bdneg", [128, 128], BF16)
    s_bdl = A("s_bdl", [128, 128], BF16)
    s_idb = A("s_idb", [128, 128], BF16)
    s_idf = A("s_idf", [128, 128], F32)
    s_h128 = A("s_h128", [128, 128], BF16)
    s_onescb = A("s_onescb", [128, 1], BF16)
    s_onescf = A("s_onescf", [128, 1], F32)
    s_onesr = A("s_onesr", [32, 128], F32)
    s_onesrb = A("s_onesrb", [32, 128], BF16)
    s_onesf128 = A("s_onesf128", [128, 128], F32)
    s_onesb128 = A("s_onesb128", [128, 128], BF16)
    s_e127 = A("s_e127", [128, 1], F32)
    s_convw = A("s_convw", [128, 256], F32)
    s_dtb = A("s_dtb", [32, 1], F32)
    s_alog = A("s_alog", [32, 1], F32)
    s_normw = A("s_normw", [128, 1], F32)
    s_oh = A("s_oh", [128, 8], F32)
    G = A("G", [32, 1024], F32)
    lnb = A("lnb", [32, 1024], F32)
    s_bb = A("s_bb", [32, TPH], F32)
    s_aa = A("s_aa", [32, TPH], F32)
    q_i = A("q_i", [128, 128], I32)
    q_t1 = A("q_t1", [128, 128], F32)
    q_t2 = A("q_t2", [128, 128], F32)
    q_t3 = A("q_t3", [128, 128], F32)
    S_pp = [A(f"S_{i}", [128, NVH * 128], BF16) for i in range(2)]

    ab = (nc.sbuf_base + 63) & ~63
    cap = nc.sbuf_top
    SZA = 68 * 1024
    OB = SZA

    def at(name, shape, dtype, off):
        esz = {F32: 4, BF16: 2, I32: 4}[dtype]
        end = ab + off + int(np.prod(shape[1:])) * esz
        assert end <= cap, f"{name}: arena overflow {end} > {cap}"
        return nc.alloc_sbuf_tensor_at(name, shape, dtype, offset=ab + off)

    class Bump:
        def __init__(self, base, limit):
            self.off, self.limit = base, limit

        def al(self, name, shape, dtype, align=64):
            esz = {F32: 4, BF16: 2, I32: 4}[dtype]
            self.off = (self.off + align - 1) & ~(align - 1)
            t = at(name, shape, dtype, self.off)
            self.off += int(np.prod(shape[1:])) * esz
            assert ab + self.off <= cap, f"{name}: bump overflow"
            return t

    bA = Bump(0, SZA)
    xf32 = bA.al("xf32", [128, NDC * TPH], F32)
    bA = Bump(0, SZA)
    cvb = bA.al("cvb", [128, 64 * 128], BF16)
    chst = bA.al("chst", [128, 16384], BF16)
    bA = Bump(0, SZA)
    chb = [bA.al(f"chb{i}", [128, 16384], BF16) for i in range(2)]
    bA = Bump(0, SZA)
    hsl = [bA.al(f"hsl{i}", [128, 512], BF16) for i in range(66)]

    b = Bump(OB, cap)
    xsqt = b.al("xsqt", [128, TPH], F32)
    mrow = b.al("mrow", [1, TPH], F32)
    invrow = b.al("invrow", [1, TPH], F32)
    nmrow = b.al("nmrow", [1, TPH], F32)
    vtmp = b.al("vtmp", [1, TPH], F32)
    vti = b.al("vti", [1, TPH], I32)
    vt3 = b.al("vt3", [1, TPH], F32)
    b = Bump(OB, cap)
    wqb = [b.al(f"wqb{i}", [128, NDC * 512], BF16) for i in range(2)]
    qev = [b.al(f"qev{i}", [128, TPH], BF16) for i in range(2)]
    s_wba = b.al("s_wba", [128, NDC * 64], BF16)
    b = Bump(OB, cap)
    mxb = b.al("mxb", [128, 64 * 131], BF16)
    cvo = b.al("cvo", [128, 64 * 128], BF16)
    cvs = b.al("cvs", [128, 64 * 128], BF16)
    b = Bump(OB, cap)
    Mco = [b.al(f"Mco_{i}", [128, NVH * 128], BF16) for i in range(2)]
    bco = [b.al(f"bco_{i}", [128, NVH * 128], BF16) for i in range(2)]
    mco_end = b.off
    ktok = [b.al(f"ktok{i}", [128, 128], BF16) for i in range(2)]
    vtok = [b.al(f"vtok{i}", [128, 128], BF16) for i in range(4)]
    khat = [b.al(f"khat{i}", [128, 128], BF16) for i in range(2)]
    Mn_t = [b.al(f"Mn_t{i}", [128, 128], BF16) for i in range(2)]
    At_t = [b.al(f"At_t{i}", [128, 128], BF16) for i in range(2)]
    nbar = [b.al(f"nbar{i}", [128, 128], BF16) for i in range(2)]
    napp = [b.al(f"napp{i}", [128, 128], BF16) for i in range(2)]
    mln = [b.al(f"mln{i}", [128, 128], BF16) for i in range(2)]
    Lapp = [[b.al(f"Lapp{p}_{i}", [128, 128], BF16) for i in range(4)]
            for p in range(2)]
    Lnrm = [[b.al(f"Lnrm{p}_{i}", [128, 128], BF16) for i in range(4)]
            for p in range(2)]
    dcur = [[b.al(f"dcur{p}_{i}", [128, 128], BF16) for i in range(2)]
            for p in range(2)]
    negG = [b.al(f"negG{i}", [128, 128], BF16) for i in range(2)]
    rhsb = [b.al(f"rhsb{i}", [128, 256], BF16) for i in range(2)]
    zev = [[b.al(f"zev{p}_{i}", [128, 256], BF16) for i in range(2)]
           for p in range(2)]
    y0t = [b.al(f"y0t{i}", [128, 256], BF16) for i in range(2)]
    etmp = [b.al(f"etmp{i}", [128, 128], F32) for i in range(2)]
    dmin = [b.al(f"dmin{i}", [128, 128], F32) for i in range(2)]
    dexp = [b.al(f"dexp{i}", [128, 128], BF16) for i in range(2)]
    sq_t = b.al("sq_t", [128, 128], BF16)
    nrm2 = b.al("nrm2", [128, 64], F32)
    rsq = b.al("rsq", [128, 64], F32)
    rqr = b.al("rqr", [32, 128], F32)
    rkr = b.al("rkr", [32, 128], F32)
    lnrq = b.al("lnrq", [32, 128], F32)
    lnrk = b.al("lnrk", [32, 128], F32)
    growk = b.al("growk", [32, 128], F32)
    gmgc = b.al("gmgc", [32, 128], F32)
    mcrow = b.al("mcrow", [32, 128], F32)
    colsF = b.al("colsF", [128, 128], F32)
    khcol = b.al("khcol", [128, 32], F32)
    bgcol = b.al("bgcol", [128, 32], F32)
    bbcol = b.al("bbcol", [128, 32], F32)
    tcol = b.al("tcol", [128, 32], F32)
    arow = b.al("arow", [32, 128], F32)
    gcI = [b.al(f"gcI{i}", [128, 128], BF16) for i in range(2)]
    dgt = [[b.al(f"dgt{p}_{i}", [128, 128], F32) for i in range(2)]
           for p in range(2)]
    dgb = [b.al(f"dgb{i}", [128, 128], BF16) for i in range(2)]
    cols2 = b.al("cols2", [128, 32], F32)
    grqcol = b.al("grqcol", [128, 32], F32)
    gamcols = b.al("gamcols", [128, 32], F32)
    gcsel = [b.al(f"gcsel{i}", [128, 1], F32) for i in range(2)]
    grow = b.al("grow", [32, 1024], F32)
    grow2 = b.al("grow2", [32, 1024], F32)
    b = Bump(OB, cap)
    Sacc = [b.al(f"Sacc{i}", [128, NVH * 128], F32) for i in range(2)]
    assert b.off - OB <= mco_end - OB
    b = Bump(mco_end, cap)
    gbuf = [b.al(f"gbuf{i}", [128, 8192], BF16) for i in range(2)]
    b = Bump(OB, cap)
    szb = [b.al(f"szb{i}", [128, NVH * 128], BF16) for i in range(2)]
    xgst = b.al("xgst", [128, NVH * 128], BF16)
    yst = b.al("yst", [128, NVH * 512], BF16)
    wob = [b.al("wob0", [128, NVH * 128], BF16)]
    astg = b.al("astg", [128, 512], BF16)
    rmsr = b.al("rmsr", [32, 128], F32)
    rsq5 = b.al("rsq5", [128, 64], F32)
    nrm5 = b.al("nrm5", [128, 64], F32)
    p5dg = [b.al(f"p5dg{i}", [128, 128], BF16) for i in range(2)]
    sqt5 = [b.al(f"sqt5{i}", [128, 128], BF16) for i in range(2)]
    b = Bump(OB, cap)
    w1b = [b.al(f"w1b{i}", [128, NDC * 128], BF16) for i in range(2)]
    w3b = [b.al(f"w3b{i}", [128, NDC * 128], BF16) for i in range(2)]
    w2b = [b.al(f"w2b{i}", [128, 64 * 128], BF16) for i in range(2)]
    gsb = b.al("gsb", [128, 512], BF16)
    xrs = [b.al(f"xrs{i}", [128, 512], F32) for i in range(2)]
    atb = [b.al(f"atb{i}", [128, 512], BF16) for i in range(2)]
    outst = [b.al(f"outst{i}", [128, 512], F32) for i in range(2)]

    PS = [nc.alloc_psum_tensor(f"ps{i}", [128, 512], F32) for i in range(6)]
    PSB = [nc.alloc_psum_tensor(f"psb{i}", [128, 1024], BF16) for i in range(2)]

    def dma(q, dst, src, reads=(), writes=(), deps=()):
        return E(q, lambda e, dst=dst, src=src: e.dma_start(out=dst, in_=src),
                 reads=reads, writes=writes, deps=deps, dma=True)

    dma("sync", s_maska[:], c_mask_a, writes=["maska"])
    dma("sync", s_maskm[:], c_mask_m, writes=["maskm"])
    dma("sync", s_bdneg[:], c_bdneg, writes=["bdneg"])
    dma("sync", s_bdl[:], c_bdl, writes=["bdl"])
    dma("sync", s_idb[:], c_idb, writes=["idb"])
    dma("sync", s_idf[:], c_idf, writes=["idf"])
    dma("sync", s_h128[:], c_h128, writes=["h128"])
    dma("sync", s_onescb[:], c_onescb, writes=["onescb"])
    dma("sync", s_onescf[:], c_onescf, writes=["onescf"])
    dma("sync", s_onesr[:], c_onesr, writes=["onesr"])
    dma("sync", s_onesrb[:], c_onesrb, writes=["onesrb"])
    dma("sync", s_onesf128[:], c_onesf128, writes=["onesf128"])
    dma("sync", s_onesb128[:], c_onesb128, writes=["onesb128"])
    dma("sync", s_e127[:], c_e127, writes=["e127"])
    dma("sync", s_convw[:], d_convw.rearrange("p c k -> p (c k)"), writes=["convw"])
    dma("sync", s_dtb[:], d_dtb, writes=["dtb"])
    dma("sync", s_alog[:], d_alog, writes=["alog"])
    dma("sync", s_normw[:], d_normw, writes=["normw"])
    dma("sync", s_oh[:], d_oh, writes=["onehot"])
    dma("gpsimd", s_wba[:].rearrange("p (dc c) -> p dc c", dc=NDC),
        d_wba.rearrange("(dc p) c -> p dc c", p=128), writes=["wba"])
    dma("sync", xf32[:].rearrange("p (dc t) -> p dc t", dc=NDC),
        d_x.rearrange("(dc p) t -> p dc t", p=128), writes=["xf32"])

    for (t0, tl) in T3S:
        for dc in range(NDC):
            st, sp = (dc == 0), (dc == NDC - 1)
            E("tensor", lambda t, dc=dc, t0=t0, tl=tl, st=st, sp=sp:
              t.matmul(PS[0][0:1, 0:tl], s_onescf[:],
                       xf32[:, dc * TPH + t0:dc * TPH + t0 + tl],
                       start=st, stop=sp),
              reads=["xf32", "onescf"], writes=[("PS", 0)])
        E("vector", lambda v, t0=t0, tl=tl:
          v.tensor_scalar(mrow[0:1, t0:t0 + tl], PS[0][0:1, 0:tl], 1.0 / DM,
                          None, ALU.mult),
          reads=[("PS", 0)], writes=["mrow"])
    for (t0, tl) in T3S:
        for dc in range(NDC):
            E("vector", lambda v, dc=dc, t0=t0, tl=tl:
              v.tensor_tensor(xsqt[:, t0:t0 + tl],
                              xf32[:, dc * TPH + t0:dc * TPH + t0 + tl],
                              xf32[:, dc * TPH + t0:dc * TPH + t0 + tl],
                              ALU.mult),
              reads=["xf32"], writes=["xsqt"])
            st, sp = (dc == 0), (dc == NDC - 1)
            E("tensor", lambda t, t0=t0, tl=tl, st=st, sp=sp:
              t.matmul(PS[1][0:1, 0:tl], s_onescf[:], xsqt[:, t0:t0 + tl],
                       start=st, stop=sp),
              reads=["xsqt", "onescf"], writes=[("PS", 1)])
        E("vector", lambda v, t0=t0, tl=tl:
          v.tensor_scalar(vtmp[0:1, t0:t0 + tl], PS[1][0:1, 0:tl], 1.0 / DM,
                          None, ALU.mult),
          reads=[("PS", 1)], writes=["vtmp"])
    E("vector", lambda v: v.tensor_tensor(nmrow[:], mrow[:], mrow[:], ALU.mult),
      reads=["mrow"], writes=["nmrow"])
    E("vector", lambda v: v.tensor_tensor(vtmp[:], vtmp[:], nmrow[:], ALU.subtract),
      reads=["nmrow"], writes=["vtmp"])
    E("vector", lambda v: v.tensor_scalar(vtmp[:], vtmp[:], LN_EPS, None, ALU.add),
      writes=["vtmp"])
    E("vector", lambda v: _quake(v, invrow[:], vtmp[:], vti[:],
                                 nmrow[:], xsqt[0:1, 0:TPH], vt3[:]),
      reads=["vtmp"], writes=["invrow", "nmrow", "xsqt", "vti", "vt3"])
    E("vector", lambda v: v.tensor_tensor(nmrow[:], mrow[:], invrow[:], ALU.mult),
      reads=["mrow", "invrow"], writes=["nmrow"])
    E("vector", lambda v: v.tensor_scalar(nmrow[:], nmrow[:], -1.0, None, ALU.mult),
      writes=["nmrow"])
    for (t0, tl) in T3S:
        E("tensor", lambda t, t0=t0, tl=tl:
          t.matmul(PS[2][:, 0:tl], s_onesr[0:1, :], invrow[0:1, t0:t0 + tl],
                   start=True, stop=True),
          reads=["invrow", "onesr"], writes=[("PS", 2)])
        E("tensor", lambda t, t0=t0, tl=tl:
          t.matmul(PS[3][:, 0:tl], s_onesr[0:1, :], nmrow[0:1, t0:t0 + tl],
                   start=True, stop=True),
          reads=["nmrow", "onesr"], writes=[("PS", 3)])
        for dc in range(NDC):
            E("vector", lambda v, dc=dc, t0=t0, tl=tl:
              v.tensor_tensor(xsqt[:, t0:t0 + tl],
                              xf32[:, dc * TPH + t0:dc * TPH + t0 + tl],
                              PS[2][:, 0:tl], ALU.mult),
              reads=["xf32", ("PS", 2)], writes=["xsqt"])
            E("vector", lambda v, dc=dc, t0=t0, tl=tl:
              v.tensor_tensor(xn[:, dc * TPH + t0:dc * TPH + t0 + tl],
                              xsqt[:, t0:t0 + tl], PS[3][:, 0:tl], ALU.add),
              reads=["xsqt", ("PS", 3)], writes=["xn"])

    em.phase_barrier()
    if debug:
        dma("sync", dbg["dbg_xn"], xn[:])
        dma("sync", dbg["dbg_p0"][0:1, 0:TPH], mrow[:])
        dma("sync", dbg["dbg_p0"][0:1, TPH:2 * TPH], invrow[:])
        dma("sync", dbg["dbg_p0"][0:1, 2 * TPH:3 * TPH], vtmp[:])
        em.phase_barrier()

    def spill_dst(cc):
        kh, r = divmod(cc, 6)
        if r == 0:
            return d_mixed[kh]
        if r == 1:
            return d_mixed[16 + kh]
        if r in (2, 3):
            return d_mixed[32 + 2 * kh + (r - 2)]
        return None

    for wg in range(24):
        wp = wg % 2
        dma("gpsimd", wqb[wp][:], d_wq[wg].rearrange("p dc c -> p (dc c)"),
            writes=[("wqb", wp)])
        for oc4 in range(4):
            cc = wg * 4 + oc4
            kh, r = divmod(cc, 6)
            qp = cc % 2
            bank = 4 + (cc % 2)
            for (t0, tl) in T3S:
                for dc in range(NDC):
                    st, sp = (dc == 0), (dc == NDC - 1)
                    E("tensor", lambda t, dc=dc, oc4=oc4, t0=t0, tl=tl, st=st,
                      sp=sp, wp=wp, bank=bank:
                      t.matmul(PS[bank][:, 0:tl],
                               wqb[wp][:, dc * 512 + oc4 * 128:
                                       dc * 512 + (oc4 + 1) * 128],
                               xn[:, dc * TPH + t0:dc * TPH + t0 + tl],
                               start=st, stop=sp),
                      reads=[("wqb", wp), "xn"], writes=[("PS", bank)])
                if r in (4, 5):
                    E("scalar", lambda s, t0=t0, tl=tl, qp=qp, bank=bank:
                      s.activation(qev[qp][:, t0:t0 + tl], PS[bank][:, 0:tl],
                                   AF.Silu),
                      reads=[("PS", bank)], writes=[("qev", qp)])
                else:
                    E("vector", lambda v, t0=t0, tl=tl, qp=qp, bank=bank:
                      v.tensor_copy(qev[qp][:, t0:t0 + tl], PS[bank][:, 0:tl]),
                      reads=[("PS", bank)], writes=[("qev", qp)])
            if r in (4, 5):
                dst = d_silz[2 * kh + (r - 4)]
            else:
                dst = spill_dst(cc)
            dma("sync", dst, qev[qp][:], reads=[("qev", qp)],
                writes=[("spill", cc)])
    for bh in range(2):
        for (t0, tl) in T3S:
            for dc in range(NDC):
                st, sp = (dc == 0), (dc == NDC - 1)
                E("tensor", lambda t, dc=dc, t0=t0, tl=tl, st=st, sp=sp, bh=bh:
                  t.matmul(PS[0][0:32, 0:tl],
                           s_wba[:, dc * 64 + bh * 32:dc * 64 + bh * 32 + 32],
                           xn[:, dc * TPH + t0:dc * TPH + t0 + tl],
                           start=st, stop=sp),
                  reads=["wba", "xn"], writes=[("PS", 0)])
            E("vector", lambda v, t0=t0, tl=tl, bh=bh:
              v.tensor_copy((s_bb if bh == 0 else s_aa)[:, t0:t0 + tl],
                            PS[0][0:32, 0:tl]),
              reads=[("PS", 0)], writes=["ba"])

    em.phase_barrier()

    for cc8 in range(NCH):
        dma("sync", mxb[:].rearrange("p (c t) -> p c t", c=64),
            d_mixed[:, :, cc8 * C:cc8 * C + 131].transpose([1, 0, 2]),
            reads=[("spill", 0)], writes=["mxb"])
        for c64 in range(64):
            def fconv(v, c64=c64):
                o = cvo[:, c64 * 128:(c64 + 1) * 128]
                m0 = mxb[:, c64 * 131:c64 * 131 + 128]
                m1 = mxb[:, c64 * 131 + 1:c64 * 131 + 129]
                m2 = mxb[:, c64 * 131 + 2:c64 * 131 + 130]
                m3 = mxb[:, c64 * 131 + 3:c64 * 131 + 131]
                v.tensor_scalar(o, m0, s_convw[:, 4 * c64:4 * c64 + 1], None,
                                ALU.mult)
                v.scalar_tensor_tensor(o, m1, s_convw[:, 4 * c64 + 1:4 * c64 + 2],
                                       o, ALU.mult, ALU.add)
                v.scalar_tensor_tensor(o, m2, s_convw[:, 4 * c64 + 2:4 * c64 + 3],
                                       o, ALU.mult, ALU.add)
                return v.scalar_tensor_tensor(
                    o, m3, s_convw[:, 4 * c64 + 3:4 * c64 + 4], o,
                    ALU.mult, ALU.add)
            E("vector", fconv, reads=["mxb", "convw"], writes=[("cvo", c64)])
        for g4 in range(16):
            E("scalar", lambda s, g4=g4:
              s.activation(cvs[:, g4 * 512:(g4 + 1) * 512],
                           cvo[:, g4 * 512:(g4 + 1) * 512], AF.Silu),
              reads=[("cvo", 4 * g4), ("cvo", 4 * g4 + 1), ("cvo", 4 * g4 + 2),
                     ("cvo", 4 * g4 + 3)], writes=[("cvs", g4)])
        dma("sync", d_conv[:, :, cc8 * C:(cc8 + 1) * C].transpose([1, 0, 2]),
            cvs[:].rearrange("p (c t) -> p c t", c=64),
            reads=[("cvs", g) for g in range(16)], writes=[("convspill", cc8)])

    em.phase_barrier()

    br = s_bb[0:32, 3:3 + 1024]
    ar = s_aa[0:32, 3:3 + 1024]
    E("scalar", lambda s: s.activation(grow[:], br, AF.Exp, scale=-1.0),
      reads=["ba"], writes=["grow"])
    E("vector", lambda v: v.tensor_scalar(grow[:], grow[:], 1.0, None, ALU.add),
      writes=["grow"])
    E("scalar", lambda s: s.activation(grow[:], grow[:], AF.Ln), writes=["grow"])
    E("vector", lambda v: v.tensor_scalar(lnb[:], grow[:], -1.0, None, ALU.mult),
      reads=["grow"], writes=["lnbr"])
    E("scalar", lambda s: s.activation(grow[:], ar, AF.Exp, bias=s_dtb[:]),
      reads=["ba", "dtb"], writes=["grow"])
    E("vector", lambda v: v.tensor_scalar(grow[:], grow[:], 1.0, None, ALU.add),
      writes=["grow"])
    E("scalar", lambda s: s.activation(grow[:], grow[:], AF.Ln), writes=["grow"])
    E("scalar", lambda s: s.activation(q_t1[0:32, 0:1], s_alog[:], AF.Exp),
      reads=["alog"], writes=["aexp"])
    E("vector", lambda v: v.tensor_scalar(q_t2[0:32, 0:1], q_t1[0:32, 0:1], -1.0,
                                          None, ALU.mult),
      reads=["aexp"], writes=["negaexp"])
    E("scalar", lambda s: s.activation(grow2[:], grow[:], AF.Copy,
                                       scale=q_t2[0:32, 0:1]),
      reads=["grow", "negaexp"], writes=["grow2"])
    for cc8 in range(NCH):
        E("vector", lambda v, cc8=cc8:
          v.tensor_tensor_scan(G[:, cc8 * C:(cc8 + 1) * C],
                               grow2[:, cc8 * C:(cc8 + 1) * C],
                               grow2[:, cc8 * C:(cc8 + 1) * C],
                               0.0, ALU.add, ALU.bypass),
          reads=["grow2"], writes=[("Grow", cc8)])

    em.phase_barrier()
    if debug:
        dma("sync", dbg["dbg_rows"][:, 0:1024], G[:])
        dma("sync", dbg["dbg_rows"][:, 1024:2048], lnb[:])
        dma("sync", dbg["dbg_ba"][:, 0:TPH], s_bb[:])
        dma("sync", dbg["dbg_ba"][:, TPH:2 * TPH], s_aa[:])
        dma("sync", dbg["dbg_g"][:, 0:1024], grow2[:])
        dma("sync", dbg["dbg_g"][:, 1024:2048], grow[:])
        E("sync", lambda e: e.dma_start(out=dbg["dbg_g"][:, 2048:2049],
                                        in_=q_t1[0:32, 0:1],
                                        allow_slow_non_contiguous=True), dma=True)
        E("sync", lambda e: e.dma_start(out=dbg["dbg_g"][:, 2049:2050],
                                        in_=q_t2[0:32, 0:1],
                                        allow_slow_non_contiguous=True), dma=True)
        em.phase_barrier()

    _p2b(nc, em, locals())
    em.phase_barrier()
    if debug:
        dma("sync", dbg["dbg_chst"], chst[:])
        dma("sync", dbg["dbg_rsq"], rsq[:])
        dma("sync", dbg["dbg_cols"][:, 0:128], colsF[:])
        dma("sync", dbg["dbg_cols"][:, 128:160], cols2[:])
        dma("sync", dbg["dbg_lr"][:, 0:128], rqr[:])
        dma("sync", dbg["dbg_lr"][:, 128:256], rkr[:])
        dma("sync", dbg["dbg_lr"][:, 256:384], lnrq[:])
        dma("sync", dbg["dbg_lr"][:, 384:512], lnrk[:])
        dma("sync", dbg["dbg_mco"], Mco[0][:])
        dma("sync", dbg["dbg_bco"], bco[0][:])
        em.phase_barrier()
    _p3(nc, em, locals())
    em.phase_barrier()
    if debug:
        dma("sync", dbg["dbg_sin"], S_pp[0][:])
        em.phase_barrier()
    _p5(nc, em, locals())
    em.phase_barrier()
    if debug:
        dma("sync", dbg["dbg_yst"], yst[:])
        dma("sync", dbg["dbg_xg"], xgst[:])
        dma("sync", dbg["dbg_sz"], szb[1][:])
        dma("sync", dbg["dbg_r5"][:, 0:32], rsq5[:, 0:32])
        dma("sync", dbg["dbg_r5"][:, 32:64], nrm5[:, 0:32])
        em.phase_barrier()
    _p4(nc, em, locals())

    with nc.Block() as block:
        em.finalize(block)
    nc._em = em
    return nc


def _p2b(nc, em, L):
    E = em.emit
    g = lambda n: L[n]
    (PS, PSB, cvb, chst, sq_t, nrm2, rsq, rqr, rkr, lnrq, lnrk, growk, gmgc,
     mcrow, colsF, khcol, bgcol,
     bbcol, tcol, arow, gcI, ktok, vtok, khat, Mn_t, At_t, nbar,
     napp, mln, Lapp, Lnrm, dcur, negG, rhsb, zev, y0t, etmp, dmin, dexp,
     G, lnb, s_onesr, s_onesrb, s_maska, s_maskm, s_bdneg, s_bdl, s_idb,
     s_onescb, s_onescf, Mco, bco, d_conv, d_chain, q_i, q_t1, q_t2) = (
        g("PS"), g("PSB"), g("cvb"), g("chst"), g("sq_t"), g("nrm2"), g("rsq"),
        g("rqr"), g("rkr"), g("lnrq"), g("lnrk"), g("growk"), g("gmgc"),
        g("mcrow"), g("colsF"), g("khcol"), g("bgcol"),
        g("bbcol"), g("tcol"), g("arow"), g("gcI"),
        g("ktok"), g("vtok"), g("khat"), g("Mn_t"), g("At_t"), g("nbar"),
        g("napp"), g("mln"), g("Lapp"), g("Lnrm"), g("dcur"), g("negG"),
        g("rhsb"), g("zev"), g("y0t"), g("etmp"), g("dmin"), g("dexp"),
        g("G"), g("lnb"), g("s_onesr"), g("s_onesrb"), g("s_maska"),
        g("s_maskm"), g("s_bdneg"), g("s_bdl"), g("s_idb"), g("s_onescb"),
        g("s_onescf"), g("Mco"), g("bco"), g("d_conv"), g("d_chain"),
        g("q_i"), g("q_t1"), g("q_t2"))
    dgt, dgb, cols2, grqcol, gamcols, gcsel = (g("dgt"), g("dgb"), g("cols2"),
                                               g("grqcol"), g("gamcols"),
                                               g("gcsel"))
    s_onesf128, s_onesb128, s_e127 = (g("s_onesf128"), g("s_onesb128"),
                                      g("s_e127"))
    dma = lambda q, dst, src, **kw: E(
        q, lambda e, dst=dst, src=src: e.dma_start(out=dst, in_=src),
        dma=True, **kw)

    for h in range(NVH):
        E("vector", lambda v, h=h:
          v.tensor_copy(Mco[0][:, h * 128:(h + 1) * 128], s_idb[:]),
          reads=["idb"], writes=[("Mco", 0, h)])
    E("vector", lambda v: v.memset(bco[0][:], 0.0),
      writes=[("bco", 0, h2) for h2 in range(NVH)])

    for cc8 in range(NCH):
        dma("sync", cvb[:].rearrange("p (c t) -> p c t", c=64),
            d_conv[:, :, cc8 * C:(cc8 + 1) * C].transpose([1, 0, 2]),
            reads=[("convspill", cc8)], writes=["cvb"])
        tend = cc8 * C + C - 1
        Gsl = G[:, cc8 * C:(cc8 + 1) * C]
        lnbsl = lnb[:, cc8 * C:(cc8 + 1) * C]
        for kh in range(NKH):
            for (which, toff) in ((0, kh * 128), (1, 2048 + kh * 128)):
                E("vector", lambda v, toff=toff:
                  v.tensor_tensor(sq_t[:], cvb[:, toff:toff + 128],
                                  cvb[:, toff:toff + 128], ALU.mult),
                  reads=["cvb"], writes=["sq_t"])
                for dup in range(2):
                    col = (0 if which == 0 else 32) + 2 * kh + dup
                    E("tensor", lambda t, col=col:
                      t.matmul(PS[0][:, col:col + 1], sq_t[:], s_onescb[:],
                               start=True, stop=True),
                      reads=["sq_t", "onescb"], writes=[("PS", 0)])
        E("vector", lambda v: v.tensor_scalar(nrm2[:], PS[0][:, 0:64], L2_EPS,
                                              None, ALU.add),
          reads=[("PS", 0)], writes=["nrm2"])
        E("vector", lambda v: _quake(v, rsq[:], nrm2[:], q_i[:, 0:64],
                                     q_t1[:, 0:64], q_t2[:, 0:64],
                                     L["q_t3"][:, 0:64]),
          reads=["nrm2"], writes=["rsq", "qscr"])
        E("vector", lambda v: v.tensor_scalar(rsq[:, 0:32], rsq[:, 0:32],
                                              1.0 / math.sqrt(DK), None,
                                              ALU.mult),
          writes=["rsq"])
        E("tensor", lambda t: t.transpose(PS[1][0:32, 0:128], rsq[:, 0:32],
                                          L["s_idf"][:]),
          reads=["rsq", "idf"], writes=[("PS", 1)])
        E("vector", lambda v: v.tensor_copy(rqr[:], PS[1][0:32, 0:128]),
          reads=[("PS", 1)], writes=["rqr"])
        E("tensor", lambda t: t.transpose(PS[1][0:32, 128:256], rsq[:, 32:64],
                                          L["s_idf"][:]),
          reads=["rsq", "idf"], writes=[("PS", 1)])
        E("vector", lambda v: v.tensor_copy(rkr[:], PS[1][0:32, 128:256]),
          reads=[("PS", 1)], writes=["rkr"])
        E("scalar", lambda s: s.activation(lnrq[:], rqr[:], AF.Ln),
          reads=["rqr"], writes=["lnrq"])
        E("scalar", lambda s: s.activation(lnrk[:], rkr[:], AF.Ln),
          reads=["rkr"], writes=["lnrk"])
        E("vector", lambda v, Gsl=Gsl:
          v.tensor_tensor(arow[:], Gsl, lnrq[:], ALU.add),
          reads=[("Grow", cc8), "lnrq"], writes=["arow"])
        E("vector", lambda v, Gsl=Gsl:
          v.tensor_tensor(growk[:], Gsl, lnrk[:], ALU.subtract),
          reads=[("Grow", cc8), "lnrk"], writes=["growk"])
        E("gpsimd", lambda g2, Gsl=Gsl, tend=tend:
          g2.tensor_scalar(gmgc[:], Gsl, G[:, tend:tend + 1], None,
                           ALU.subtract),
          reads=[("Grow", cc8)], writes=["gmgc"])
        E("vector", lambda v, Gsl=Gsl, lnbsl=lnbsl:
          v.tensor_tensor(mcrow[:], Gsl, lnbsl, ALU.add),
          reads=[("Grow", cc8), "lnbr"], writes=["mcrow"])
        E("vector", lambda v:
          v.tensor_tensor(mcrow[:], mcrow[:], lnrk[:], ALU.add),
          reads=["lnrk"], writes=["mcrow"])
        E("tensor", lambda t: t.transpose(PS[1][:, 0:32], growk[:],
                                          L["s_idf"][0:32, 0:32]),
          reads=["growk", "idf"], writes=[("PS", 1)])
        E("tensor", lambda t: t.transpose(PS[1][:, 32:64], gmgc[:],
                                          L["s_idf"][0:32, 0:32]),
          reads=["gmgc", "idf"], writes=[("PS", 1)])
        E("tensor", lambda t: t.transpose(PS[1][:, 64:96], mcrow[:],
                                          L["s_idf"][0:32, 0:32]),
          reads=["mcrow", "idf"], writes=[("PS", 1)])
        E("tensor", lambda t, Gsl=Gsl: t.transpose(PS[1][:, 96:128], Gsl,
                                                   L["s_idf"][0:32, 0:32]),
          reads=[("Grow", cc8), "idf"], writes=[("PS", 1)])
        E("tensor", lambda t: t.transpose(PS[1][:, 128:160], arow[:],
                                          L["s_idf"][0:32, 0:32]),
          reads=["arow", "idf"], writes=[("PS", 1)])
        E("scalar", lambda s: s.activation(colsF[:], PS[1][:, 0:128], AF.Copy),
          reads=[("PS", 1)], writes=["colsF"])
        E("scalar", lambda s: s.activation(cols2[:], PS[1][:, 128:160], AF.Copy),
          reads=[("PS", 1)], writes=["cols2"])
        E("scalar", lambda s: s.activation(grqcol[:], cols2[:], AF.Exp),
          reads=["cols2"], writes=["grqcol"])
        E("scalar", lambda s: s.activation(gamcols[:], colsF[:, 96:128], AF.Exp),
          reads=["colsF"], writes=["gamcols"])
        E("scalar", lambda s: s.activation(khcol[:], colsF[:, 32:64], AF.Exp,
                                           scale=-1.0),
          reads=["colsF"], writes=["khcol"])
        E("vector", lambda v: v.tensor_tensor(tcol[:], colsF[:, 64:96],
                                              colsF[:, 0:32], ALU.add),
          reads=["colsF"], writes=["tcol"])
        E("vector", lambda v: v.scalar_tensor_tensor(tcol[:], colsF[:, 96:128],
                                                     -2.0, tcol[:], ALU.mult,
                                                     ALU.add),
          reads=["colsF"], writes=["tcol"])
        E("scalar", lambda s: s.activation(bbcol[:], tcol[:], AF.Exp),
          reads=["tcol"], writes=["bbcol"])
        E("vector", lambda v: v.tensor_tensor(tcol[:], tcol[:], colsF[:, 96:128],
                                              ALU.add),
          reads=["colsF"], writes=["tcol"])
        E("scalar", lambda s: s.activation(bgcol[:], tcol[:], AF.Exp),
          reads=["tcol"], writes=["bgcol"])
        for kh in range(NKH):
            kp = kh % 2
            ksl = cvb[:, 2048 + kh * 128:2048 + (kh + 1) * 128]
            qsl = cvb[:, kh * 128:(kh + 1) * 128]
            E("tensor", lambda t, ksl=ksl, kp=kp:
              t.transpose(PSB[0][:, kp * 384:kp * 384 + 128], ksl,
                          L["s_idb"][:]),
              reads=["cvb", "idb"], writes=[("PSB", 0)])
            E("scalar", lambda s, kh=kh, kp=kp:
              s.activation(ktok[kp][:], PSB[0][:, kp * 384:kp * 384 + 128],
                           AF.Copy, scale=rsq[:, 32 + 2 * kh:33 + 2 * kh]),
              reads=[("PSB", 0), "rsq"], writes=[("ktok", kp)])
            for j in range(2):
                vsl = cvb[:, 4096 + (2 * kh + j) * 128:
                          4096 + (2 * kh + j + 1) * 128]
                E("tensor", lambda t, vsl=vsl, j=j, kp=kp:
                  t.transpose(PSB[0][:, kp * 384 + 128 * (j + 1):
                                      kp * 384 + 128 * (j + 2)], vsl,
                              L["s_idb"][:]),
                  reads=["cvb", "idb"], writes=[("PSB", 0)])
                E("vector", lambda v, j=j, kp=kp:
                  v.tensor_copy(vtok[2 * kp + j][:],
                                PSB[0][:, kp * 384 + 128 * (j + 1):
                                        kp * 384 + 128 * (j + 2)]),
                  reads=[("PSB", 0)], writes=[("vtok", 2 * kp + j)])

            for hh in range(2):
                h = 2 * kh + hh
                p = hh
                A, B, Cz = PS[p], PS[2 + p], PS[4 + p]
                rA, rB, rC = ("PS", p), ("PS", 2 + p), ("PS", 4 + p)
                E("tensor", lambda t, ksl=ksl, A=A:
                  t.matmul(A[:, 0:128], ksl, ksl, start=True, stop=True),
                  reads=["cvb"], writes=[rA])
                E("tensor", lambda t, ksl=ksl, qsl=qsl, A=A:
                  t.matmul(A[:, 128:256], ksl, qsl, start=True, stop=True),
                  reads=["cvb"], writes=[rA])
                E("vector", lambda v, h=h, p=p:
                  v.tensor_scalar(dgt[p][0][:], L["s_idf"][:],
                                  colsF[:, h:h + 1], None, ALU.mult),
                  reads=["idf", "colsF"], writes=[("dgt", p, 0)])
                E("tensor", lambda t, p=p, A=A:
                  t.matmul(A[:, 256:384], s_onesf128[:], dgt[p][0][:],
                           start=True, stop=True),
                  reads=["onesf128", ("dgt", p, 0)], writes=[rA])
                E("vector", lambda v, h=h, p=p:
                  v.tensor_scalar(dgt[p][1][:], L["s_idf"][:],
                                  cols2[:, h:h + 1], None, ALU.mult),
                  reads=["idf", "cols2"], writes=[("dgt", p, 1)])
                E("tensor", lambda t, p=p, A=A:
                  t.matmul(A[:, 384:512], s_onesf128[:], dgt[p][1][:],
                           start=True, stop=True),
                  reads=["onesf128", ("dgt", p, 1)], writes=[rA])
                E("vector", lambda v, h=h, p=p, A=A:
                  v.scalar_tensor_tensor(dmin[p][:], A[:, 256:384],
                                         colsF[:, 64 + h:65 + h], s_maskm[:],
                                         ALU.subtract, ALU.add),
                  reads=[rA, "colsF", "maskm"], writes=[("dmin", p)])
                E("scalar", lambda s, p=p:
                  s.activation(dexp[p][:], dmin[p][:], AF.Exp, scale=-1.0),
                  reads=[("dmin", p)], writes=[("dexp", p)])
                E("vector", lambda v, p=p, A=A:
                  v.tensor_tensor(Mn_t[p][:], A[:, 0:128], dexp[p][:],
                                  ALU.mult),
                  reads=[rA, ("dexp", p)], writes=[("Mn", p)])
                E("vector", lambda v, h=h, p=p, A=A:
                  v.scalar_tensor_tensor(dmin[p][:], A[:, 384:512],
                                         colsF[:, h:h + 1], s_maska[:],
                                         ALU.subtract, ALU.add),
                  reads=[rA, "colsF", "maska"], writes=[("dmin", p)])
                E("scalar", lambda s, p=p:
                  s.activation(dexp[p][:], dmin[p][:], AF.Exp),
                  reads=[("dmin", p)], writes=[("dexp", p)])
                E("vector", lambda v, p=p, A=A:
                  v.tensor_tensor(At_t[p][:], A[:, 128:256], dexp[p][:],
                                  ALU.mult),
                  reads=[rA, ("dexp", p)], writes=[("At", p)])
                E("vector", lambda v, p=p:
                  v.tensor_tensor(nbar[p][:], Mn_t[p][:], s_bdneg[:], ALU.mult),
                  reads=[("Mn", p), "bdneg"], writes=[("nbar", p)])
                E("vector", lambda v, p=p:
                  v.tensor_tensor(mln[p][:], Mn_t[p][:], s_bdl[:], ALU.mult),
                  reads=[("Mn", p), "bdl"], writes=[("mln", p)])
                E("vector", lambda v, p=p: v.transpose(napp[p][:], nbar[p][:]),
                  reads=[("nbar", p)], writes=[("napp", p)])
                capp, cnrm = napp[p], nbar[p]
                rapp, rnrm = ("napp", p), ("nbar", p)
                for lv in range(4):
                    E("tensor", lambda t, capp=capp, cnrm=cnrm, B=B:
                      t.matmul(B[:, 0:128], cnrm[:], capp[:],
                               start=True, stop=True),
                      reads=[rapp, rnrm], writes=[rB])
                    E("tensor", lambda t, capp=capp, cnrm=cnrm, B=B:
                      t.matmul(B[:, 128:256], capp[:], cnrm[:],
                               start=True, stop=True),
                      reads=[rapp, rnrm], writes=[rB])
                    E("vector", lambda v, lv=lv, p=p, B=B:
                      v.tensor_copy(Lapp[p][lv][:], B[:, 0:128]),
                      reads=[rB], writes=[("Lapp", p, lv)])
                    E("vector", lambda v, lv=lv, p=p, B=B:
                      v.tensor_copy(Lnrm[p][lv][:], B[:, 128:256]),
                      reads=[rB], writes=[("Lnrm", p, lv)])
                    capp, cnrm = Lapp[p][lv], Lnrm[p][lv]
                    rapp, rnrm = ("Lapp", p, lv), ("Lnrm", p, lv)
                E("vector", lambda v, p=p:
                  v.tensor_tensor(dcur[p][0][:], napp[p][:], s_idb[:], ALU.add),
                  reads=[("napp", p), "idb"], writes=[("dcur", p, 0)])
                dp = 0
                for lv in range(4):
                    E("tensor", lambda t, lv=lv, dp=dp, p=p, B=B:
                      t.matmul(B[:, 256:384], Lnrm[p][lv][:], dcur[p][dp][:],
                               start=True, stop=False),
                      reads=[("Lnrm", p, lv), ("dcur", p, dp)], writes=[rB])
                    E("tensor", lambda t, dp=dp, p=p, B=B:
                      t.matmul(B[:, 256:384], s_idb[:], dcur[p][dp][:],
                               start=False, stop=True),
                      reads=["idb", ("dcur", p, dp)], writes=[rB])
                    E("vector", lambda v, dp=dp, p=p, B=B:
                      v.tensor_copy(dcur[p][1 - dp][:], B[:, 256:384]),
                      reads=[rB], writes=[("dcur", p, 1 - dp)])
                    dp = 1 - dp
                dapp, rdapp = dcur[p][dp], ("dcur", p, dp)
                E("tensor", lambda t, dapp=dapp, p=p, B=B:
                  t.matmul(B[:, 256:384], mln[p][:], dapp[:],
                           start=True, stop=True),
                  reads=[("mln", p), rdapp], writes=[rB])
                E("vector", lambda v, p=p, B=B:
                  v.tensor_scalar(negG[p][:], B[:, 256:384], -1.0, None,
                                  ALU.mult),
                  reads=[rB], writes=[("negG", p)])
                E("vector", lambda v, hh=hh, h=h, p=p, kp=kp:
                  v.tensor_scalar(rhsb[p][:, 0:128], vtok[2 * kp + hh][:],
                                  bbcol[:, h:h + 1], None, ALU.mult),
                  reads=[("vtok", 2 * kp + hh), "bbcol"], writes=[("rhsb", p)])
                E("vector", lambda v, h=h, p=p, kp=kp:
                  v.tensor_scalar(rhsb[p][:, 128:256], ktok[kp][:],
                                  bgcol[:, h:h + 1], None, ALU.mult),
                  reads=[("ktok", kp), "bgcol"], writes=[("rhsb", p)])
                E("tensor", lambda t, dapp=dapp, p=p, Cz=Cz:
                  t.matmul(Cz[:, 0:256], dapp[:], rhsb[p][:],
                           start=True, stop=True),
                  reads=[rdapp, ("rhsb", p)], writes=[rC])
                E("vector", lambda v, p=p, Cz=Cz:
                  v.tensor_copy(y0t[p][:], Cz[:, 0:256]),
                  reads=[rC], writes=[("y0t", p)])
                zcur, rz = y0t[p], ("y0t", p)
                for it in range(3):
                    sl_z = slice(256, 512) if it % 2 == 0 else slice(0, 256)
                    E("tensor", lambda t, zcur=zcur, p=p, Cz=Cz, sl_z=sl_z:
                      t.matmul(Cz[:, sl_z], negG[p][:], zcur[:],
                               start=True, stop=False),
                      reads=[("negG", p), rz], writes=[rC])
                    E("tensor", lambda t, p=p, Cz=Cz, sl_z=sl_z:
                      t.matmul(Cz[:, sl_z], s_idb[:], y0t[p][:],
                               start=False, stop=True),
                      reads=["idb", ("y0t", p)], writes=[rC])
                    E("vector", lambda v, it=it, p=p, Cz=Cz, sl_z=sl_z:
                      v.tensor_copy(zev[p][it % 2][:], Cz[:, sl_z]),
                      reads=[rC], writes=[("zev", p, it % 2)])
                    zcur, rz = zev[p][it % 2], ("zev", p, it % 2)
                Xt, rX = zcur, rz
                E("vector", lambda v, h=h, p=p, kp=kp:
                  v.tensor_scalar(khat[p][:], ktok[kp][:], khcol[:, h:h + 1],
                                  None, ALU.mult),
                  reads=[("ktok", kp), "khcol"], writes=[("khat", p)])
                E("vector", lambda v, h=h, p=p:
                  v.tensor_scalar(dgb[p][:], s_idb[:], grqcol[:, h:h + 1],
                                  None, ALU.mult),
                  reads=["idb", "grqcol"], writes=[("dgb", p)])
                E("tensor", lambda t, p=p, A=A:
                  t.matmul(A[:, 256:384], s_onesb128[:], dgb[p][:],
                           start=True, stop=True),
                  reads=["onesb128", ("dgb", p)], writes=[rA])
                E("tensor", lambda t, Xt=Xt, p=p, B=B:
                  t.matmul(B[:, 384:512], Xt[:, 128:256], At_t[p][:],
                           start=True, stop=True),
                  reads=[rX, ("At", p)], writes=[rB])
                E("vector", lambda v, qsl=qsl, p=p, A=A:
                  v.tensor_tensor(etmp[p][:], qsl, A[:, 256:384], ALU.mult),
                  reads=["cvb", rA], writes=[("etmp", p)])
                E("vector", lambda v, h=h, p=p, B=B:
                  v.tensor_tensor(chst[:, h * 512:h * 512 + 128], etmp[p][:],
                                  B[:, 384:512], ALU.subtract),
                  reads=[("etmp", p), rB], writes=[("chE", h)])
                E("tensor", lambda t, Xt=Xt, p=p, B=B:
                  t.matmul(B[:, 0:128], Xt[:, 0:128], At_t[p][:],
                           start=True, stop=True),
                  reads=[rX, ("At", p)], writes=[rB])
                E("vector", lambda v, h=h, p=p, B=B:
                  v.tensor_copy(chst[:, h * 512 + 128:h * 512 + 256],
                                B[:, 0:128]),
                  reads=[rB], writes=[("chO", h)])
                E("vector", lambda v, h=h, p=p:
                  v.tensor_scalar(gcsel[p][:], s_e127[:], gamcols[:, h:h + 1],
                                  None, ALU.mult),
                  reads=["e127", "gamcols"], writes=[("gcsel", p)])
                E("tensor", lambda t, p=p, B=B:
                  t.matmul(B[:, 256:257], s_onesf128[:], gcsel[p][:],
                           start=True, stop=True),
                  reads=["onesf128", ("gcsel", p)], writes=[rB])
                E("vector", lambda v, p=p, B=B:
                  v.tensor_scalar(gcI[p][:], s_idb[:], B[:, 256:257], None,
                                  ALU.mult),
                  reads=["idb", rB], writes=[("gcI", p)])
                E("tensor", lambda t, Xt=Xt, p=p, B=B:
                  t.matmul(B[:, 128:256], Xt[:, 128:256], khat[p][:],
                           start=True, stop=True),
                  reads=[rX, ("khat", p)], writes=[rB])
                E("vector", lambda v, h=h, p=p, B=B:
                  v.scalar_tensor_tensor(chst[:, h * 512 + 256:h * 512 + 384],
                                         B[:, 128:256], -1.0, gcI[p][:],
                                         ALU.mult, ALU.add),
                  reads=[rB, ("gcI", p)], writes=[("chM", h)])
                E("tensor", lambda t, Xt=Xt, p=p, B=B:
                  t.matmul(B[:, 128:256], khat[p][:], Xt[:, 0:128],
                           start=True, stop=True),
                  reads=[("khat", p), rX], writes=[rB])
                E("vector", lambda v, h=h, p=p, B=B:
                  v.tensor_copy(chst[:, h * 512 + 384:h * 512 + 512],
                                B[:, 128:256]),
                  reads=[rB], writes=[("chB", h)])
                cp = cc8 % 2
                E("tensor", lambda t, h=h, cp=cp, B=B:
                  t.matmul(B[:, 128:256],
                           chst[:, h * 512 + 256:h * 512 + 384],
                           Mco[cp][:, h * 128:(h + 1) * 128],
                           start=True, stop=True),
                  reads=[("chM", h), ("Mco", cp, h)], writes=[rB])
                E("vector", lambda v, h=h, cp=cp, B=B:
                  v.tensor_copy(Mco[1 - cp][:, h * 128:(h + 1) * 128],
                                B[:, 128:256]),
                  reads=[rB], writes=[("Mco", 1 - cp, h)])
                E("tensor", lambda t, h=h, cp=cp, B=B:
                  t.matmul(B[:, 128:256],
                           chst[:, h * 512 + 256:h * 512 + 384],
                           bco[cp][:, h * 128:(h + 1) * 128],
                           start=True, stop=False),
                  reads=[("chM", h), ("bco", cp, h)], writes=[rB])
                E("tensor", lambda t, h=h, B=B:
                  t.matmul(B[:, 128:256], s_idb[:],
                           chst[:, h * 512 + 384:h * 512 + 512],
                           start=False, stop=True),
                  reads=["idb", ("chB", h)], writes=[rB])
                E("vector", lambda v, h=h, cp=cp, B=B:
                  v.tensor_copy(bco[1 - cp][:, h * 128:(h + 1) * 128],
                                B[:, 128:256]),
                  reads=[rB], writes=[("bco", 1 - cp, h)])
        dma("sync", d_chain[cc8], chst[:],
            reads=[("chE", h2) for h2 in range(NVH)] +
                  [("chO", h2) for h2 in range(NVH)] +
                  [("chM", h2) for h2 in range(NVH)] +
                  [("chB", h2) for h2 in range(NVH)],
            writes=[("chainspill", cc8)])


def _p3(nc, em, L):
    E = em.emit
    PS, PSB = L["PS"], L["PSB"]
    Mco, bco, S_pp, Sacc, gbuf = L["Mco"], L["bco"], L["S_pp"], L["Sacc"], L["gbuf"]
    s_idb, s_oh = L["s_idb"], L["s_oh"]
    d_ccin, d_ccout = L["d_ccin"], L["d_ccout"]
    dma = lambda q, dst, src, **kw: E(
        q, lambda e, dst=dst, src=src: e.dma_start(out=dst, in_=src),
        dma=True, **kw)
    pf = NCH % 2
    for h in range(NVH):
        sl = PSB[0][:, (h % 8) * 128:(h % 8 + 1) * 128]
        E("tensor", lambda t, h=h, sl=sl:
          t.transpose(sl, Mco[pf][:, h * 128:(h + 1) * 128], s_idb[:]),
          reads=[("Mco", pf, h), "idb"], writes=[("PSB", 0)])
        E("vector", lambda v, h=h, sl=sl:
          v.tensor_copy(gbuf[0][:, h * 128:(h + 1) * 128], sl),
          reads=[("PSB", 0)], writes=[("mgT", h)])
    dma("sync", d_ccin[:, 0:4096], gbuf[0][:, 0:4096],
        reads=[("mgT", h) for h in range(NVH)], writes=["ccin"])
    dma("sync", d_ccin[:, 4096:8192], bco[pf][:],
        reads=[("bco", pf, h) for h in range(NVH)], writes=["ccin2"])
    cc_tok = E("gpsimd", lambda g2:
               g2.collective_compute(
                   "AllGather", ALU.bypass,
                   replica_groups=[list(range(NCORE))],
                   ins=[d_ccin[:]], outs=[d_ccout[:]]),
               reads=["ccin", "ccin2"], writes=["ccout"])
    E("vector", lambda v: v.memset(S_pp[0][:], 0.0),
      writes=[("Spre", 0, h) for h in range(NVH)])
    E("vector", lambda v: v.memset(Sacc[0][:], 0.0), writes=[("Sacc", 0)],
      deps=[cc_tok])
    for g in range(NCORE):
        p = g % 2
        dma("sync", gbuf[p][:], d_ccout[g], reads=["ccout"],
            writes=[("gbuf", p)])
        E("vector", lambda v, p=p, g=g:
          v.scalar_tensor_tensor(Sacc[1 - p][:], S_pp[p][:], s_oh[:, g:g + 1],
                                 Sacc[p][:], ALU.mult, ALU.add),
          reads=[("Spre", p, h) for h in range(NVH)] + [("Sacc", p), "onehot"],
          writes=[("Sacc", 1 - p)])
        for h in range(NVH):
            E("tensor", lambda t, p=p, h=h:
              t.matmul(PS[h % 4][:, 0:128], gbuf[p][:, h * 128:(h + 1) * 128],
                       S_pp[p][:, h * 128:(h + 1) * 128],
                       start=True, stop=False),
              reads=[("gbuf", p), ("Spre", p, h)], writes=[("PS", h % 4)])
            E("tensor", lambda t, p=p, h=h:
              t.matmul(PS[h % 4][:, 0:128], s_idb[:],
                       gbuf[p][:, 4096 + h * 128:4096 + (h + 1) * 128],
                       start=False, stop=True),
              reads=["idb", ("gbuf", p)], writes=[("PS", h % 4)])
            E("vector", lambda v, p=p, h=h:
              v.tensor_copy(S_pp[1 - p][:, h * 128:(h + 1) * 128],
                            PS[h % 4][:, 0:128]),
              reads=[("PS", h % 4)], writes=[("Spre", 1 - p, h)])
    E("vector", lambda v: v.tensor_copy(S_pp[0][:], Sacc[0][:]),
      reads=[("Sacc", 0)], writes=[("Sst", 0, h) for h in range(NVH)])


def _p5(nc, em, L):
    E = em.emit
    PS, PSB = L["PS"], L["PSB"]
    S_pp, chb, szb, xgst, yst = L["S_pp"], L["chb"], L["szb"], L["xgst"], L["yst"]
    sqt5 = L["sqt5"]
    wob, astg, rmsr, rsq5, nrm5 = (L["wob"], L["astg"], L["rmsr"], L["rsq5"],
                                   L["nrm5"])
    sq_t, s_idb, s_onescb, s_onesr, s_normw = (L["sq_t"], L["s_idb"],
                                               L["s_onescb"], L["s_onesr"],
                                               L["s_normw"])
    s_idf = L["s_idf"]
    s_onesb128, p5dg = L["s_onesb128"], L["p5dg"]
    q_i, q_t1, q_t2, etmp = L["q_i"], L["q_t1"], L["q_t2"], L["etmp"]
    d_chain, d_silz, d_wo, d_attn = (L["d_chain"], L["d_silz"], L["d_wo"],
                                     L["d_attn"])
    dma = lambda q, dst, src, **kw: E(
        q, lambda e, dst=dst, src=src: e.dma_start(out=dst, in_=src),
        dma=True, **kw)
    for cc8 in range(NCH):
        p = cc8 % 2
        dma("sync", chb[p][:], d_chain[cc8], reads=[("chainspill", cc8)],
            writes=[("chb", p)])
        dma("sync", szb[p][:].rearrange("p (c t) -> p c t", c=32),
            d_silz[:, :, 3 + cc8 * C:3 + (cc8 + 1) * C].transpose([1, 0, 2]),
            reads=[("spill", 4)], writes=[("szb", p)])
        for h in range(NVH):
            hsl = slice(h * 128, (h + 1) * 128)
            hp = h % 2
            Po, Ps_ = PS[hp], PS[2 + hp]
            ro, rs = ("PS", hp), ("PS", 2 + hp)
            E("tensor", lambda t, p=p, h=h, hsl=hsl, Po=Po:
              t.matmul(Po[:, 0:128], S_pp[p][:, hsl],
                       chb[p][:, h * 512:h * 512 + 128], start=True, stop=False),
              reads=[("Sst", p, h), ("chb", p)], writes=[ro])
            E("tensor", lambda t, p=p, h=h, Po=Po:
              t.matmul(Po[:, 0:128], s_idb[:],
                       chb[p][:, h * 512 + 128:h * 512 + 256],
                       start=False, stop=True),
              reads=["idb", ("chb", p)], writes=[ro])
            E("tensor", lambda t, p=p, h=h, hsl=hsl, Ps_=Ps_:
              t.matmul(Ps_[:, 0:128], chb[p][:, h * 512 + 256:h * 512 + 384],
                       S_pp[p][:, hsl], start=True, stop=False),
              reads=[("chb", p), ("Sst", p, h)], writes=[rs])
            E("tensor", lambda t, p=p, h=h, Ps_=Ps_:
              t.matmul(Ps_[:, 0:128], s_idb[:],
                       chb[p][:, h * 512 + 384:h * 512 + 512],
                       start=False, stop=True),
              reads=["idb", ("chb", p)], writes=[rs])
            E("vector", lambda v, p=p, h=h, hsl=hsl, Ps_=Ps_:
              v.tensor_copy(S_pp[1 - p][:, hsl], Ps_[:, 0:128]),
              reads=[rs], writes=[("Sst", 1 - p, h)])
            E("vector", lambda v, p=p, h=h, hsl=hsl, Po=Po:
              v.tensor_tensor(xgst[:, hsl], Po[:, 0:128], szb[p][:, hsl],
                              ALU.mult),
              reads=[ro, ("szb", p)], writes=[("xg", h)])
            E("vector", lambda v, h=h, hp=hp, hsl=hsl:
              v.tensor_tensor(sqt5[hp][:], xgst[:, hsl], xgst[:, hsl],
                              ALU.mult),
              reads=[("xg", h)], writes=[("sqt5", hp)])
            E("tensor", lambda t, h=h, hp=hp:
              t.matmul(PS[4][:, h:h + 1], sqt5[hp][:], s_onescb[:],
                       start=True, stop=True),
              reads=[("sqt5", hp), "onescb"], writes=[("PS", 4)])
        E("vector", lambda v:
          v.tensor_scalar(nrm5[:], PS[4][:, 0:64], 128.0 * RMS_EPS, None,
                          ALU.add),
          reads=[("PS", 4)], writes=["nrm5"])
        E("vector", lambda v:
          _quake(v, rsq5[:], nrm5[:], q_i[:, 0:64], q_t1[:, 0:64],
                 q_t2[:, 0:64], L["q_t3"][:, 0:64],
                 final_scale=math.sqrt(128.0)),
          reads=["nrm5"], writes=["rsq5", "qscr"])
        for h in range(NVH):
            hsl = slice(h * 128, (h + 1) * 128)
            hp = h % 2
            E("scalar", lambda s, h=h, hp=hp:
              s.activation(p5dg[hp][:], s_idb[:], AF.Copy,
                           scale=rsq5[:, h:h + 1]),
              reads=["idb", "rsq5"], writes=[("p5dg", hp)])
            E("tensor", lambda t, hp=hp:
              t.matmul(PS[5][:, hp * 256:hp * 256 + 128], s_onesb128[:],
                       p5dg[hp][:], start=True, stop=True),
              reads=["onesb128", ("p5dg", hp)], writes=[("PS", 5)])
            E("vector", lambda v, h=h, hp=hp, hsl=hsl:
              v.tensor_tensor(q_t1[:, 0:128] if hp == 0 else q_t2[:, 0:128],
                              xgst[:, hsl], PS[5][:, hp * 256:hp * 256 + 128],
                              ALU.mult),
              reads=[("xg", h), ("PS", 5)], writes=[("qscr5", hp)])
            ysl = slice(h * 512 + (cc8 % 4) * 128, h * 512 + (cc8 % 4 + 1) * 128)
            E("vector", lambda v, ysl=ysl, hp=hp:
              v.tensor_scalar(yst[:, ysl],
                              q_t1[:, 0:128] if hp == 0 else q_t2[:, 0:128],
                              s_normw[:], None, ALU.mult),
              reads=[("qscr5", hp), "normw"], writes=[("yst", h, cc8 % 4)])
        if cc8 % 4 == 3:
            half = cc8 // 4
            for dc in range(NDC):
                dma("gpsimd", wob[0][:].rearrange("p (c t) -> p c t", c=32),
                    d_wo[dc].transpose([1, 0, 2]), writes=[("wob", 0)])
                for hv in range(NVH):
                    st, sp = (hv == 0), (hv == NVH - 1)
                    E("tensor", lambda t, hv=hv, st=st, sp=sp:
                      t.matmul(PS[4][:, 0:512], wob[0][:, hv * 128:(hv + 1) * 128],
                               yst[:, hv * 512:(hv + 1) * 512],
                               start=st, stop=sp),
                      reads=[("wob", 0)] + [("yst", hv, j) for j in range(4)],
                      writes=[("PS", 4)])
                E("vector", lambda v: v.tensor_copy(astg[:], PS[4][:, 0:512]),
                  reads=[("PS", 4)], writes=["astg"])
                dma("sync", d_attn[dc, :, half * 512:(half + 1) * 512], astg[:],
                    reads=["astg"], writes=[("attnspill", dc, half)])


def _p4(nc, em, L):
    E = em.emit
    PS = L["PS"]
    hsl_t, w1b, w3b, w2b, gsb = L["hsl"], L["w1b"], L["w3b"], L["w2b"], L["gsb"]
    xrs, atb, outst = L["xrs"], L["atb"], L["outst"]
    xn, s_h128 = L["xn"], L["s_h128"]
    d_w1, d_w3, d_w2, d_x, d_attn, d_out = (L["d_w1"], L["d_w3"], L["d_w2"],
                                            L["d_x"], L["d_attn"], L["d_out"])
    dma = lambda q, dst, src, **kw: E(
        q, lambda e, dst=dst, src=src: e.dma_start(out=dst, in_=src),
        dma=True, **kw)
    out_toks = []
    for tg in range(2):
        toff = 3 + tg * 512
        slot_of = list(range(64))
        free = [64, 65]
        for hc in range(64):
            wp = hc % 2
            dma("gpsimd", w1b[wp][:].rearrange("p (dc t) -> p dc t", dc=NDC),
                d_w1[hc].transpose([1, 0, 2]), writes=[("w1b", wp)])
            dma("gpsimd", w3b[wp][:].rearrange("p (dc t) -> p dc t", dc=NDC),
                d_w3[hc].transpose([1, 0, 2]), writes=[("w3b", wp)])
            for dc in range(NDC):
                st, sp = (dc == 0), (dc == NDC - 1)
                E("tensor", lambda t, wp=wp, dc=dc, st=st, sp=sp, toff=toff:
                  t.matmul(PS[0][:, 0:512], w1b[wp][:, dc * 128:(dc + 1) * 128],
                           xn[:, dc * TPH + toff:dc * TPH + toff + 512],
                           start=st, stop=sp),
                  reads=[("w1b", wp), "xn"], writes=[("PS", 0)])
                E("tensor", lambda t, wp=wp, dc=dc, st=st, sp=sp, toff=toff:
                  t.matmul(PS[1][:, 0:512], w3b[wp][:, dc * 128:(dc + 1) * 128],
                           xn[:, dc * TPH + toff:dc * TPH + toff + 512],
                           start=st, stop=sp),
                  reads=[("w3b", wp), "xn"], writes=[("PS", 1)])
            E("scalar", lambda s: s.activation(gsb[:], PS[0][:, 0:512], AF.Silu),
              reads=[("PS", 0)], writes=["gsb"])
            sl = slot_of[hc]
            E("vector", lambda v, sl=sl:
              v.tensor_tensor(hsl_t[sl][:], PS[1][:, 0:512], gsb[:], ALU.mult),
              reads=[("PS", 1), "gsb"], writes=[("hs", sl)])
        step = 1
        while step < 64:
            nf = []
            base = 0
            while base < 64:
                for off in range(step):
                    a, b = base + off, base + off + step
                    sa, sb = slot_of[a], slot_of[b]
                    d0, d1 = free[0], free[1]
                    E("vector", lambda v, sa=sa, sb=sb, d0=d0:
                      v.tensor_tensor(hsl_t[d0][:], hsl_t[sa][:], hsl_t[sb][:],
                                      ALU.add),
                      reads=[("hs", sa), ("hs", sb)], writes=[("hs", d0)])
                    E("vector", lambda v, sa=sa, sb=sb, d1=d1:
                      v.tensor_tensor(hsl_t[d1][:], hsl_t[sa][:], hsl_t[sb][:],
                                      ALU.subtract),
                      reads=[("hs", sa), ("hs", sb)], writes=[("hs", d1)])
                    slot_of[a], slot_of[b] = d0, d1
                    free = [sa, sb]
                base += 2 * step
            step *= 2
        for hc in range(64):
            sl = slot_of[hc]
            E("tensor", lambda t, sl=sl:
              t.matmul(PS[2 + (sl % 2)][:, 0:512], s_h128[:], hsl_t[sl][:],
                       start=True, stop=True),
              reads=[("hs", sl), "h128"], writes=[("PS", 2 + (sl % 2))])
            E("vector", lambda v, sl=sl:
              v.tensor_scalar(hsl_t[sl][:], PS[2 + (sl % 2)][:, 0:512],
                              1.0 / math.sqrt(HID), None, ALU.mult),
              reads=[("PS", 2 + (sl % 2))], writes=[("hs", sl)])
        for dc in range(NDC):
            wp = dc % 2
            dma("gpsimd", w2b[wp][:].rearrange("p (c t) -> p c t", c=64),
                d_w2[dc].transpose([1, 0, 2]), writes=[("w2b", wp)])
            dma("sync", xrs[wp][:], d_x[dc * 128:(dc + 1) * 128,
                                        3 + tg * 512:3 + tg * 512 + 512],
                writes=[("xrs", wp)])
            dma("sync", atb[wp][:], d_attn[dc, :, tg * 512:(tg + 1) * 512],
                reads=[("attnspill", dc, tg)], writes=[("atb", wp)])
            for hc in range(64):
                st, sp = (hc == 0), (hc == 63)
                sl = slot_of[hc]
                E("tensor", lambda t, wp=wp, hc=hc, sl=sl, st=st, sp=sp:
                  t.matmul(PS[4][:, 0:512], w2b[wp][:, hc * 128:(hc + 1) * 128],
                           hsl_t[sl][:], start=st, stop=sp),
                  reads=[("w2b", wp), ("hs", sl)], writes=[("PS", 4)])
            E("vector", lambda v, wp=wp:
              v.tensor_tensor(outst[wp][:], PS[4][:, 0:512], xrs[wp][:],
                              ALU.add),
              reads=[("PS", 4), ("xrs", wp)], writes=[("outst", wp)])
            E("vector", lambda v, wp=wp:
              v.tensor_tensor(outst[wp][:], outst[wp][:], atb[wp][:], ALU.add),
              reads=[("atb", wp)], writes=[("outst", wp)])
            tk = dma("sync", d_out[dc * 128:(dc + 1) * 128,
                                   tg * 512:(tg + 1) * 512], outst[wp][:],
                     reads=[("outst", wp)], writes=[("outdone", dc, tg)])
            out_toks.append(tk)
    E("gpsimd", lambda g: g.engine_nop(), deps=out_toks)


def kernel(**inputs):
    per_core = _prep_host(inputs)
    nc = _build()
    res = run_bass_kernel_spmd(nc, per_core, core_ids=list(range(NCORE)))
    outs = [np.asarray(res.results[cid]["out_feat"]) for cid in range(NCORE)]
    full = np.concatenate([o.T for o in outs], axis=0)
    return np.ascontiguousarray(full.reshape(1, NCORE * TPC, DM)).astype(np.float32)


# version 08
# speedup vs baseline: 1.0276x; 1.0276x over previous
import math
import numpy as np
import ml_dtypes

import concourse.bass as bass
import concourse.mybir as mybir
from concourse.bass_utils import run_bass_kernel_spmd

F32 = mybir.dt.float32
BF16 = mybir.dt.bfloat16
I32 = mybir.dt.int32
AF = mybir.ActivationFunctionType
ALU = mybir.AluOpType
BF = ml_dtypes.bfloat16

NCORE = 8
TPC = 1024
TPH = TPC + 3
C = 128
NCH = TPC // C
NB = 32
DK = DV = 128
NKH, NVH = 16, 32
DM = 2048
HID = 8192
NDC = DM // 128
L2_EPS = 1e-6
RMS_EPS = 1e-6
LN_EPS = 1e-5
NEG = -1e30
MAGIC = 0x5F3759DF
T3S = [(0, 343), (343, 343), (686, 341)]


class Tok:
    __slots__ = ("eng", "idx", "dma", "sem_val", "observed")

    def __init__(self, eng, idx, dma):
        self.eng, self.idx, self.dma = eng, idx, dma
        self.sem_val = None
        self.observed = False


class Emitter:
    CENGS = ("tensor", "vector", "scalar", "gpsimd", "sync")

    def __init__(self, nc):
        self.nc = nc
        self.ops = {e: [] for e in self.CENGS}
        self.last_writer = {}
        self.readers = {}
        self.phase_deps = []

    def emit(self, eng, fn, reads=(), writes=(), deps=(), dma=False):
        t = Tok(eng, len(self.ops[eng]), dma)
        dl = list(deps) + list(self.phase_deps)
        for r in reads:
            w = self.last_writer.get(r)
            if w is not None:
                dl.append(w)
            self.readers.setdefault(r, []).append(t)
        for r in writes:
            for rd in self.readers.get(r, ()):
                dl.append(rd)
            w = self.last_writer.get(r)
            if w is not None:
                dl.append(w)
            self.readers[r] = []
            self.last_writer[r] = t
        dl2, seen = [], set()
        for d in dl:
            if d is t or id(d) in seen:
                continue
            seen.add(id(d))
            if (not d.dma) and d.eng == eng:
                continue
            dl2.append(d)
        self.ops[eng].append((t, fn, dl2))
        return t

    def phase_barrier(self):
        self.phase_deps = [ops[-1][0] for ops in self.ops.values() if ops]

    def finalize(self, block):
        for e in self.CENGS:
            for (t, fn, dl) in self.ops[e]:
                for d in dl:
                    d.observed = True
        for e in self.CENGS:
            cc, dc = 0, 0
            for (t, fn, dl) in self.ops[e]:
                if t.dma:
                    dc += 16
                    t.sem_val = dc
                    t.observed = True
                elif t.observed:
                    cc += 1
                    t.sem_val = cc
        nc = self.nc
        self.sems = {}
        for e in self.CENGS:
            self.sems[("c", e)] = nc.alloc_semaphore(f"sc_{e}")
            self.sems[("d", e)] = nc.alloc_semaphore(f"sd_{e}")
        em = self

        def run_engine(engobj, ename):
            hw = {}
            for (t, fn, dl) in em.ops[ename]:
                for d in dl:
                    key = ("d" if d.dma else "c", d.eng)
                    v = d.sem_val
                    if hw.get(key, 0) < v:
                        hw[key] = v
                        engobj.wait_ge(em.sems[key], v)
                inst = fn(engobj)
                if t.dma:
                    inst.then_inc(em.sems[("d", ename)], 16)
                elif t.observed:
                    if inst is None:
                        engobj.sem_inc(em.sems[("c", ename)], 1)
                    else:
                        inst.then_inc(em.sems[("c", ename)], 1)

        block.tensor(lambda t: run_engine(t, "tensor"))
        block.vector(lambda v: run_engine(v, "vector"))
        block.scalar(lambda s: run_engine(s, "scalar"))
        block.gpsimd(lambda g: run_engine(g, "gpsimd"))
        block.sync(lambda s: run_engine(s, "sync"))


def _quake(v, out, vin, ti, t1, t2, t3, n_newton=3, final_scale=None):
    v.tensor_scalar(ti, vin.bitcast(I32), 1, None, ALU.arith_shift_right)
    v.tensor_scalar(ti, ti, -1, None, ALU.bitwise_xor)
    v.tensor_scalar(ti, ti, MAGIC + 1, None, ALU.add)
    r_cur = ti.bitcast(F32)
    inst = None
    assert n_newton % 2 == 1
    for it in range(n_newton):
        v.tensor_tensor(t1, r_cur, r_cur, ALU.mult)
        v.tensor_tensor(t1, t1, vin, ALU.mult)
        v.tensor_scalar(t2, t1, -0.5, 1.5, ALU.mult, ALU.add)
        dst = out if it % 2 == 0 else t3
        inst = v.tensor_tensor(dst, r_cur, t2, ALU.mult)
        r_cur = dst
    if final_scale is not None:
        inst = v.tensor_scalar(out, out, final_scale, None, ALU.mult)
    return inst


def _prep_host(inputs):
    x = np.ascontiguousarray(np.asarray(inputs["L0_attn_hidden_states"],
                                        dtype=np.float32)[0])
    Wq = np.asarray(inputs["L0_attn_w_qkvz"], dtype=np.float32)
    Wba = np.asarray(inputs["L0_attn_w_ba"], dtype=np.float32)
    convw = np.asarray(inputs["L0_attn_conv_w"], dtype=np.float32)[:, 0, :]
    dtb = np.asarray(inputs["L0_attn_dt_bias"], dtype=np.float32)
    Alog = np.asarray(inputs["L0_attn_A_log"], dtype=np.float32)
    normw = np.asarray(inputs["L0_attn_norm_w"], dtype=np.float32)
    Wout = np.asarray(inputs["L0_attn_w_out"], dtype=np.float32)
    W1 = np.asarray(inputs["L0_ffn_W1"], dtype=np.float32)
    W2 = np.asarray(inputs["L0_ffn_W2"], dtype=np.float32)
    W3 = np.asarray(inputs["L0_ffn_W3"], dtype=np.float32)

    xT = np.ascontiguousarray(x.T)
    xslices = []
    for cid in range(NCORE):
        lo = TPC * cid
        sl = np.zeros((DM, TPH), np.float32)
        nh = min(3, lo)
        sl[:, 3 - nh:] = xT[:, lo - nh:lo + TPC]
        xslices.append(np.ascontiguousarray(sl))

    WQ = np.ascontiguousarray(Wq.reshape(NDC, 128, 24, 512).transpose(2, 1, 0, 3))
    bcols = np.array([4 * (h // 2) + (h % 2) for h in range(NVH)])
    Wba_p = np.ascontiguousarray(Wba[:, np.concatenate([bcols, bcols + 2])])
    convw_t = np.ascontiguousarray(convw.reshape(64, 128, 4).transpose(1, 0, 2))
    W1T = np.ascontiguousarray(W1.reshape(64, 128, NDC, 128).transpose(0, 2, 3, 1))
    W3T = np.ascontiguousarray(W3.reshape(64, 128, NDC, 128).transpose(0, 2, 3, 1))
    W2T = np.ascontiguousarray(W2.T.reshape(64, 128, NDC, 128).transpose(2, 0, 1, 3))
    WoutT = np.ascontiguousarray(Wout.reshape(NVH, 128, NDC, 128).transpose(2, 0, 1, 3))

    per_core = []
    for cid in range(NCORE):
        oh = np.zeros((128, 8), np.float32)
        oh[:, cid] = 1.0
        per_core.append(dict(
            x_sl=xslices[cid], wq=WQ, wba=Wba_p, convw=convw_t,
            dtb=np.ascontiguousarray(dtb.reshape(32, 1)),
            alog=np.ascontiguousarray(Alog.reshape(32, 1)),
            normw=np.ascontiguousarray(normw.reshape(128, 1)),
            w1t=W1T, w3t=W3T, w2t=W2T, woutt=WoutT, onehot=oh,
        ))
    return per_core


DEBUG = False


def _build(debug=False):
    nc = bass.Bass("TRN2", target_bir_lowering=False)
    em = Emitter(nc)
    E = em.emit
    dbg = {}
    if debug:
        for nm, shp, dt_ in [("dbg_xn", [128, NDC * TPH], BF16),
                             ("dbg_p0", [1, 3 * TPH], F32),
                             ("dbg_rows", [32, 2048], F32),
                             ("dbg_chst", [128, 16384], BF16),
                             ("dbg_rsq", [128, 64], F32),
                             ("dbg_cols", [128, 160], F32),
                             ("dbg_mco", [128, NVH * 128], BF16),
                             ("dbg_bco", [128, NVH * 128], BF16),
                             ("dbg_sin", [128, NVH * 128], BF16),
                             ("dbg_yst", [128, NVH * 512], BF16),
                             ("dbg_ba", [32, 2 * TPH], F32),
                             ("dbg_g", [32, 2112], F32),
                             ("dbg_lr", [32, 512], F32),
                             ("dbg_xg", [128, NVH * 128], BF16),
                             ("dbg_sz", [128, NVH * 128], BF16),
                             ("dbg_r5", [128, 64], F32)]:
            dbg[nm] = nc.dram_tensor(nm, shp, dt_, kind="ExternalOutput").ap()

    d_x = nc.dram_tensor("x_sl", [DM, TPH], F32, kind="ExternalInput").ap()
    d_wq = nc.dram_tensor("wq", [24, 128, NDC, 512], F32, kind="ExternalInput").ap()
    d_wba = nc.dram_tensor("wba", [DM, 64], F32, kind="ExternalInput").ap()
    d_convw = nc.dram_tensor("convw", [128, 64, 4], F32, kind="ExternalInput").ap()
    d_dtb = nc.dram_tensor("dtb", [32, 1], F32, kind="ExternalInput").ap()
    d_alog = nc.dram_tensor("alog", [32, 1], F32, kind="ExternalInput").ap()
    d_normw = nc.dram_tensor("normw", [128, 1], F32, kind="ExternalInput").ap()
    d_w1 = nc.dram_tensor("w1t", [64, NDC, 128, 128], F32, kind="ExternalInput").ap()
    d_w3 = nc.dram_tensor("w3t", [64, NDC, 128, 128], F32, kind="ExternalInput").ap()
    d_w2 = nc.dram_tensor("w2t", [NDC, 64, 128, 128], F32, kind="ExternalInput").ap()
    d_wo = nc.dram_tensor("woutt", [NDC, NVH, 128, 128], F32,
                          kind="ExternalInput").ap()
    d_oh = nc.dram_tensor("onehot", [128, 8], F32, kind="ExternalInput").ap()
    d_out = nc.dram_tensor("out_feat", [DM, TPC], F32, kind="ExternalOutput").ap()
    d_mixed = nc.dram_tensor("mixed_hbm", [64, 128, TPH], BF16).ap()
    d_conv = nc.dram_tensor("conv_hbm", [64, 128, TPC], BF16).ap()
    d_silz = nc.dram_tensor("silz_hbm", [32, 128, TPH], BF16).ap()
    d_chain = nc.dram_tensor("chain_hbm", [NCH, 128, 16384], BF16).ap()
    d_attn = nc.dram_tensor("attn_hbm", [NDC, 128, TPC], BF16).ap()
    d_ccin = nc.dram_tensor("cc_in", [128, 8192], BF16).ap()
    d_ccout = nc.dram_tensor("cc_out", [NCORE, 128, 8192], BF16,
                             addr_space="Shared").ap()

    ax0, ax1 = np.meshgrid(np.arange(C), np.arange(C), indexing="ij")
    c_mask_a = nc.inline_tensor(np.where(ax0 <= ax1, 0.0, NEG).astype(np.float32),
                                "c_mask_a").ap()
    c_mask_m = nc.inline_tensor(np.where(ax1 < ax0, 0.0, -NEG).astype(np.float32),
                                "c_mask_m").ap()
    c_bdneg = nc.inline_tensor((-((ax0 // NB) == (ax1 // NB)).astype(np.float32)
                                ).astype(BF), "c_bdneg").ap()
    c_bdl = nc.inline_tensor((((ax0 // NB) > (ax1 // NB)).astype(np.float32)
                              ).astype(BF), "c_bdl").ap()
    c_idb = nc.inline_tensor(np.eye(128, dtype=np.float32).astype(BF), "c_idb").ap()
    c_idf = nc.inline_tensor(np.eye(128, dtype=np.float32), "c_idf").ap()
    Hm = np.array([[1.0]], np.float32)
    while Hm.shape[0] < 128:
        Hm = np.block([[Hm, Hm], [Hm, -Hm]])
    c_h128 = nc.inline_tensor(Hm.astype(BF), "c_h128").ap()
    c_onescb = nc.inline_tensor(np.ones((128, 1), BF), "c_onescb").ap()
    c_onescf = nc.inline_tensor(np.ones((128, 1), np.float32), "c_onescf").ap()
    c_onesr = nc.inline_tensor(np.ones((32, 128), np.float32), "c_onesr").ap()
    c_onesrb = nc.inline_tensor(np.ones((32, 128), BF), "c_onesrb").ap()
    c_onesf128 = nc.inline_tensor(np.ones((128, 128), np.float32), "c_onesf128").ap()
    c_onesb128 = nc.inline_tensor(np.ones((128, 128), BF), "c_onesb128").ap()
    e127 = np.zeros((128, 1), np.float32); e127[127, 0] = 1.0
    c_e127 = nc.inline_tensor(e127, "c_e127").ap()

    A = nc.alloc_sbuf_tensor
    xn = A("xn", [128, NDC * TPH], BF16)
    s_maska = A("s_maska", [128, 128], F32)
    s_maskm = A("s_maskm", [128, 128], F32)
    s_bdneg = A("s_bdneg", [128, 128], BF16)
    s_bdl = A("s_bdl", [128, 128], BF16)
    s_idb = A("s_idb", [128, 128], BF16)
    s_idf = A("s_idf", [128, 128], F32)
    s_h128 = A("s_h128", [128, 128], BF16)
    s_onescb = A("s_onescb", [128, 1], BF16)
    s_onescf = A("s_onescf", [128, 1], F32)
    s_onesr = A("s_onesr", [32, 128], F32)
    s_onesrb = A("s_onesrb", [32, 128], BF16)
    s_onesf128 = A("s_onesf128", [128, 128], F32)
    s_onesb128 = A("s_onesb128", [128, 128], BF16)
    s_e127 = A("s_e127", [128, 1], F32)
    s_convw = A("s_convw", [128, 256], F32)
    s_dtb = A("s_dtb", [32, 1], F32)
    s_alog = A("s_alog", [32, 1], F32)
    s_normw = A("s_normw", [128, 1], F32)
    s_oh = A("s_oh", [128, 8], F32)
    G = A("G", [32, 1024], F32)
    lnb = A("lnb", [32, 1024], F32)
    s_bb = A("s_bb", [32, TPH], F32)
    s_aa = A("s_aa", [32, TPH], F32)
    q_i = A("q_i", [128, 128], I32)
    q_t1 = A("q_t1", [128, 128], F32)
    q_t2 = A("q_t2", [128, 128], F32)
    q_t3 = A("q_t3", [128, 128], F32)
    S_pp = [A(f"S_{i}", [128, NVH * 128], BF16) for i in range(2)]

    ab = (nc.sbuf_base + 63) & ~63
    cap = nc.sbuf_top
    SZA = 68 * 1024
    OB = SZA

    def at(name, shape, dtype, off):
        esz = {F32: 4, BF16: 2, I32: 4}[dtype]
        end = ab + off + int(np.prod(shape[1:])) * esz
        assert end <= cap, f"{name}: arena overflow {end} > {cap}"
        return nc.alloc_sbuf_tensor_at(name, shape, dtype, offset=ab + off)

    class Bump:
        def __init__(self, base, limit):
            self.off, self.limit = base, limit

        def al(self, name, shape, dtype, align=64):
            esz = {F32: 4, BF16: 2, I32: 4}[dtype]
            self.off = (self.off + align - 1) & ~(align - 1)
            t = at(name, shape, dtype, self.off)
            self.off += int(np.prod(shape[1:])) * esz
            assert ab + self.off <= cap, f"{name}: bump overflow"
            return t

    bA = Bump(0, SZA)
    xf32 = bA.al("xf32", [128, NDC * TPH], F32)
    bA = Bump(0, SZA)
    cvb = bA.al("cvb", [128, 64 * 128], BF16)
    chst = bA.al("chst", [128, 16384], BF16)
    bA = Bump(0, SZA)
    chb = [bA.al(f"chb{i}", [128, 16384], BF16) for i in range(2)]
    bA = Bump(0, SZA)
    hsl = [bA.al(f"hsl{i}", [128, 512], BF16) for i in range(66)]

    b = Bump(OB, cap)
    xsqt = b.al("xsqt", [128, TPH], F32)
    mrow = b.al("mrow", [1, TPH], F32)
    invrow = b.al("invrow", [1, TPH], F32)
    nmrow = b.al("nmrow", [1, TPH], F32)
    vtmp = b.al("vtmp", [1, TPH], F32)
    vti = b.al("vti", [1, TPH], I32)
    vt3 = b.al("vt3", [1, TPH], F32)
    b = Bump(OB, cap)
    wqb = [b.al(f"wqb{i}", [128, NDC * 512], BF16) for i in range(2)]
    qev = [b.al(f"qev{i}", [128, TPH], BF16) for i in range(2)]
    s_wba = b.al("s_wba", [128, NDC * 64], BF16)
    b = Bump(OB, cap)
    mxb = b.al("mxb", [128, 64 * 131], BF16)
    cvo = b.al("cvo", [128, 64 * 128], BF16)
    cvs = b.al("cvs", [128, 64 * 128], BF16)
    b = Bump(OB, cap)
    Mco = [b.al(f"Mco_{i}", [128, NVH * 128], BF16) for i in range(2)]
    bco = [b.al(f"bco_{i}", [128, NVH * 128], BF16) for i in range(2)]
    mco_end = b.off
    ktok = [b.al(f"ktok{i}", [128, 128], BF16) for i in range(2)]
    vtok = [b.al(f"vtok{i}", [128, 128], BF16) for i in range(4)]
    khat = [b.al(f"khat{i}", [128, 128], BF16) for i in range(2)]
    Mn_t = [b.al(f"Mn_t{i}", [128, 128], BF16) for i in range(2)]
    At_t = [b.al(f"At_t{i}", [128, 128], BF16) for i in range(2)]
    nbar = [b.al(f"nbar{i}", [128, 128], BF16) for i in range(2)]
    napp = [b.al(f"napp{i}", [128, 128], BF16) for i in range(2)]
    mln = [b.al(f"mln{i}", [128, 128], BF16) for i in range(2)]
    Lapp = [[b.al(f"Lapp{p}_{i}", [128, 128], BF16) for i in range(4)]
            for p in range(2)]
    Lnrm = [[b.al(f"Lnrm{p}_{i}", [128, 128], BF16) for i in range(4)]
            for p in range(2)]
    dcur = [[b.al(f"dcur{p}_{i}", [128, 128], BF16) for i in range(2)]
            for p in range(2)]
    negG = [b.al(f"negG{i}", [128, 128], BF16) for i in range(2)]
    rhsb = [b.al(f"rhsb{i}", [128, 256], BF16) for i in range(2)]
    zev = [[b.al(f"zev{p}_{i}", [128, 256], BF16) for i in range(2)]
           for p in range(2)]
    y0t = [b.al(f"y0t{i}", [128, 256], BF16) for i in range(2)]
    etmp = [b.al(f"etmp{i}", [128, 128], F32) for i in range(2)]
    dmin = [b.al(f"dmin{i}", [128, 128], F32) for i in range(2)]
    dexp = [b.al(f"dexp{i}", [128, 128], BF16) for i in range(2)]
    sq_t = b.al("sq_t", [128, 512], BF16)
    nrm2 = b.al("nrm2", [128, 64], F32)
    rsq = b.al("rsq", [128, 64], F32)
    rqr = b.al("rqr", [32, 128], F32)
    rkr = b.al("rkr", [32, 128], F32)
    lnrq = b.al("lnrq", [32, 128], F32)
    lnrk = b.al("lnrk", [32, 128], F32)
    growk = b.al("growk", [32, 128], F32)
    gmgc = b.al("gmgc", [32, 128], F32)
    mcrow = b.al("mcrow", [32, 128], F32)
    colsF = b.al("colsF", [128, 128], F32)
    khcol = b.al("khcol", [128, 32], F32)
    bgcol = b.al("bgcol", [128, 32], F32)
    bbcol = b.al("bbcol", [128, 32], F32)
    tcol = b.al("tcol", [128, 32], F32)
    arow = b.al("arow", [32, 128], F32)
    gcI = [b.al(f"gcI{i}", [128, 128], BF16) for i in range(2)]
    dgt = [[b.al(f"dgt{p}_{i}", [128, 128], F32) for i in range(2)]
           for p in range(2)]
    dgb = [b.al(f"dgb{i}", [128, 128], BF16) for i in range(2)]
    cols2 = b.al("cols2", [128, 32], F32)
    grqcol = b.al("grqcol", [128, 32], F32)
    gamcols = b.al("gamcols", [128, 32], F32)
    gcsel = [b.al(f"gcsel{i}", [128, 1], F32) for i in range(2)]
    grow = b.al("grow", [32, 1024], F32)
    grow2 = b.al("grow2", [32, 1024], F32)
    b = Bump(OB, cap)
    Sacc = [b.al(f"Sacc{i}", [128, NVH * 128], F32) for i in range(2)]
    assert b.off - OB <= mco_end - OB
    b = Bump(mco_end, cap)
    gbuf = [b.al(f"gbuf{i}", [128, 8192], BF16) for i in range(2)]
    b = Bump(OB, cap)
    szb = [b.al(f"szb{i}", [128, NVH * 128], BF16) for i in range(2)]
    xgst = b.al("xgst", [128, NVH * 128], BF16)
    yst = b.al("yst", [128, NVH * 512], BF16)
    wob = [b.al("wob0", [128, NVH * 128], BF16)]
    astg = b.al("astg", [128, 512], BF16)
    rmsr = b.al("rmsr", [32, 128], F32)
    rsq5 = b.al("rsq5", [128, 64], F32)
    nrm5 = b.al("nrm5", [128, 64], F32)
    p5dg = [b.al(f"p5dg{i}", [128, 128], BF16) for i in range(2)]
    sqt5 = [b.al(f"sqt5{i}", [128, 128], BF16) for i in range(2)]
    b = Bump(OB, cap)
    w1b = [b.al(f"w1b{i}", [128, NDC * 128], BF16) for i in range(2)]
    w3b = [b.al(f"w3b{i}", [128, NDC * 128], BF16) for i in range(2)]
    w2b = [b.al(f"w2b{i}", [128, 64 * 128], BF16) for i in range(2)]
    gsb = b.al("gsb", [128, 512], BF16)
    xrs = [b.al(f"xrs{i}", [128, 512], F32) for i in range(2)]
    atb = [b.al(f"atb{i}", [128, 512], BF16) for i in range(2)]
    outst = [b.al(f"outst{i}", [128, 512], F32) for i in range(2)]

    PS = [nc.alloc_psum_tensor(f"ps{i}", [128, 512], F32) for i in range(6)]
    PSB = [nc.alloc_psum_tensor(f"psb{i}", [128, 1024], BF16) for i in range(2)]

    def dma(q, dst, src, reads=(), writes=(), deps=()):
        return E(q, lambda e, dst=dst, src=src: e.dma_start(out=dst, in_=src),
                 reads=reads, writes=writes, deps=deps, dma=True)

    dma("sync", s_maska[:], c_mask_a, writes=["maska"])
    dma("sync", s_maskm[:], c_mask_m, writes=["maskm"])
    dma("sync", s_bdneg[:], c_bdneg, writes=["bdneg"])
    dma("sync", s_bdl[:], c_bdl, writes=["bdl"])
    dma("sync", s_idb[:], c_idb, writes=["idb"])
    dma("sync", s_idf[:], c_idf, writes=["idf"])
    dma("sync", s_h128[:], c_h128, writes=["h128"])
    dma("sync", s_onescb[:], c_onescb, writes=["onescb"])
    dma("sync", s_onescf[:], c_onescf, writes=["onescf"])
    dma("sync", s_onesr[:], c_onesr, writes=["onesr"])
    dma("sync", s_onesrb[:], c_onesrb, writes=["onesrb"])
    dma("sync", s_onesf128[:], c_onesf128, writes=["onesf128"])
    dma("sync", s_onesb128[:], c_onesb128, writes=["onesb128"])
    dma("sync", s_e127[:], c_e127, writes=["e127"])
    dma("sync", s_convw[:], d_convw.rearrange("p c k -> p (c k)"), writes=["convw"])
    dma("sync", s_dtb[:], d_dtb, writes=["dtb"])
    dma("sync", s_alog[:], d_alog, writes=["alog"])
    dma("sync", s_normw[:], d_normw, writes=["normw"])
    dma("sync", s_oh[:], d_oh, writes=["onehot"])
    dma("gpsimd", s_wba[:].rearrange("p (dc c) -> p dc c", dc=NDC),
        d_wba.rearrange("(dc p) c -> p dc c", p=128), writes=["wba"])
    dma("sync", xf32[:].rearrange("p (dc t) -> p dc t", dc=NDC),
        d_x.rearrange("(dc p) t -> p dc t", p=128), writes=["xf32"])

    for (t0, tl) in T3S:
        for dc in range(NDC):
            st, sp = (dc == 0), (dc == NDC - 1)
            E("tensor", lambda t, dc=dc, t0=t0, tl=tl, st=st, sp=sp:
              t.matmul(PS[0][0:1, 0:tl], s_onescf[:],
                       xf32[:, dc * TPH + t0:dc * TPH + t0 + tl],
                       start=st, stop=sp),
              reads=["xf32", "onescf"], writes=[("PS", 0)])
        E("vector", lambda v, t0=t0, tl=tl:
          v.tensor_scalar(mrow[0:1, t0:t0 + tl], PS[0][0:1, 0:tl], 1.0 / DM,
                          None, ALU.mult),
          reads=[("PS", 0)], writes=["mrow"])
    for (t0, tl) in T3S:
        for dc in range(NDC):
            E("vector", lambda v, dc=dc, t0=t0, tl=tl:
              v.tensor_tensor(xsqt[:, t0:t0 + tl],
                              xf32[:, dc * TPH + t0:dc * TPH + t0 + tl],
                              xf32[:, dc * TPH + t0:dc * TPH + t0 + tl],
                              ALU.mult),
              reads=["xf32"], writes=["xsqt"])
            st, sp = (dc == 0), (dc == NDC - 1)
            E("tensor", lambda t, t0=t0, tl=tl, st=st, sp=sp:
              t.matmul(PS[1][0:1, 0:tl], s_onescf[:], xsqt[:, t0:t0 + tl],
                       start=st, stop=sp),
              reads=["xsqt", "onescf"], writes=[("PS", 1)])
        E("vector", lambda v, t0=t0, tl=tl:
          v.tensor_scalar(vtmp[0:1, t0:t0 + tl], PS[1][0:1, 0:tl], 1.0 / DM,
                          None, ALU.mult),
          reads=[("PS", 1)], writes=["vtmp"])
    E("vector", lambda v: v.tensor_tensor(nmrow[:], mrow[:], mrow[:], ALU.mult),
      reads=["mrow"], writes=["nmrow"])
    E("vector", lambda v: v.tensor_tensor(vtmp[:], vtmp[:], nmrow[:], ALU.subtract),
      reads=["nmrow"], writes=["vtmp"])
    E("vector", lambda v: v.tensor_scalar(vtmp[:], vtmp[:], LN_EPS, None, ALU.add),
      writes=["vtmp"])
    E("vector", lambda v: _quake(v, invrow[:], vtmp[:], vti[:],
                                 nmrow[:], xsqt[0:1, 0:TPH], vt3[:]),
      reads=["vtmp"], writes=["invrow", "nmrow", "xsqt", "vti", "vt3"])
    E("vector", lambda v: v.tensor_tensor(nmrow[:], mrow[:], invrow[:], ALU.mult),
      reads=["mrow", "invrow"], writes=["nmrow"])
    E("vector", lambda v: v.tensor_scalar(nmrow[:], nmrow[:], -1.0, None, ALU.mult),
      writes=["nmrow"])
    for (t0, tl) in T3S:
        E("tensor", lambda t, t0=t0, tl=tl:
          t.matmul(PS[2][:, 0:tl], s_onesr[0:1, :], invrow[0:1, t0:t0 + tl],
                   start=True, stop=True),
          reads=["invrow", "onesr"], writes=[("PS", 2)])
        E("tensor", lambda t, t0=t0, tl=tl:
          t.matmul(PS[3][:, 0:tl], s_onesr[0:1, :], nmrow[0:1, t0:t0 + tl],
                   start=True, stop=True),
          reads=["nmrow", "onesr"], writes=[("PS", 3)])
        for dc in range(NDC):
            E("vector", lambda v, dc=dc, t0=t0, tl=tl:
              v.tensor_tensor(xsqt[:, t0:t0 + tl],
                              xf32[:, dc * TPH + t0:dc * TPH + t0 + tl],
                              PS[2][:, 0:tl], ALU.mult),
              reads=["xf32", ("PS", 2)], writes=["xsqt"])
            E("vector", lambda v, dc=dc, t0=t0, tl=tl:
              v.tensor_tensor(xn[:, dc * TPH + t0:dc * TPH + t0 + tl],
                              xsqt[:, t0:t0 + tl], PS[3][:, 0:tl], ALU.add),
              reads=["xsqt", ("PS", 3)], writes=["xn"])

    em.phase_barrier()
    if debug:
        dma("sync", dbg["dbg_xn"], xn[:])
        dma("sync", dbg["dbg_p0"][0:1, 0:TPH], mrow[:])
        dma("sync", dbg["dbg_p0"][0:1, TPH:2 * TPH], invrow[:])
        dma("sync", dbg["dbg_p0"][0:1, 2 * TPH:3 * TPH], vtmp[:])
        em.phase_barrier()

    def spill_dst(cc):
        kh, r = divmod(cc, 6)
        if r == 0:
            return d_mixed[kh]
        if r == 1:
            return d_mixed[16 + kh]
        if r in (2, 3):
            return d_mixed[32 + 2 * kh + (r - 2)]
        return None

    for wg in range(24):
        wp = wg % 2
        dma("gpsimd", wqb[wp][:], d_wq[wg].rearrange("p dc c -> p (dc c)"),
            writes=[("wqb", wp)])
        for oc4 in range(4):
            cc = wg * 4 + oc4
            kh, r = divmod(cc, 6)
            qp = cc % 2
            bank = 4 + (cc % 2)
            for (t0, tl) in T3S:
                for dc in range(NDC):
                    st, sp = (dc == 0), (dc == NDC - 1)
                    E("tensor", lambda t, dc=dc, oc4=oc4, t0=t0, tl=tl, st=st,
                      sp=sp, wp=wp, bank=bank:
                      t.matmul(PS[bank][:, 0:tl],
                               wqb[wp][:, dc * 512 + oc4 * 128:
                                       dc * 512 + (oc4 + 1) * 128],
                               xn[:, dc * TPH + t0:dc * TPH + t0 + tl],
                               start=st, stop=sp),
                      reads=[("wqb", wp), "xn"], writes=[("PS", bank)])
                if r in (4, 5):
                    E("scalar", lambda s, t0=t0, tl=tl, qp=qp, bank=bank:
                      s.activation(qev[qp][:, t0:t0 + tl], PS[bank][:, 0:tl],
                                   AF.Silu),
                      reads=[("PS", bank)], writes=[("qev", qp)])
                else:
                    E("scalar", lambda s, t0=t0, tl=tl, qp=qp, bank=bank:
                      s.activation(qev[qp][:, t0:t0 + tl], PS[bank][:, 0:tl],
                                   AF.Copy),
                      reads=[("PS", bank)], writes=[("qev", qp)])
            if r in (4, 5):
                dst = d_silz[2 * kh + (r - 4)]
            else:
                dst = spill_dst(cc)
            dma("sync", dst, qev[qp][:], reads=[("qev", qp)],
                writes=[("spill", cc)])
    for bh in range(2):
        for (t0, tl) in T3S:
            for dc in range(NDC):
                st, sp = (dc == 0), (dc == NDC - 1)
                E("tensor", lambda t, dc=dc, t0=t0, tl=tl, st=st, sp=sp, bh=bh:
                  t.matmul(PS[0][0:32, 0:tl],
                           s_wba[:, dc * 64 + bh * 32:dc * 64 + bh * 32 + 32],
                           xn[:, dc * TPH + t0:dc * TPH + t0 + tl],
                           start=st, stop=sp),
                  reads=["wba", "xn"], writes=[("PS", 0)])
            E("vector", lambda v, t0=t0, tl=tl, bh=bh:
              v.tensor_copy((s_bb if bh == 0 else s_aa)[:, t0:t0 + tl],
                            PS[0][0:32, 0:tl]),
              reads=[("PS", 0)], writes=["ba"])

    em.phase_barrier()

    for cc8 in range(NCH):
        dma("sync", mxb[:].rearrange("p (c t) -> p c t", c=64),
            d_mixed[:, :, cc8 * C:cc8 * C + 131].transpose([1, 0, 2]),
            reads=[("spill", 0)], writes=["mxb"])
        for c64 in range(64):
            def fconv(v, c64=c64):
                o = cvo[:, c64 * 128:(c64 + 1) * 128]
                m0 = mxb[:, c64 * 131:c64 * 131 + 128]
                m1 = mxb[:, c64 * 131 + 1:c64 * 131 + 129]
                m2 = mxb[:, c64 * 131 + 2:c64 * 131 + 130]
                m3 = mxb[:, c64 * 131 + 3:c64 * 131 + 131]
                v.tensor_scalar(o, m0, s_convw[:, 4 * c64:4 * c64 + 1], None,
                                ALU.mult)
                v.scalar_tensor_tensor(o, m1, s_convw[:, 4 * c64 + 1:4 * c64 + 2],
                                       o, ALU.mult, ALU.add)
                v.scalar_tensor_tensor(o, m2, s_convw[:, 4 * c64 + 2:4 * c64 + 3],
                                       o, ALU.mult, ALU.add)
                return v.scalar_tensor_tensor(
                    o, m3, s_convw[:, 4 * c64 + 3:4 * c64 + 4], o,
                    ALU.mult, ALU.add)
            E("vector", fconv, reads=["mxb", "convw"], writes=[("cvo", c64)])
        for g4 in range(16):
            E("scalar", lambda s, g4=g4:
              s.activation(cvs[:, g4 * 512:(g4 + 1) * 512],
                           cvo[:, g4 * 512:(g4 + 1) * 512], AF.Silu),
              reads=[("cvo", 4 * g4), ("cvo", 4 * g4 + 1), ("cvo", 4 * g4 + 2),
                     ("cvo", 4 * g4 + 3)], writes=[("cvs", g4)])
        dma("sync", d_conv[:, :, cc8 * C:(cc8 + 1) * C].transpose([1, 0, 2]),
            cvs[:].rearrange("p (c t) -> p c t", c=64),
            reads=[("cvs", g) for g in range(16)], writes=[("convspill", cc8)])

    em.phase_barrier()

    br = s_bb[0:32, 3:3 + 1024]
    ar = s_aa[0:32, 3:3 + 1024]
    E("scalar", lambda s: s.activation(grow[:], br, AF.Exp, scale=-1.0),
      reads=["ba"], writes=["grow"])
    E("vector", lambda v: v.tensor_scalar(grow[:], grow[:], 1.0, None, ALU.add),
      writes=["grow"])
    E("scalar", lambda s: s.activation(grow[:], grow[:], AF.Ln), writes=["grow"])
    E("vector", lambda v: v.tensor_scalar(lnb[:], grow[:], -1.0, None, ALU.mult),
      reads=["grow"], writes=["lnbr"])
    E("scalar", lambda s: s.activation(grow[:], ar, AF.Exp, bias=s_dtb[:]),
      reads=["ba", "dtb"], writes=["grow"])
    E("vector", lambda v: v.tensor_scalar(grow[:], grow[:], 1.0, None, ALU.add),
      writes=["grow"])
    E("scalar", lambda s: s.activation(grow[:], grow[:], AF.Ln), writes=["grow"])
    E("scalar", lambda s: s.activation(q_t1[0:32, 0:1], s_alog[:], AF.Exp),
      reads=["alog"], writes=["aexp"])
    E("vector", lambda v: v.tensor_scalar(q_t2[0:32, 0:1], q_t1[0:32, 0:1], -1.0,
                                          None, ALU.mult),
      reads=["aexp"], writes=["negaexp"])
    E("scalar", lambda s: s.activation(grow2[:], grow[:], AF.Copy,
                                       scale=q_t2[0:32, 0:1]),
      reads=["grow", "negaexp"], writes=["grow2"])
    for cc8 in range(NCH):
        E("vector", lambda v, cc8=cc8:
          v.tensor_tensor_scan(G[:, cc8 * C:(cc8 + 1) * C],
                               grow2[:, cc8 * C:(cc8 + 1) * C],
                               grow2[:, cc8 * C:(cc8 + 1) * C],
                               0.0, ALU.add, ALU.bypass),
          reads=["grow2"], writes=[("Grow", cc8)])

    em.phase_barrier()
    if debug:
        dma("sync", dbg["dbg_rows"][:, 0:1024], G[:])
        dma("sync", dbg["dbg_rows"][:, 1024:2048], lnb[:])
        dma("sync", dbg["dbg_ba"][:, 0:TPH], s_bb[:])
        dma("sync", dbg["dbg_ba"][:, TPH:2 * TPH], s_aa[:])
        dma("sync", dbg["dbg_g"][:, 0:1024], grow2[:])
        dma("sync", dbg["dbg_g"][:, 1024:2048], grow[:])
        E("sync", lambda e: e.dma_start(out=dbg["dbg_g"][:, 2048:2049],
                                        in_=q_t1[0:32, 0:1],
                                        allow_slow_non_contiguous=True), dma=True)
        E("sync", lambda e: e.dma_start(out=dbg["dbg_g"][:, 2049:2050],
                                        in_=q_t2[0:32, 0:1],
                                        allow_slow_non_contiguous=True), dma=True)
        em.phase_barrier()

    _p2b(nc, em, locals())
    em.phase_barrier()
    if debug:
        dma("sync", dbg["dbg_chst"], chst[:])
        dma("sync", dbg["dbg_rsq"], rsq[:])
        dma("sync", dbg["dbg_cols"][:, 0:128], colsF[:])
        dma("sync", dbg["dbg_cols"][:, 128:160], cols2[:])
        dma("sync", dbg["dbg_lr"][:, 0:128], rqr[:])
        dma("sync", dbg["dbg_lr"][:, 128:256], rkr[:])
        dma("sync", dbg["dbg_lr"][:, 256:384], lnrq[:])
        dma("sync", dbg["dbg_lr"][:, 384:512], lnrk[:])
        dma("sync", dbg["dbg_mco"], Mco[0][:])
        dma("sync", dbg["dbg_bco"], bco[0][:])
        em.phase_barrier()
    _p3(nc, em, locals())
    em.phase_barrier()
    if debug:
        dma("sync", dbg["dbg_sin"], S_pp[0][:])
        em.phase_barrier()
    _p5(nc, em, locals())
    em.phase_barrier()
    if debug:
        dma("sync", dbg["dbg_yst"], yst[:])
        dma("sync", dbg["dbg_xg"], xgst[:])
        dma("sync", dbg["dbg_sz"], szb[1][:])
        dma("sync", dbg["dbg_r5"][:, 0:32], rsq5[:, 0:32])
        dma("sync", dbg["dbg_r5"][:, 32:64], nrm5[:, 0:32])
        em.phase_barrier()
    _p4(nc, em, locals())

    with nc.Block() as block:
        em.finalize(block)
    nc._em = em
    return nc


def _p2b(nc, em, L):
    E = em.emit
    g = lambda n: L[n]
    (PS, PSB, cvb, chst, sq_t, nrm2, rsq, rqr, rkr, lnrq, lnrk, growk, gmgc,
     mcrow, colsF, khcol, bgcol,
     bbcol, tcol, arow, gcI, ktok, vtok, khat, Mn_t, At_t, nbar,
     napp, mln, Lapp, Lnrm, dcur, negG, rhsb, zev, y0t, etmp, dmin, dexp,
     G, lnb, s_onesr, s_onesrb, s_maska, s_maskm, s_bdneg, s_bdl, s_idb,
     s_onescb, s_onescf, Mco, bco, d_conv, d_chain, q_i, q_t1, q_t2) = (
        g("PS"), g("PSB"), g("cvb"), g("chst"), g("sq_t"), g("nrm2"), g("rsq"),
        g("rqr"), g("rkr"), g("lnrq"), g("lnrk"), g("growk"), g("gmgc"),
        g("mcrow"), g("colsF"), g("khcol"), g("bgcol"),
        g("bbcol"), g("tcol"), g("arow"), g("gcI"),
        g("ktok"), g("vtok"), g("khat"), g("Mn_t"), g("At_t"), g("nbar"),
        g("napp"), g("mln"), g("Lapp"), g("Lnrm"), g("dcur"), g("negG"),
        g("rhsb"), g("zev"), g("y0t"), g("etmp"), g("dmin"), g("dexp"),
        g("G"), g("lnb"), g("s_onesr"), g("s_onesrb"), g("s_maska"),
        g("s_maskm"), g("s_bdneg"), g("s_bdl"), g("s_idb"), g("s_onescb"),
        g("s_onescf"), g("Mco"), g("bco"), g("d_conv"), g("d_chain"),
        g("q_i"), g("q_t1"), g("q_t2"))
    dgt, dgb, cols2, grqcol, gamcols, gcsel = (g("dgt"), g("dgb"), g("cols2"),
                                               g("grqcol"), g("gamcols"),
                                               g("gcsel"))
    s_onesf128, s_onesb128, s_e127 = (g("s_onesf128"), g("s_onesb128"),
                                      g("s_e127"))
    dma = lambda q, dst, src, **kw: E(
        q, lambda e, dst=dst, src=src: e.dma_start(out=dst, in_=src),
        dma=True, **kw)

    for h in range(NVH):
        E("vector", lambda v, h=h:
          v.tensor_copy(Mco[0][:, h * 128:(h + 1) * 128], s_idb[:]),
          reads=["idb"], writes=[("Mco", 0, h)])
    E("vector", lambda v: v.memset(bco[0][:], 0.0),
      writes=[("bco", 0, h2) for h2 in range(NVH)])

    for cc8 in range(NCH):
        dma("sync", cvb[:].rearrange("p (c t) -> p c t", c=64),
            d_conv[:, :, cc8 * C:(cc8 + 1) * C].transpose([1, 0, 2]),
            reads=[("convspill", cc8)], writes=["cvb"])
        tend = cc8 * C + C - 1
        Gsl = G[:, cc8 * C:(cc8 + 1) * C]
        lnbsl = lnb[:, cc8 * C:(cc8 + 1) * C]
        for grp in range(8):
            which, kh0 = divmod(grp, 4)
            toff = which * 2048 + kh0 * 512
            E("vector", lambda v, toff=toff:
              v.tensor_tensor(sq_t[:], cvb[:, toff:toff + 512],
                              cvb[:, toff:toff + 512], ALU.mult),
              reads=["cvb"], writes=["sq_t"])
            for k4 in range(4):
                kh = kh0 * 4 + k4
                for dup in range(2):
                    col = (0 if which == 0 else 32) + 2 * kh + dup
                    E("tensor", lambda t, col=col, k4=k4:
                      t.matmul(PS[0][:, col:col + 1],
                               sq_t[:, k4 * 128:(k4 + 1) * 128], s_onescb[:],
                               start=True, stop=True),
                      reads=["sq_t", "onescb"], writes=[("PS", 0)])
        E("vector", lambda v: v.tensor_scalar(nrm2[:], PS[0][:, 0:64], L2_EPS,
                                              None, ALU.add),
          reads=[("PS", 0)], writes=["nrm2"])
        E("vector", lambda v: _quake(v, rsq[:], nrm2[:], q_i[:, 0:64],
                                     q_t1[:, 0:64], q_t2[:, 0:64],
                                     L["q_t3"][:, 0:64]),
          reads=["nrm2"], writes=["rsq", "qscr"])
        E("vector", lambda v: v.tensor_scalar(rsq[:, 0:32], rsq[:, 0:32],
                                              1.0 / math.sqrt(DK), None,
                                              ALU.mult),
          writes=["rsq"])
        E("tensor", lambda t: t.transpose(PS[1][0:32, 0:128], rsq[:, 0:32],
                                          L["s_idf"][:]),
          reads=["rsq", "idf"], writes=[("PS", 1)])
        E("vector", lambda v: v.tensor_copy(rqr[:], PS[1][0:32, 0:128]),
          reads=[("PS", 1)], writes=["rqr"])
        E("tensor", lambda t: t.transpose(PS[1][0:32, 128:256], rsq[:, 32:64],
                                          L["s_idf"][:]),
          reads=["rsq", "idf"], writes=[("PS", 1)])
        E("vector", lambda v: v.tensor_copy(rkr[:], PS[1][0:32, 128:256]),
          reads=[("PS", 1)], writes=["rkr"])
        E("scalar", lambda s: s.activation(lnrq[:], rqr[:], AF.Ln),
          reads=["rqr"], writes=["lnrq"])
        E("scalar", lambda s: s.activation(lnrk[:], rkr[:], AF.Ln),
          reads=["rkr"], writes=["lnrk"])
        E("vector", lambda v, Gsl=Gsl:
          v.tensor_tensor(arow[:], Gsl, lnrq[:], ALU.add),
          reads=[("Grow", cc8), "lnrq"], writes=["arow"])
        E("vector", lambda v, Gsl=Gsl:
          v.tensor_tensor(growk[:], Gsl, lnrk[:], ALU.subtract),
          reads=[("Grow", cc8), "lnrk"], writes=["growk"])
        E("gpsimd", lambda g2, Gsl=Gsl, tend=tend:
          g2.tensor_scalar(gmgc[:], Gsl, G[:, tend:tend + 1], None,
                           ALU.subtract),
          reads=[("Grow", cc8)], writes=["gmgc"])
        E("vector", lambda v, Gsl=Gsl, lnbsl=lnbsl:
          v.tensor_tensor(mcrow[:], Gsl, lnbsl, ALU.add),
          reads=[("Grow", cc8), "lnbr"], writes=["mcrow"])
        E("vector", lambda v:
          v.tensor_tensor(mcrow[:], mcrow[:], lnrk[:], ALU.add),
          reads=["lnrk"], writes=["mcrow"])
        E("tensor", lambda t: t.transpose(PS[1][:, 0:32], growk[:],
                                          L["s_idf"][0:32, 0:32]),
          reads=["growk", "idf"], writes=[("PS", 1)])
        E("tensor", lambda t: t.transpose(PS[1][:, 32:64], gmgc[:],
                                          L["s_idf"][0:32, 0:32]),
          reads=["gmgc", "idf"], writes=[("PS", 1)])
        E("tensor", lambda t: t.transpose(PS[1][:, 64:96], mcrow[:],
                                          L["s_idf"][0:32, 0:32]),
          reads=["mcrow", "idf"], writes=[("PS", 1)])
        E("tensor", lambda t, Gsl=Gsl: t.transpose(PS[1][:, 96:128], Gsl,
                                                   L["s_idf"][0:32, 0:32]),
          reads=[("Grow", cc8), "idf"], writes=[("PS", 1)])
        E("tensor", lambda t: t.transpose(PS[1][:, 128:160], arow[:],
                                          L["s_idf"][0:32, 0:32]),
          reads=["arow", "idf"], writes=[("PS", 1)])
        E("scalar", lambda s: s.activation(colsF[:], PS[1][:, 0:128], AF.Copy),
          reads=[("PS", 1)], writes=["colsF"])
        E("scalar", lambda s: s.activation(cols2[:], PS[1][:, 128:160], AF.Copy),
          reads=[("PS", 1)], writes=["cols2"])
        E("scalar", lambda s: s.activation(grqcol[:], cols2[:], AF.Exp),
          reads=["cols2"], writes=["grqcol"])
        E("scalar", lambda s: s.activation(gamcols[:], colsF[:, 96:128], AF.Exp),
          reads=["colsF"], writes=["gamcols"])
        E("scalar", lambda s: s.activation(khcol[:], colsF[:, 32:64], AF.Exp,
                                           scale=-1.0),
          reads=["colsF"], writes=["khcol"])
        E("vector", lambda v: v.tensor_tensor(tcol[:], colsF[:, 64:96],
                                              colsF[:, 0:32], ALU.add),
          reads=["colsF"], writes=["tcol"])
        E("vector", lambda v: v.scalar_tensor_tensor(tcol[:], colsF[:, 96:128],
                                                     -2.0, tcol[:], ALU.mult,
                                                     ALU.add),
          reads=["colsF"], writes=["tcol"])
        E("scalar", lambda s: s.activation(bbcol[:], tcol[:], AF.Exp),
          reads=["tcol"], writes=["bbcol"])
        E("vector", lambda v: v.tensor_tensor(tcol[:], tcol[:], colsF[:, 96:128],
                                              ALU.add),
          reads=["colsF"], writes=["tcol"])
        E("scalar", lambda s: s.activation(bgcol[:], tcol[:], AF.Exp),
          reads=["tcol"], writes=["bgcol"])
        for kh in range(NKH):
            kp = kh % 2
            ksl = cvb[:, 2048 + kh * 128:2048 + (kh + 1) * 128]
            qsl = cvb[:, kh * 128:(kh + 1) * 128]
            E("tensor", lambda t, ksl=ksl, kp=kp:
              t.transpose(PSB[0][:, kp * 384:kp * 384 + 128], ksl,
                          L["s_idb"][:]),
              reads=["cvb", "idb"], writes=[("PSB", 0)])
            E("scalar", lambda s, kh=kh, kp=kp:
              s.activation(ktok[kp][:], PSB[0][:, kp * 384:kp * 384 + 128],
                           AF.Copy, scale=rsq[:, 32 + 2 * kh:33 + 2 * kh]),
              reads=[("PSB", 0), "rsq"], writes=[("ktok", kp)])
            for j in range(2):
                vsl = cvb[:, 4096 + (2 * kh + j) * 128:
                          4096 + (2 * kh + j + 1) * 128]
                E("tensor", lambda t, vsl=vsl, j=j, kp=kp:
                  t.transpose(PSB[0][:, kp * 384 + 128 * (j + 1):
                                      kp * 384 + 128 * (j + 2)], vsl,
                              L["s_idb"][:]),
                  reads=["cvb", "idb"], writes=[("PSB", 0)])
                E("scalar", lambda s, j=j, kp=kp:
                  s.activation(vtok[2 * kp + j][:],
                               PSB[0][:, kp * 384 + 128 * (j + 1):
                                       kp * 384 + 128 * (j + 2)], AF.Copy),
                  reads=[("PSB", 0)], writes=[("vtok", 2 * kp + j)])

            for hh in range(2):
                h = 2 * kh + hh
                p = hh
                A, B, Cz = PS[p], PS[2 + p], PS[4 + p]
                rA, rB, rC = ("PS", p), ("PS", 2 + p), ("PS", 4 + p)
                E("tensor", lambda t, ksl=ksl, A=A:
                  t.matmul(A[:, 0:128], ksl, ksl, start=True, stop=True),
                  reads=["cvb"], writes=[rA])
                E("tensor", lambda t, ksl=ksl, qsl=qsl, A=A:
                  t.matmul(A[:, 128:256], ksl, qsl, start=True, stop=True),
                  reads=["cvb"], writes=[rA])
                E("vector", lambda v, h=h, p=p:
                  v.tensor_scalar(dgt[p][0][:], L["s_idf"][:],
                                  colsF[:, h:h + 1], None, ALU.mult),
                  reads=["idf", "colsF"], writes=[("dgt", p, 0)])
                E("tensor", lambda t, p=p, A=A:
                  t.matmul(A[:, 256:384], s_onesf128[:], dgt[p][0][:],
                           start=True, stop=True),
                  reads=["onesf128", ("dgt", p, 0)], writes=[rA])
                E("vector", lambda v, h=h, p=p:
                  v.tensor_scalar(dgt[p][1][:], L["s_idf"][:],
                                  cols2[:, h:h + 1], None, ALU.mult),
                  reads=["idf", "cols2"], writes=[("dgt", p, 1)])
                E("tensor", lambda t, p=p, A=A:
                  t.matmul(A[:, 384:512], s_onesf128[:], dgt[p][1][:],
                           start=True, stop=True),
                  reads=["onesf128", ("dgt", p, 1)], writes=[rA])
                E("vector", lambda v, h=h, p=p, A=A:
                  v.scalar_tensor_tensor(dmin[p][:], A[:, 256:384],
                                         colsF[:, 64 + h:65 + h], s_maskm[:],
                                         ALU.subtract, ALU.add),
                  reads=[rA, "colsF", "maskm"], writes=[("dmin", p)])
                E("scalar", lambda s, p=p:
                  s.activation(dexp[p][:], dmin[p][:], AF.Exp, scale=-1.0),
                  reads=[("dmin", p)], writes=[("dexp", p)])
                E("vector", lambda v, p=p, A=A:
                  v.tensor_tensor(Mn_t[p][:], A[:, 0:128], dexp[p][:],
                                  ALU.mult),
                  reads=[rA, ("dexp", p)], writes=[("Mn", p)])
                E("vector", lambda v, h=h, p=p, A=A:
                  v.scalar_tensor_tensor(dmin[p][:], A[:, 384:512],
                                         colsF[:, h:h + 1], s_maska[:],
                                         ALU.subtract, ALU.add),
                  reads=[rA, "colsF", "maska"], writes=[("dmin", p)])
                E("scalar", lambda s, p=p:
                  s.activation(dexp[p][:], dmin[p][:], AF.Exp),
                  reads=[("dmin", p)], writes=[("dexp", p)])
                E("vector", lambda v, p=p, A=A:
                  v.tensor_tensor(At_t[p][:], A[:, 128:256], dexp[p][:],
                                  ALU.mult),
                  reads=[rA, ("dexp", p)], writes=[("At", p)])
                E("vector", lambda v, p=p:
                  v.tensor_tensor(nbar[p][:], Mn_t[p][:], s_bdneg[:], ALU.mult),
                  reads=[("Mn", p), "bdneg"], writes=[("nbar", p)])
                E("vector", lambda v, p=p:
                  v.tensor_tensor(mln[p][:], Mn_t[p][:], s_bdl[:], ALU.mult),
                  reads=[("Mn", p), "bdl"], writes=[("mln", p)])
                E("vector", lambda v, p=p: v.transpose(napp[p][:], nbar[p][:]),
                  reads=[("nbar", p)], writes=[("napp", p)])
                capp, cnrm = napp[p], nbar[p]
                rapp, rnrm = ("napp", p), ("nbar", p)
                for lv in range(4):
                    E("tensor", lambda t, capp=capp, cnrm=cnrm, B=B:
                      t.matmul(B[:, 0:128], cnrm[:], capp[:],
                               start=True, stop=True),
                      reads=[rapp, rnrm], writes=[rB])
                    E("tensor", lambda t, capp=capp, cnrm=cnrm, B=B:
                      t.matmul(B[:, 128:256], capp[:], cnrm[:],
                               start=True, stop=True),
                      reads=[rapp, rnrm], writes=[rB])
                    E("scalar", lambda s, lv=lv, p=p, B=B:
                      s.activation(Lapp[p][lv][:], B[:, 0:128], AF.Copy),
                      reads=[rB], writes=[("Lapp", p, lv)])
                    E("scalar", lambda s, lv=lv, p=p, B=B:
                      s.activation(Lnrm[p][lv][:], B[:, 128:256], AF.Copy),
                      reads=[rB], writes=[("Lnrm", p, lv)])
                    capp, cnrm = Lapp[p][lv], Lnrm[p][lv]
                    rapp, rnrm = ("Lapp", p, lv), ("Lnrm", p, lv)
                E("vector", lambda v, p=p:
                  v.tensor_tensor(dcur[p][0][:], napp[p][:], s_idb[:], ALU.add),
                  reads=[("napp", p), "idb"], writes=[("dcur", p, 0)])
                dp = 0
                for lv in range(4):
                    E("tensor", lambda t, lv=lv, dp=dp, p=p, B=B:
                      t.matmul(B[:, 256:384], Lnrm[p][lv][:], dcur[p][dp][:],
                               start=True, stop=False),
                      reads=[("Lnrm", p, lv), ("dcur", p, dp)], writes=[rB])
                    E("tensor", lambda t, dp=dp, p=p, B=B:
                      t.matmul(B[:, 256:384], s_idb[:], dcur[p][dp][:],
                               start=False, stop=True),
                      reads=["idb", ("dcur", p, dp)], writes=[rB])
                    E("scalar", lambda s, dp=dp, p=p, B=B:
                      s.activation(dcur[p][1 - dp][:], B[:, 256:384], AF.Copy),
                      reads=[rB], writes=[("dcur", p, 1 - dp)])
                    dp = 1 - dp
                dapp, rdapp = dcur[p][dp], ("dcur", p, dp)
                E("tensor", lambda t, dapp=dapp, p=p, B=B:
                  t.matmul(B[:, 256:384], mln[p][:], dapp[:],
                           start=True, stop=True),
                  reads=[("mln", p), rdapp], writes=[rB])
                E("vector", lambda v, p=p, B=B:
                  v.tensor_scalar(negG[p][:], B[:, 256:384], -1.0, None,
                                  ALU.mult),
                  reads=[rB], writes=[("negG", p)])
                E("vector", lambda v, hh=hh, h=h, p=p, kp=kp:
                  v.tensor_scalar(rhsb[p][:, 0:128], vtok[2 * kp + hh][:],
                                  bbcol[:, h:h + 1], None, ALU.mult),
                  reads=[("vtok", 2 * kp + hh), "bbcol"], writes=[("rhsb", p)])
                E("vector", lambda v, h=h, p=p, kp=kp:
                  v.tensor_scalar(rhsb[p][:, 128:256], ktok[kp][:],
                                  bgcol[:, h:h + 1], None, ALU.mult),
                  reads=[("ktok", kp), "bgcol"], writes=[("rhsb", p)])
                E("tensor", lambda t, dapp=dapp, p=p, Cz=Cz:
                  t.matmul(Cz[:, 0:256], dapp[:], rhsb[p][:],
                           start=True, stop=True),
                  reads=[rdapp, ("rhsb", p)], writes=[rC])
                E("scalar", lambda s, p=p, Cz=Cz:
                  s.activation(y0t[p][:], Cz[:, 0:256], AF.Copy),
                  reads=[rC], writes=[("y0t", p)])
                zcur, rz = y0t[p], ("y0t", p)
                for it in range(3):
                    sl_z = slice(256, 512) if it % 2 == 0 else slice(0, 256)
                    E("tensor", lambda t, zcur=zcur, p=p, Cz=Cz, sl_z=sl_z:
                      t.matmul(Cz[:, sl_z], negG[p][:], zcur[:],
                               start=True, stop=False),
                      reads=[("negG", p), rz], writes=[rC])
                    E("tensor", lambda t, p=p, Cz=Cz, sl_z=sl_z:
                      t.matmul(Cz[:, sl_z], s_idb[:], y0t[p][:],
                               start=False, stop=True),
                      reads=["idb", ("y0t", p)], writes=[rC])
                    E("scalar", lambda s, it=it, p=p, Cz=Cz, sl_z=sl_z:
                      s.activation(zev[p][it % 2][:], Cz[:, sl_z], AF.Copy),
                      reads=[rC], writes=[("zev", p, it % 2)])
                    zcur, rz = zev[p][it % 2], ("zev", p, it % 2)
                Xt, rX = zcur, rz
                E("vector", lambda v, h=h, p=p, kp=kp:
                  v.tensor_scalar(khat[p][:], ktok[kp][:], khcol[:, h:h + 1],
                                  None, ALU.mult),
                  reads=[("ktok", kp), "khcol"], writes=[("khat", p)])
                E("vector", lambda v, h=h, p=p:
                  v.tensor_scalar(dgb[p][:], s_idb[:], grqcol[:, h:h + 1],
                                  None, ALU.mult),
                  reads=["idb", "grqcol"], writes=[("dgb", p)])
                E("tensor", lambda t, p=p, A=A:
                  t.matmul(A[:, 256:384], s_onesb128[:], dgb[p][:],
                           start=True, stop=True),
                  reads=["onesb128", ("dgb", p)], writes=[rA])
                E("tensor", lambda t, Xt=Xt, p=p, B=B:
                  t.matmul(B[:, 384:512], Xt[:, 128:256], At_t[p][:],
                           start=True, stop=True),
                  reads=[rX, ("At", p)], writes=[rB])
                E("vector", lambda v, qsl=qsl, p=p, A=A:
                  v.tensor_tensor(etmp[p][:], qsl, A[:, 256:384], ALU.mult),
                  reads=["cvb", rA], writes=[("etmp", p)])
                E("vector", lambda v, h=h, p=p, B=B:
                  v.tensor_tensor(chst[:, h * 512:h * 512 + 128], etmp[p][:],
                                  B[:, 384:512], ALU.subtract),
                  reads=[("etmp", p), rB], writes=[("chE", h)])
                E("tensor", lambda t, Xt=Xt, p=p, B=B:
                  t.matmul(B[:, 0:128], Xt[:, 0:128], At_t[p][:],
                           start=True, stop=True),
                  reads=[rX, ("At", p)], writes=[rB])
                E("scalar", lambda s, h=h, p=p, B=B:
                  s.activation(chst[:, h * 512 + 128:h * 512 + 256],
                               B[:, 0:128], AF.Copy),
                  reads=[rB], writes=[("chO", h)])
                E("vector", lambda v, h=h, p=p:
                  v.tensor_scalar(gcsel[p][:], s_e127[:], gamcols[:, h:h + 1],
                                  None, ALU.mult),
                  reads=["e127", "gamcols"], writes=[("gcsel", p)])
                E("tensor", lambda t, p=p, B=B:
                  t.matmul(B[:, 256:257], s_onesf128[:], gcsel[p][:],
                           start=True, stop=True),
                  reads=["onesf128", ("gcsel", p)], writes=[rB])
                E("vector", lambda v, p=p, B=B:
                  v.tensor_scalar(gcI[p][:], s_idb[:], B[:, 256:257], None,
                                  ALU.mult),
                  reads=["idb", rB], writes=[("gcI", p)])
                E("tensor", lambda t, Xt=Xt, p=p, B=B:
                  t.matmul(B[:, 128:256], Xt[:, 128:256], khat[p][:],
                           start=True, stop=True),
                  reads=[rX, ("khat", p)], writes=[rB])
                E("vector", lambda v, h=h, p=p, B=B:
                  v.scalar_tensor_tensor(chst[:, h * 512 + 256:h * 512 + 384],
                                         B[:, 128:256], -1.0, gcI[p][:],
                                         ALU.mult, ALU.add),
                  reads=[rB, ("gcI", p)], writes=[("chM", h)])
                E("tensor", lambda t, Xt=Xt, p=p, B=B:
                  t.matmul(B[:, 128:256], khat[p][:], Xt[:, 0:128],
                           start=True, stop=True),
                  reads=[("khat", p), rX], writes=[rB])
                E("scalar", lambda s, h=h, p=p, B=B:
                  s.activation(chst[:, h * 512 + 384:h * 512 + 512],
                               B[:, 128:256], AF.Copy),
                  reads=[rB], writes=[("chB", h)])
                cp = cc8 % 2
                E("tensor", lambda t, h=h, cp=cp, B=B:
                  t.matmul(B[:, 128:256],
                           chst[:, h * 512 + 256:h * 512 + 384],
                           Mco[cp][:, h * 128:(h + 1) * 128],
                           start=True, stop=True),
                  reads=[("chM", h), ("Mco", cp, h)], writes=[rB])
                E("scalar", lambda s, h=h, cp=cp, B=B:
                  s.activation(Mco[1 - cp][:, h * 128:(h + 1) * 128],
                               B[:, 128:256], AF.Copy),
                  reads=[rB], writes=[("Mco", 1 - cp, h)])
                E("tensor", lambda t, h=h, cp=cp, B=B:
                  t.matmul(B[:, 128:256],
                           chst[:, h * 512 + 256:h * 512 + 384],
                           bco[cp][:, h * 128:(h + 1) * 128],
                           start=True, stop=False),
                  reads=[("chM", h), ("bco", cp, h)], writes=[rB])
                E("tensor", lambda t, h=h, B=B:
                  t.matmul(B[:, 128:256], s_idb[:],
                           chst[:, h * 512 + 384:h * 512 + 512],
                           start=False, stop=True),
                  reads=["idb", ("chB", h)], writes=[rB])
                E("scalar", lambda s, h=h, cp=cp, B=B:
                  s.activation(bco[1 - cp][:, h * 128:(h + 1) * 128],
                               B[:, 128:256], AF.Copy),
                  reads=[rB], writes=[("bco", 1 - cp, h)])
        dma("sync", d_chain[cc8], chst[:],
            reads=[("chE", h2) for h2 in range(NVH)] +
                  [("chO", h2) for h2 in range(NVH)] +
                  [("chM", h2) for h2 in range(NVH)] +
                  [("chB", h2) for h2 in range(NVH)],
            writes=[("chainspill", cc8)])


def _p3(nc, em, L):
    E = em.emit
    PS, PSB = L["PS"], L["PSB"]
    Mco, bco, S_pp, Sacc, gbuf = L["Mco"], L["bco"], L["S_pp"], L["Sacc"], L["gbuf"]
    s_idb, s_oh = L["s_idb"], L["s_oh"]
    d_ccin, d_ccout = L["d_ccin"], L["d_ccout"]
    dma = lambda q, dst, src, **kw: E(
        q, lambda e, dst=dst, src=src: e.dma_start(out=dst, in_=src),
        dma=True, **kw)
    pf = NCH % 2
    for h in range(NVH):
        sl = PSB[0][:, (h % 8) * 128:(h % 8 + 1) * 128]
        E("tensor", lambda t, h=h, sl=sl:
          t.transpose(sl, Mco[pf][:, h * 128:(h + 1) * 128], s_idb[:]),
          reads=[("Mco", pf, h), "idb"], writes=[("PSB", 0)])
        E("vector", lambda v, h=h, sl=sl:
          v.tensor_copy(gbuf[0][:, h * 128:(h + 1) * 128], sl),
          reads=[("PSB", 0)], writes=[("mgT", h)])
    dma("sync", d_ccin[:, 0:4096], gbuf[0][:, 0:4096],
        reads=[("mgT", h) for h in range(NVH)], writes=["ccin"])
    dma("sync", d_ccin[:, 4096:8192], bco[pf][:],
        reads=[("bco", pf, h) for h in range(NVH)], writes=["ccin2"])
    cc_tok = E("gpsimd", lambda g2:
               g2.collective_compute(
                   "AllGather", ALU.bypass,
                   replica_groups=[list(range(NCORE))],
                   ins=[d_ccin[:]], outs=[d_ccout[:]]),
               reads=["ccin", "ccin2"], writes=["ccout"])
    E("vector", lambda v: v.memset(S_pp[0][:], 0.0),
      writes=[("Spre", 0, h) for h in range(NVH)])
    E("vector", lambda v: v.memset(Sacc[0][:], 0.0), writes=[("Sacc", 0)],
      deps=[cc_tok])
    for g in range(NCORE):
        p = g % 2
        dma("sync", gbuf[p][:], d_ccout[g], reads=["ccout"],
            writes=[("gbuf", p)])
        E("vector", lambda v, p=p, g=g:
          v.scalar_tensor_tensor(Sacc[1 - p][:], S_pp[p][:], s_oh[:, g:g + 1],
                                 Sacc[p][:], ALU.mult, ALU.add),
          reads=[("Spre", p, h) for h in range(NVH)] + [("Sacc", p), "onehot"],
          writes=[("Sacc", 1 - p)])
        for h in range(NVH):
            E("tensor", lambda t, p=p, h=h:
              t.matmul(PS[h % 4][:, 0:128], gbuf[p][:, h * 128:(h + 1) * 128],
                       S_pp[p][:, h * 128:(h + 1) * 128],
                       start=True, stop=False),
              reads=[("gbuf", p), ("Spre", p, h)], writes=[("PS", h % 4)])
            E("tensor", lambda t, p=p, h=h:
              t.matmul(PS[h % 4][:, 0:128], s_idb[:],
                       gbuf[p][:, 4096 + h * 128:4096 + (h + 1) * 128],
                       start=False, stop=True),
              reads=["idb", ("gbuf", p)], writes=[("PS", h % 4)])
            E("vector", lambda v, p=p, h=h:
              v.tensor_copy(S_pp[1 - p][:, h * 128:(h + 1) * 128],
                            PS[h % 4][:, 0:128]),
              reads=[("PS", h % 4)], writes=[("Spre", 1 - p, h)])
    E("vector", lambda v: v.tensor_copy(S_pp[0][:], Sacc[0][:]),
      reads=[("Sacc", 0)], writes=[("Sst", 0, h) for h in range(NVH)])


def _p5(nc, em, L):
    E = em.emit
    PS, PSB = L["PS"], L["PSB"]
    S_pp, chb, szb, xgst, yst = L["S_pp"], L["chb"], L["szb"], L["xgst"], L["yst"]
    sqt5 = L["sqt5"]
    wob, astg, rmsr, rsq5, nrm5 = (L["wob"], L["astg"], L["rmsr"], L["rsq5"],
                                   L["nrm5"])
    sq_t, s_idb, s_onescb, s_onesr, s_normw = (L["sq_t"], L["s_idb"],
                                               L["s_onescb"], L["s_onesr"],
                                               L["s_normw"])
    s_idf = L["s_idf"]
    s_onesb128, p5dg = L["s_onesb128"], L["p5dg"]
    q_i, q_t1, q_t2, etmp = L["q_i"], L["q_t1"], L["q_t2"], L["etmp"]
    d_chain, d_silz, d_wo, d_attn = (L["d_chain"], L["d_silz"], L["d_wo"],
                                     L["d_attn"])
    dma = lambda q, dst, src, **kw: E(
        q, lambda e, dst=dst, src=src: e.dma_start(out=dst, in_=src),
        dma=True, **kw)
    for cc8 in range(NCH):
        p = cc8 % 2
        dma("sync", chb[p][:], d_chain[cc8], reads=[("chainspill", cc8)],
            writes=[("chb", p)])
        dma("sync", szb[p][:].rearrange("p (c t) -> p c t", c=32),
            d_silz[:, :, 3 + cc8 * C:3 + (cc8 + 1) * C].transpose([1, 0, 2]),
            reads=[("spill", 4)], writes=[("szb", p)])
        for h in range(NVH):
            hsl = slice(h * 128, (h + 1) * 128)
            hp = h % 2
            Po, Ps_ = PS[hp], PS[2 + hp]
            ro, rs = ("PS", hp), ("PS", 2 + hp)
            E("tensor", lambda t, p=p, h=h, hsl=hsl, Po=Po:
              t.matmul(Po[:, 0:128], S_pp[p][:, hsl],
                       chb[p][:, h * 512:h * 512 + 128], start=True, stop=False),
              reads=[("Sst", p, h), ("chb", p)], writes=[ro])
            E("tensor", lambda t, p=p, h=h, Po=Po:
              t.matmul(Po[:, 0:128], s_idb[:],
                       chb[p][:, h * 512 + 128:h * 512 + 256],
                       start=False, stop=True),
              reads=["idb", ("chb", p)], writes=[ro])
            E("tensor", lambda t, p=p, h=h, hsl=hsl, Ps_=Ps_:
              t.matmul(Ps_[:, 0:128], chb[p][:, h * 512 + 256:h * 512 + 384],
                       S_pp[p][:, hsl], start=True, stop=False),
              reads=[("chb", p), ("Sst", p, h)], writes=[rs])
            E("tensor", lambda t, p=p, h=h, Ps_=Ps_:
              t.matmul(Ps_[:, 0:128], s_idb[:],
                       chb[p][:, h * 512 + 384:h * 512 + 512],
                       start=False, stop=True),
              reads=["idb", ("chb", p)], writes=[rs])
            E("scalar", lambda s, p=p, h=h, hsl=hsl, Ps_=Ps_:
              s.activation(S_pp[1 - p][:, hsl], Ps_[:, 0:128], AF.Copy),
              reads=[rs], writes=[("Sst", 1 - p, h)])
            E("vector", lambda v, p=p, h=h, hsl=hsl, Po=Po:
              v.tensor_tensor(xgst[:, hsl], Po[:, 0:128], szb[p][:, hsl],
                              ALU.mult),
              reads=[ro, ("szb", p)], writes=[("xg", h)])
            E("vector", lambda v, h=h, hp=hp, hsl=hsl:
              v.tensor_tensor(sqt5[hp][:], xgst[:, hsl], xgst[:, hsl],
                              ALU.mult),
              reads=[("xg", h)], writes=[("sqt5", hp)])
            E("tensor", lambda t, h=h, hp=hp:
              t.matmul(PS[4][:, h:h + 1], sqt5[hp][:], s_onescb[:],
                       start=True, stop=True),
              reads=[("sqt5", hp), "onescb"], writes=[("PS", 4)])
        E("vector", lambda v:
          v.tensor_scalar(nrm5[:], PS[4][:, 0:64], 128.0 * RMS_EPS, None,
                          ALU.add),
          reads=[("PS", 4)], writes=["nrm5"])
        E("vector", lambda v:
          _quake(v, rsq5[:], nrm5[:], q_i[:, 0:64], q_t1[:, 0:64],
                 q_t2[:, 0:64], L["q_t3"][:, 0:64],
                 final_scale=math.sqrt(128.0)),
          reads=["nrm5"], writes=["rsq5", "qscr"])
        for h in range(NVH):
            hsl = slice(h * 128, (h + 1) * 128)
            hp = h % 2
            E("scalar", lambda s, h=h, hp=hp:
              s.activation(p5dg[hp][:], s_idb[:], AF.Copy,
                           scale=rsq5[:, h:h + 1]),
              reads=["idb", "rsq5"], writes=[("p5dg", hp)])
            E("tensor", lambda t, hp=hp:
              t.matmul(PS[5][:, hp * 256:hp * 256 + 128], s_onesb128[:],
                       p5dg[hp][:], start=True, stop=True),
              reads=["onesb128", ("p5dg", hp)], writes=[("PS", 5)])
            E("vector", lambda v, h=h, hp=hp, hsl=hsl:
              v.tensor_tensor(q_t1[:, 0:128] if hp == 0 else q_t2[:, 0:128],
                              xgst[:, hsl], PS[5][:, hp * 256:hp * 256 + 128],
                              ALU.mult),
              reads=[("xg", h), ("PS", 5)], writes=[("qscr5", hp)])
            ysl = slice(h * 512 + (cc8 % 4) * 128, h * 512 + (cc8 % 4 + 1) * 128)
            E("vector", lambda v, ysl=ysl, hp=hp:
              v.tensor_scalar(yst[:, ysl],
                              q_t1[:, 0:128] if hp == 0 else q_t2[:, 0:128],
                              s_normw[:], None, ALU.mult),
              reads=[("qscr5", hp), "normw"], writes=[("yst", h, cc8 % 4)])
        if cc8 % 4 == 3:
            half = cc8 // 4
            for dc in range(NDC):
                dma("gpsimd", wob[0][:].rearrange("p (c t) -> p c t", c=32),
                    d_wo[dc].transpose([1, 0, 2]), writes=[("wob", 0)])
                for hv in range(NVH):
                    st, sp = (hv == 0), (hv == NVH - 1)
                    E("tensor", lambda t, hv=hv, st=st, sp=sp:
                      t.matmul(PS[4][:, 0:512], wob[0][:, hv * 128:(hv + 1) * 128],
                               yst[:, hv * 512:(hv + 1) * 512],
                               start=st, stop=sp),
                      reads=[("wob", 0)] + [("yst", hv, j) for j in range(4)],
                      writes=[("PS", 4)])
                E("vector", lambda v: v.tensor_copy(astg[:], PS[4][:, 0:512]),
                  reads=[("PS", 4)], writes=["astg"])
                dma("sync", d_attn[dc, :, half * 512:(half + 1) * 512], astg[:],
                    reads=["astg"], writes=[("attnspill", dc, half)])


def _p4(nc, em, L):
    E = em.emit
    PS = L["PS"]
    hsl_t, w1b, w3b, w2b, gsb = L["hsl"], L["w1b"], L["w3b"], L["w2b"], L["gsb"]
    xrs, atb, outst = L["xrs"], L["atb"], L["outst"]
    xn, s_h128 = L["xn"], L["s_h128"]
    d_w1, d_w3, d_w2, d_x, d_attn, d_out = (L["d_w1"], L["d_w3"], L["d_w2"],
                                            L["d_x"], L["d_attn"], L["d_out"])
    dma = lambda q, dst, src, **kw: E(
        q, lambda e, dst=dst, src=src: e.dma_start(out=dst, in_=src),
        dma=True, **kw)
    out_toks = []
    for tg in range(2):
        toff = 3 + tg * 512
        slot_of = list(range(64))
        free = [64, 65]
        for hc in range(64):
            wp = hc % 2
            dma("gpsimd", w1b[wp][:].rearrange("p (dc t) -> p dc t", dc=NDC),
                d_w1[hc].transpose([1, 0, 2]), writes=[("w1b", wp)])
            dma("gpsimd", w3b[wp][:].rearrange("p (dc t) -> p dc t", dc=NDC),
                d_w3[hc].transpose([1, 0, 2]), writes=[("w3b", wp)])
            for dc in range(NDC):
                st, sp = (dc == 0), (dc == NDC - 1)
                E("tensor", lambda t, wp=wp, dc=dc, st=st, sp=sp, toff=toff:
                  t.matmul(PS[0][:, 0:512], w1b[wp][:, dc * 128:(dc + 1) * 128],
                           xn[:, dc * TPH + toff:dc * TPH + toff + 512],
                           start=st, stop=sp),
                  reads=[("w1b", wp), "xn"], writes=[("PS", 0)])
                E("tensor", lambda t, wp=wp, dc=dc, st=st, sp=sp, toff=toff:
                  t.matmul(PS[1][:, 0:512], w3b[wp][:, dc * 128:(dc + 1) * 128],
                           xn[:, dc * TPH + toff:dc * TPH + toff + 512],
                           start=st, stop=sp),
                  reads=[("w3b", wp), "xn"], writes=[("PS", 1)])
            E("scalar", lambda s: s.activation(gsb[:], PS[0][:, 0:512], AF.Silu),
              reads=[("PS", 0)], writes=["gsb"])
            sl = slot_of[hc]
            E("vector", lambda v, sl=sl:
              v.tensor_tensor(hsl_t[sl][:], PS[1][:, 0:512], gsb[:], ALU.mult),
              reads=[("PS", 1), "gsb"], writes=[("hs", sl)])
        step = 1
        while step < 64:
            nf = []
            base = 0
            while base < 64:
                for off in range(step):
                    a, b = base + off, base + off + step
                    sa, sb = slot_of[a], slot_of[b]
                    d0, d1 = free[0], free[1]
                    E("vector", lambda v, sa=sa, sb=sb, d0=d0:
                      v.tensor_tensor(hsl_t[d0][:], hsl_t[sa][:], hsl_t[sb][:],
                                      ALU.add),
                      reads=[("hs", sa), ("hs", sb)], writes=[("hs", d0)])
                    E("vector", lambda v, sa=sa, sb=sb, d1=d1:
                      v.tensor_tensor(hsl_t[d1][:], hsl_t[sa][:], hsl_t[sb][:],
                                      ALU.subtract),
                      reads=[("hs", sa), ("hs", sb)], writes=[("hs", d1)])
                    slot_of[a], slot_of[b] = d0, d1
                    free = [sa, sb]
                base += 2 * step
            step *= 2
        for hc in range(64):
            sl = slot_of[hc]
            E("tensor", lambda t, sl=sl:
              t.matmul(PS[2 + (sl % 2)][:, 0:512], s_h128[:], hsl_t[sl][:],
                       start=True, stop=True),
              reads=[("hs", sl), "h128"], writes=[("PS", 2 + (sl % 2))])
            E("vector", lambda v, sl=sl:
              v.tensor_scalar(hsl_t[sl][:], PS[2 + (sl % 2)][:, 0:512],
                              1.0 / math.sqrt(HID), None, ALU.mult),
              reads=[("PS", 2 + (sl % 2))], writes=[("hs", sl)])
        for dc in range(NDC):
            wp = dc % 2
            dma("gpsimd", w2b[wp][:].rearrange("p (c t) -> p c t", c=64),
                d_w2[dc].transpose([1, 0, 2]), writes=[("w2b", wp)])
            dma("sync", xrs[wp][:], d_x[dc * 128:(dc + 1) * 128,
                                        3 + tg * 512:3 + tg * 512 + 512],
                writes=[("xrs", wp)])
            dma("sync", atb[wp][:], d_attn[dc, :, tg * 512:(tg + 1) * 512],
                reads=[("attnspill", dc, tg)], writes=[("atb", wp)])
            for hc in range(64):
                st, sp = (hc == 0), (hc == 63)
                sl = slot_of[hc]
                E("tensor", lambda t, wp=wp, hc=hc, sl=sl, st=st, sp=sp:
                  t.matmul(PS[4][:, 0:512], w2b[wp][:, hc * 128:(hc + 1) * 128],
                           hsl_t[sl][:], start=st, stop=sp),
                  reads=[("w2b", wp), ("hs", sl)], writes=[("PS", 4)])
            E("vector", lambda v, wp=wp:
              v.tensor_tensor(outst[wp][:], PS[4][:, 0:512], xrs[wp][:],
                              ALU.add),
              reads=[("PS", 4), ("xrs", wp)], writes=[("outst", wp)])
            E("vector", lambda v, wp=wp:
              v.tensor_tensor(outst[wp][:], outst[wp][:], atb[wp][:], ALU.add),
              reads=[("atb", wp)], writes=[("outst", wp)])
            tk = dma("sync", d_out[dc * 128:(dc + 1) * 128,
                                   tg * 512:(tg + 1) * 512], outst[wp][:],
                     reads=[("outst", wp)], writes=[("outdone", dc, tg)])
            out_toks.append(tk)
    E("gpsimd", lambda g: g.engine_nop(), deps=out_toks)


def kernel(**inputs):
    per_core = _prep_host(inputs)
    nc = _build()
    res = run_bass_kernel_spmd(nc, per_core, core_ids=list(range(NCORE)))
    outs = [np.asarray(res.results[cid]["out_feat"]) for cid in range(NCORE)]
    full = np.concatenate([o.T for o in outs], axis=0)
    return np.ascontiguousarray(full.reshape(1, NCORE * TPC, DM)).astype(np.float32)


# version 09 (final)
# speedup vs baseline: 1.0473x; 1.0192x over previous
import math
import numpy as np
import ml_dtypes

import concourse.bass as bass
import concourse.mybir as mybir
from concourse.bass_utils import run_bass_kernel_spmd

F32 = mybir.dt.float32
BF16 = mybir.dt.bfloat16
I32 = mybir.dt.int32
AF = mybir.ActivationFunctionType
ALU = mybir.AluOpType
BF = ml_dtypes.bfloat16

NCORE = 8
TPC = 1024
TPH = TPC + 3
C = 128
NCH = TPC // C
NB = 32
DK = DV = 128
NKH, NVH = 16, 32
DM = 2048
HID = 8192
NDC = DM // 128
L2_EPS = 1e-6
RMS_EPS = 1e-6
LN_EPS = 1e-5
NEG = -1e30
MAGIC = 0x5F3759DF
T3S = [(0, 343), (343, 343), (686, 341)]


class Tok:
    __slots__ = ("eng", "idx", "dma", "sem_val", "observed")

    def __init__(self, eng, idx, dma):
        self.eng, self.idx, self.dma = eng, idx, dma
        self.sem_val = None
        self.observed = False


class Emitter:
    CENGS = ("tensor", "vector", "scalar", "gpsimd", "sync")

    def __init__(self, nc):
        self.nc = nc
        self.ops = {e: [] for e in self.CENGS}
        self.last_writer = {}
        self.readers = {}
        self.phase_deps = []

    def emit(self, eng, fn, reads=(), writes=(), deps=(), dma=False):
        t = Tok(eng, len(self.ops[eng]), dma)
        dl = list(deps) + list(self.phase_deps)
        for r in reads:
            w = self.last_writer.get(r)
            if w is not None:
                dl.append(w)
            self.readers.setdefault(r, []).append(t)
        for r in writes:
            for rd in self.readers.get(r, ()):
                dl.append(rd)
            w = self.last_writer.get(r)
            if w is not None:
                dl.append(w)
            self.readers[r] = []
            self.last_writer[r] = t
        dl2, seen = [], set()
        for d in dl:
            if d is t or id(d) in seen:
                continue
            seen.add(id(d))
            if (not d.dma) and d.eng == eng:
                continue
            dl2.append(d)
        self.ops[eng].append((t, fn, dl2))
        return t

    def phase_barrier(self):
        self.phase_deps = [ops[-1][0] for ops in self.ops.values() if ops]

    def finalize(self, block):
        for e in self.CENGS:
            for (t, fn, dl) in self.ops[e]:
                for d in dl:
                    d.observed = True
        for e in self.CENGS:
            cc, dc = 0, 0
            for (t, fn, dl) in self.ops[e]:
                if t.dma:
                    dc += 16
                    t.sem_val = dc
                    t.observed = True
                elif t.observed:
                    cc += 1
                    t.sem_val = cc
        nc = self.nc
        self.sems = {}
        for e in self.CENGS:
            self.sems[("c", e)] = nc.alloc_semaphore(f"sc_{e}")
            self.sems[("d", e)] = nc.alloc_semaphore(f"sd_{e}")
        em = self

        def run_engine(engobj, ename):
            hw = {}
            for (t, fn, dl) in em.ops[ename]:
                for d in dl:
                    key = ("d" if d.dma else "c", d.eng)
                    v = d.sem_val
                    if hw.get(key, 0) < v:
                        hw[key] = v
                        engobj.wait_ge(em.sems[key], v)
                inst = fn(engobj)
                if t.dma:
                    inst.then_inc(em.sems[("d", ename)], 16)
                elif t.observed:
                    if inst is None:
                        engobj.sem_inc(em.sems[("c", ename)], 1)
                    else:
                        inst.then_inc(em.sems[("c", ename)], 1)

        block.tensor(lambda t: run_engine(t, "tensor"))
        block.vector(lambda v: run_engine(v, "vector"))
        block.scalar(lambda s: run_engine(s, "scalar"))
        block.gpsimd(lambda g: run_engine(g, "gpsimd"))
        block.sync(lambda s: run_engine(s, "sync"))


def _quake(v, out, vin, ti, t1, t2, t3, n_newton=3, final_scale=None):
    v.tensor_scalar(ti, vin.bitcast(I32), 1, None, ALU.arith_shift_right)
    v.tensor_scalar(ti, ti, -1, None, ALU.bitwise_xor)
    v.tensor_scalar(ti, ti, MAGIC + 1, None, ALU.add)
    r_cur = ti.bitcast(F32)
    inst = None
    assert n_newton % 2 == 1
    for it in range(n_newton):
        v.tensor_tensor(t1, r_cur, r_cur, ALU.mult)
        v.tensor_tensor(t1, t1, vin, ALU.mult)
        v.tensor_scalar(t2, t1, -0.5, 1.5, ALU.mult, ALU.add)
        dst = out if it % 2 == 0 else t3
        inst = v.tensor_tensor(dst, r_cur, t2, ALU.mult)
        r_cur = dst
    if final_scale is not None:
        inst = v.tensor_scalar(out, out, final_scale, None, ALU.mult)
    return inst


def _prep_host(inputs):
    x = np.ascontiguousarray(np.asarray(inputs["L0_attn_hidden_states"],
                                        dtype=np.float32)[0])
    Wq = np.asarray(inputs["L0_attn_w_qkvz"], dtype=np.float32)
    Wba = np.asarray(inputs["L0_attn_w_ba"], dtype=np.float32)
    convw = np.asarray(inputs["L0_attn_conv_w"], dtype=np.float32)[:, 0, :]
    dtb = np.asarray(inputs["L0_attn_dt_bias"], dtype=np.float32)
    Alog = np.asarray(inputs["L0_attn_A_log"], dtype=np.float32)
    normw = np.asarray(inputs["L0_attn_norm_w"], dtype=np.float32)
    Wout = np.asarray(inputs["L0_attn_w_out"], dtype=np.float32)
    W1 = np.asarray(inputs["L0_ffn_W1"], dtype=np.float32)
    W2 = np.asarray(inputs["L0_ffn_W2"], dtype=np.float32)
    W3 = np.asarray(inputs["L0_ffn_W3"], dtype=np.float32)

    xT = np.ascontiguousarray(x.T)
    xslices = []
    for cid in range(NCORE):
        lo = TPC * cid
        sl = np.zeros((DM, TPH), np.float32)
        nh = min(3, lo)
        sl[:, 3 - nh:] = xT[:, lo - nh:lo + TPC]
        xslices.append(np.ascontiguousarray(sl))

    WQ = np.ascontiguousarray(Wq.reshape(NDC, 128, 24, 512).transpose(2, 1, 0, 3))
    bcols = np.array([4 * (h // 2) + (h % 2) for h in range(NVH)])
    Wba_p = np.ascontiguousarray(Wba[:, np.concatenate([bcols, bcols + 2])])
    convw_t = np.ascontiguousarray(convw.reshape(64, 128, 4).transpose(1, 0, 2))
    W1T = np.ascontiguousarray(W1.reshape(64, 128, NDC, 128).transpose(0, 2, 3, 1))
    W3T = np.ascontiguousarray(W3.reshape(64, 128, NDC, 128).transpose(0, 2, 3, 1))
    W2T = np.ascontiguousarray(W2.T.reshape(64, 128, NDC, 128).transpose(2, 0, 1, 3))
    WoutT = np.ascontiguousarray(Wout.reshape(NVH, 128, NDC, 128).transpose(2, 0, 1, 3))

    per_core = []
    for cid in range(NCORE):
        oh = np.zeros((128, 8), np.float32)
        oh[:, cid] = 1.0
        per_core.append(dict(
            x_sl=xslices[cid], wq=WQ, wba=Wba_p, convw=convw_t,
            dtb=np.ascontiguousarray(dtb.reshape(32, 1)),
            alog=np.ascontiguousarray(Alog.reshape(32, 1)),
            normw=np.ascontiguousarray(normw.reshape(128, 1)),
            w1t=W1T, w3t=W3T, w2t=W2T, woutt=WoutT, onehot=oh,
        ))
    return per_core


DEBUG = False


def _build(debug=False):
    nc = bass.Bass("TRN2", target_bir_lowering=False)
    em = Emitter(nc)
    E = em.emit
    dbg = {}
    if debug:
        for nm, shp, dt_ in [("dbg_xn", [128, NDC * TPH], BF16),
                             ("dbg_p0", [1, 3 * TPH], F32),
                             ("dbg_rows", [32, 2048], F32),
                             ("dbg_chst", [128, 16384], BF16),
                             ("dbg_rsq", [128, 64], F32),
                             ("dbg_cols", [128, 160], F32),
                             ("dbg_mco", [128, NVH * 128], BF16),
                             ("dbg_bco", [128, NVH * 128], BF16),
                             ("dbg_sin", [128, NVH * 128], BF16),
                             ("dbg_yst", [128, NVH * 512], BF16),
                             ("dbg_ba", [32, 2 * TPH], F32),
                             ("dbg_g", [32, 2112], F32),
                             ("dbg_lr", [32, 512], F32),
                             ("dbg_xg", [128, NVH * 128], BF16),
                             ("dbg_sz", [128, NVH * 128], BF16),
                             ("dbg_r5", [128, 64], F32)]:
            dbg[nm] = nc.dram_tensor(nm, shp, dt_, kind="ExternalOutput").ap()

    d_x = nc.dram_tensor("x_sl", [DM, TPH], F32, kind="ExternalInput").ap()
    d_wq = nc.dram_tensor("wq", [24, 128, NDC, 512], F32, kind="ExternalInput").ap()
    d_wba = nc.dram_tensor("wba", [DM, 64], F32, kind="ExternalInput").ap()
    d_convw = nc.dram_tensor("convw", [128, 64, 4], F32, kind="ExternalInput").ap()
    d_dtb = nc.dram_tensor("dtb", [32, 1], F32, kind="ExternalInput").ap()
    d_alog = nc.dram_tensor("alog", [32, 1], F32, kind="ExternalInput").ap()
    d_normw = nc.dram_tensor("normw", [128, 1], F32, kind="ExternalInput").ap()
    d_w1 = nc.dram_tensor("w1t", [64, NDC, 128, 128], F32, kind="ExternalInput").ap()
    d_w3 = nc.dram_tensor("w3t", [64, NDC, 128, 128], F32, kind="ExternalInput").ap()
    d_w2 = nc.dram_tensor("w2t", [NDC, 64, 128, 128], F32, kind="ExternalInput").ap()
    d_wo = nc.dram_tensor("woutt", [NDC, NVH, 128, 128], F32,
                          kind="ExternalInput").ap()
    d_oh = nc.dram_tensor("onehot", [128, 8], F32, kind="ExternalInput").ap()
    d_out = nc.dram_tensor("out_feat", [DM, TPC], F32, kind="ExternalOutput").ap()
    d_mixed = nc.dram_tensor("mixed_hbm", [64, 128, TPH], BF16).ap()
    d_conv = nc.dram_tensor("conv_hbm", [64, 128, TPC], BF16).ap()
    d_silz = nc.dram_tensor("silz_hbm", [32, 128, TPH], BF16).ap()
    d_chain = nc.dram_tensor("chain_hbm", [NCH, 128, 16384], BF16).ap()
    d_attn = nc.dram_tensor("attn_hbm", [NDC, 128, TPC], BF16).ap()
    d_ccin = nc.dram_tensor("cc_in", [128, 8192], BF16).ap()
    d_ccout = nc.dram_tensor("cc_out", [NCORE, 128, 8192], BF16,
                             addr_space="Shared").ap()

    ax0, ax1 = np.meshgrid(np.arange(C), np.arange(C), indexing="ij")
    c_mask_a = nc.inline_tensor(np.where(ax0 <= ax1, 0.0, NEG).astype(np.float32),
                                "c_mask_a").ap()
    c_mask_m = nc.inline_tensor(np.where(ax1 < ax0, 0.0, -NEG).astype(np.float32),
                                "c_mask_m").ap()
    c_bdneg = nc.inline_tensor((-((ax0 // NB) == (ax1 // NB)).astype(np.float32)
                                ).astype(BF), "c_bdneg").ap()
    c_bdl = nc.inline_tensor((((ax0 // NB) > (ax1 // NB)).astype(np.float32)
                              ).astype(BF), "c_bdl").ap()
    c_idb = nc.inline_tensor(np.eye(128, dtype=np.float32).astype(BF), "c_idb").ap()
    c_idf = nc.inline_tensor(np.eye(128, dtype=np.float32), "c_idf").ap()
    Hm = np.array([[1.0]], np.float32)
    while Hm.shape[0] < 128:
        Hm = np.block([[Hm, Hm], [Hm, -Hm]])
    c_h128 = nc.inline_tensor(Hm.astype(BF), "c_h128").ap()
    c_onescb = nc.inline_tensor(np.ones((128, 1), BF), "c_onescb").ap()
    c_onescf = nc.inline_tensor(np.ones((128, 1), np.float32), "c_onescf").ap()
    c_onesr = nc.inline_tensor(np.ones((32, 128), np.float32), "c_onesr").ap()
    c_onesrb = nc.inline_tensor(np.ones((32, 128), BF), "c_onesrb").ap()
    c_onesf128 = nc.inline_tensor(np.ones((128, 128), np.float32), "c_onesf128").ap()
    c_onesb128 = nc.inline_tensor(np.ones((128, 128), BF), "c_onesb128").ap()
    e127 = np.zeros((128, 1), np.float32); e127[127, 0] = 1.0
    c_e127 = nc.inline_tensor(e127, "c_e127").ap()

    A = nc.alloc_sbuf_tensor
    xn = A("xn", [128, NDC * TPH], BF16)
    s_maska = A("s_maska", [128, 128], F32)
    s_maskm = A("s_maskm", [128, 128], F32)
    s_bdneg = A("s_bdneg", [128, 128], BF16)
    s_bdl = A("s_bdl", [128, 128], BF16)
    s_idb = A("s_idb", [128, 128], BF16)
    s_idf = A("s_idf", [128, 128], F32)
    s_h128 = A("s_h128", [128, 128], BF16)
    s_onescb = A("s_onescb", [128, 1], BF16)
    s_onescf = A("s_onescf", [128, 1], F32)
    s_onesr = A("s_onesr", [32, 128], F32)
    s_onesrb = A("s_onesrb", [32, 128], BF16)
    s_onesf128 = A("s_onesf128", [128, 128], F32)
    s_onesb128 = A("s_onesb128", [128, 128], BF16)
    s_e127 = A("s_e127", [128, 1], F32)
    s_convw = A("s_convw", [128, 256], F32)
    s_dtb = A("s_dtb", [32, 1], F32)
    s_alog = A("s_alog", [32, 1], F32)
    s_normw = A("s_normw", [128, 1], F32)
    s_oh = A("s_oh", [128, 8], F32)
    G = A("G", [32, 1024], F32)
    lnb = A("lnb", [32, 1024], F32)
    s_bb = A("s_bb", [32, TPH], F32)
    s_aa = A("s_aa", [32, TPH], F32)
    q_i = A("q_i", [128, 128], I32)
    q_t1 = A("q_t1", [128, 128], F32)
    q_t2 = A("q_t2", [128, 128], F32)
    q_t3 = A("q_t3", [128, 128], F32)
    S_pp = [A(f"S_{i}", [128, NVH * 128], BF16) for i in range(2)]

    ab = (nc.sbuf_base + 63) & ~63
    cap = nc.sbuf_top
    SZA = 68 * 1024
    OB = SZA

    def at(name, shape, dtype, off):
        esz = {F32: 4, BF16: 2, I32: 4}[dtype]
        end = ab + off + int(np.prod(shape[1:])) * esz
        assert end <= cap, f"{name}: arena overflow {end} > {cap}"
        return nc.alloc_sbuf_tensor_at(name, shape, dtype, offset=ab + off)

    class Bump:
        def __init__(self, base, limit):
            self.off, self.limit = base, limit

        def al(self, name, shape, dtype, align=64):
            esz = {F32: 4, BF16: 2, I32: 4}[dtype]
            self.off = (self.off + align - 1) & ~(align - 1)
            t = at(name, shape, dtype, self.off)
            self.off += int(np.prod(shape[1:])) * esz
            assert ab + self.off <= cap, f"{name}: bump overflow"
            return t

    bA = Bump(0, SZA)
    xf32 = bA.al("xf32", [128, NDC * TPH], F32)
    bA = Bump(0, SZA)
    cvb = bA.al("cvb", [128, 64 * 128], BF16)
    chst = bA.al("chst", [128, 16384], BF16)
    bA = Bump(0, SZA)
    chb = [bA.al(f"chb{i}", [128, 16384], BF16) for i in range(2)]
    bA = Bump(0, SZA)
    hsl = [bA.al(f"hsl{i}", [128, 512], BF16) for i in range(66)]

    b = Bump(OB, cap)
    xsqt = b.al("xsqt", [128, TPH], F32)
    mrow = b.al("mrow", [1, TPH], F32)
    invrow = b.al("invrow", [1, TPH], F32)
    nmrow = b.al("nmrow", [1, TPH], F32)
    vtmp = b.al("vtmp", [1, TPH], F32)
    vti = b.al("vti", [1, TPH], I32)
    vt3 = b.al("vt3", [1, TPH], F32)
    b = Bump(OB, cap)
    wqb = [b.al(f"wqb{i}", [128, NDC * 512], BF16) for i in range(2)]
    qev = [b.al(f"qev{i}", [128, TPH], BF16) for i in range(2)]
    s_wba = b.al("s_wba", [128, NDC * 64], BF16)
    b = Bump(OB, cap)
    mxb = b.al("mxb", [128, 64 * 131], BF16)
    cvo = b.al("cvo", [128, 64 * 128], BF16)
    cvs = b.al("cvs", [128, 64 * 128], BF16)
    b = Bump(OB, cap)
    Mco = [b.al(f"Mco_{i}", [128, NVH * 128], BF16) for i in range(2)]
    bco = [b.al(f"bco_{i}", [128, NVH * 128], BF16) for i in range(2)]
    mco_end = b.off
    ktok = [b.al(f"ktok{i}", [128, 128], BF16) for i in range(2)]
    vtok = [b.al(f"vtok{i}", [128, 128], BF16) for i in range(4)]
    khat = [b.al(f"khat{i}", [128, 128], BF16) for i in range(2)]
    Mn_t = [b.al(f"Mn_t{i}", [128, 128], BF16) for i in range(2)]
    At_t = [b.al(f"At_t{i}", [128, 128], BF16) for i in range(2)]
    nbar = [b.al(f"nbar{i}", [128, 128], BF16) for i in range(2)]
    napp = [b.al(f"napp{i}", [128, 128], BF16) for i in range(2)]
    mln = [b.al(f"mln{i}", [128, 128], BF16) for i in range(2)]
    Lapp = [[b.al(f"Lapp{p}_{i}", [128, 128], BF16) for i in range(4)]
            for p in range(2)]
    Lnrm = [[b.al(f"Lnrm{p}_{i}", [128, 128], BF16) for i in range(4)]
            for p in range(2)]
    dcur = [[b.al(f"dcur{p}_{i}", [128, 128], BF16) for i in range(2)]
            for p in range(2)]
    negG = [b.al(f"negG{i}", [128, 128], BF16) for i in range(2)]
    rhsb = [b.al(f"rhsb{i}", [128, 256], BF16) for i in range(2)]
    zev = [[b.al(f"zev{p}_{i}", [128, 256], BF16) for i in range(2)]
           for p in range(2)]
    y0t = [b.al(f"y0t{i}", [128, 256], BF16) for i in range(2)]
    etmp = [b.al(f"etmp{i}", [128, 128], F32) for i in range(2)]
    dmin = [b.al(f"dmin{i}", [128, 128], F32) for i in range(2)]
    dexp = [b.al(f"dexp{i}", [128, 128], BF16) for i in range(2)]
    sq_t = b.al("sq_t", [128, 512], BF16)
    nrm2 = b.al("nrm2", [128, 64], F32)
    rsq = b.al("rsq", [128, 64], F32)
    rqr = b.al("rqr", [32, 128], F32)
    rkr = b.al("rkr", [32, 128], F32)
    lnrq = b.al("lnrq", [32, 128], F32)
    lnrk = b.al("lnrk", [32, 128], F32)
    growk = b.al("growk", [32, 128], F32)
    gmgc = b.al("gmgc", [32, 128], F32)
    mcrow = b.al("mcrow", [32, 128], F32)
    colsF = b.al("colsF", [128, 128], F32)
    khcol = b.al("khcol", [128, 32], F32)
    bgcol = b.al("bgcol", [128, 32], F32)
    bbcol = b.al("bbcol", [128, 32], F32)
    tcol = b.al("tcol", [128, 32], F32)
    arow = b.al("arow", [32, 128], F32)
    gcI = [b.al(f"gcI{i}", [128, 128], BF16) for i in range(2)]
    dgt = [[b.al(f"dgt{p}_{i}", [128, 128], F32) for i in range(2)]
           for p in range(2)]
    dgb = [b.al(f"dgb{i}", [128, 128], BF16) for i in range(2)]
    cols2 = b.al("cols2", [128, 32], F32)
    grqcol = b.al("grqcol", [128, 32], F32)
    gamcols = b.al("gamcols", [128, 32], F32)
    gcsel = [b.al(f"gcsel{i}", [128, 1], F32) for i in range(2)]
    grow = b.al("grow", [32, 1024], F32)
    grow2 = b.al("grow2", [32, 1024], F32)
    b = Bump(OB, cap)
    Sacc = [b.al(f"Sacc{i}", [128, NVH * 128], F32) for i in range(2)]
    assert b.off - OB <= mco_end - OB
    b = Bump(mco_end, cap)
    gbuf = [b.al(f"gbuf{i}", [128, 8192], BF16) for i in range(2)]
    b = Bump(OB, cap)
    szb = [b.al(f"szb{i}", [128, NVH * 128], BF16) for i in range(2)]
    xgst = b.al("xgst", [128, NVH * 128], BF16)
    yst = b.al("yst", [128, NVH * 512], BF16)
    wob = [b.al(f"wob{i}", [128, NVH * 64], BF16) for i in range(2)]
    astg = b.al("astg", [128, 512], BF16)
    rmsr = b.al("rmsr", [32, 128], F32)
    rsq5 = b.al("rsq5", [128, 64], F32)
    nrm5 = b.al("nrm5", [128, 64], F32)
    p5dg = [b.al(f"p5dg{i}", [128, 128], BF16) for i in range(2)]
    sqt5 = [b.al(f"sqt5{i}", [128, 128], BF16) for i in range(2)]
    b = Bump(OB, cap)
    w1b = [b.al(f"w1b{i}", [128, NDC * 128], BF16) for i in range(2)]
    w3b = [b.al(f"w3b{i}", [128, NDC * 128], BF16) for i in range(2)]
    w2b = [b.al(f"w2b{i}", [128, 64 * 128], BF16) for i in range(2)]
    gsb = b.al("gsb", [128, 512], BF16)
    xrs = [b.al(f"xrs{i}", [128, 512], F32) for i in range(2)]
    atb = [b.al(f"atb{i}", [128, 512], BF16) for i in range(2)]
    outst = [b.al(f"outst{i}", [128, 512], F32) for i in range(2)]

    PS = [nc.alloc_psum_tensor(f"ps{i}", [128, 512], F32) for i in range(6)]
    PSB = [nc.alloc_psum_tensor(f"psb{i}", [128, 1024], BF16) for i in range(2)]

    def dma(q, dst, src, reads=(), writes=(), deps=()):
        return E(q, lambda e, dst=dst, src=src: e.dma_start(out=dst, in_=src),
                 reads=reads, writes=writes, deps=deps, dma=True)

    dma("sync", s_maska[:], c_mask_a, writes=["maska"])
    dma("sync", s_maskm[:], c_mask_m, writes=["maskm"])
    dma("sync", s_bdneg[:], c_bdneg, writes=["bdneg"])
    dma("sync", s_bdl[:], c_bdl, writes=["bdl"])
    dma("sync", s_idb[:], c_idb, writes=["idb"])
    dma("sync", s_idf[:], c_idf, writes=["idf"])
    dma("sync", s_h128[:], c_h128, writes=["h128"])
    dma("sync", s_onescb[:], c_onescb, writes=["onescb"])
    dma("sync", s_onescf[:], c_onescf, writes=["onescf"])
    dma("sync", s_onesr[:], c_onesr, writes=["onesr"])
    dma("sync", s_onesrb[:], c_onesrb, writes=["onesrb"])
    dma("sync", s_onesf128[:], c_onesf128, writes=["onesf128"])
    dma("sync", s_onesb128[:], c_onesb128, writes=["onesb128"])
    dma("sync", s_e127[:], c_e127, writes=["e127"])
    dma("sync", s_convw[:], d_convw.rearrange("p c k -> p (c k)"), writes=["convw"])
    dma("sync", s_dtb[:], d_dtb, writes=["dtb"])
    dma("sync", s_alog[:], d_alog, writes=["alog"])
    dma("sync", s_normw[:], d_normw, writes=["normw"])
    dma("sync", s_oh[:], d_oh, writes=["onehot"])
    dma("gpsimd", s_wba[:].rearrange("p (dc c) -> p dc c", dc=NDC),
        d_wba.rearrange("(dc p) c -> p dc c", p=128), writes=["wba"])
    dma("sync", xf32[:].rearrange("p (dc t) -> p dc t", dc=NDC),
        d_x.rearrange("(dc p) t -> p dc t", p=128), writes=["xf32"])

    for (t0, tl) in T3S:
        for dc in range(NDC):
            st, sp = (dc == 0), (dc == NDC - 1)
            E("tensor", lambda t, dc=dc, t0=t0, tl=tl, st=st, sp=sp:
              t.matmul(PS[0][0:1, 0:tl], s_onescf[:],
                       xf32[:, dc * TPH + t0:dc * TPH + t0 + tl],
                       start=st, stop=sp),
              reads=["xf32", "onescf"], writes=[("PS", 0)])
        E("vector", lambda v, t0=t0, tl=tl:
          v.tensor_scalar(mrow[0:1, t0:t0 + tl], PS[0][0:1, 0:tl], 1.0 / DM,
                          None, ALU.mult),
          reads=[("PS", 0)], writes=["mrow"])
    for (t0, tl) in T3S:
        for dc in range(NDC):
            E("vector", lambda v, dc=dc, t0=t0, tl=tl:
              v.tensor_tensor(xsqt[:, t0:t0 + tl],
                              xf32[:, dc * TPH + t0:dc * TPH + t0 + tl],
                              xf32[:, dc * TPH + t0:dc * TPH + t0 + tl],
                              ALU.mult),
              reads=["xf32"], writes=["xsqt"])
            st, sp = (dc == 0), (dc == NDC - 1)
            E("tensor", lambda t, t0=t0, tl=tl, st=st, sp=sp:
              t.matmul(PS[1][0:1, 0:tl], s_onescf[:], xsqt[:, t0:t0 + tl],
                       start=st, stop=sp),
              reads=["xsqt", "onescf"], writes=[("PS", 1)])
        E("vector", lambda v, t0=t0, tl=tl:
          v.tensor_scalar(vtmp[0:1, t0:t0 + tl], PS[1][0:1, 0:tl], 1.0 / DM,
                          None, ALU.mult),
          reads=[("PS", 1)], writes=["vtmp"])
    E("vector", lambda v: v.tensor_tensor(nmrow[:], mrow[:], mrow[:], ALU.mult),
      reads=["mrow"], writes=["nmrow"])
    E("vector", lambda v: v.tensor_tensor(vtmp[:], vtmp[:], nmrow[:], ALU.subtract),
      reads=["nmrow"], writes=["vtmp"])
    E("vector", lambda v: v.tensor_scalar(vtmp[:], vtmp[:], LN_EPS, None, ALU.add),
      writes=["vtmp"])
    E("vector", lambda v: _quake(v, invrow[:], vtmp[:], vti[:],
                                 nmrow[:], xsqt[0:1, 0:TPH], vt3[:]),
      reads=["vtmp"], writes=["invrow", "nmrow", "xsqt", "vti", "vt3"])
    E("vector", lambda v: v.tensor_tensor(nmrow[:], mrow[:], invrow[:], ALU.mult),
      reads=["mrow", "invrow"], writes=["nmrow"])
    E("vector", lambda v: v.tensor_scalar(nmrow[:], nmrow[:], -1.0, None, ALU.mult),
      writes=["nmrow"])
    for (t0, tl) in T3S:
        E("tensor", lambda t, t0=t0, tl=tl:
          t.matmul(PS[2][:, 0:tl], s_onesr[0:1, :], invrow[0:1, t0:t0 + tl],
                   start=True, stop=True),
          reads=["invrow", "onesr"], writes=[("PS", 2)])
        E("tensor", lambda t, t0=t0, tl=tl:
          t.matmul(PS[3][:, 0:tl], s_onesr[0:1, :], nmrow[0:1, t0:t0 + tl],
                   start=True, stop=True),
          reads=["nmrow", "onesr"], writes=[("PS", 3)])
        for dc in range(NDC):
            E("vector", lambda v, dc=dc, t0=t0, tl=tl:
              v.tensor_tensor(xsqt[:, t0:t0 + tl],
                              xf32[:, dc * TPH + t0:dc * TPH + t0 + tl],
                              PS[2][:, 0:tl], ALU.mult),
              reads=["xf32", ("PS", 2)], writes=["xsqt"])
            E("vector", lambda v, dc=dc, t0=t0, tl=tl:
              v.tensor_tensor(xn[:, dc * TPH + t0:dc * TPH + t0 + tl],
                              xsqt[:, t0:t0 + tl], PS[3][:, 0:tl], ALU.add),
              reads=["xsqt", ("PS", 3)], writes=["xn"])

    em.phase_barrier()
    if debug:
        dma("sync", dbg["dbg_xn"], xn[:])
        dma("sync", dbg["dbg_p0"][0:1, 0:TPH], mrow[:])
        dma("sync", dbg["dbg_p0"][0:1, TPH:2 * TPH], invrow[:])
        dma("sync", dbg["dbg_p0"][0:1, 2 * TPH:3 * TPH], vtmp[:])
        em.phase_barrier()

    def spill_dst(cc):
        kh, r = divmod(cc, 6)
        if r == 0:
            return d_mixed[kh]
        if r == 1:
            return d_mixed[16 + kh]
        if r in (2, 3):
            return d_mixed[32 + 2 * kh + (r - 2)]
        return None

    for wg in range(24):
        wp = wg % 2
        dma("gpsimd", wqb[wp][:], d_wq[wg].rearrange("p dc c -> p (dc c)"),
            writes=[("wqb", wp)])
        for oc4 in range(4):
            cc = wg * 4 + oc4
            kh, r = divmod(cc, 6)
            qp = cc % 2
            bank = 4 + (cc % 2)
            for (t0, tl) in T3S:
                for dc in range(NDC):
                    st, sp = (dc == 0), (dc == NDC - 1)
                    E("tensor", lambda t, dc=dc, oc4=oc4, t0=t0, tl=tl, st=st,
                      sp=sp, wp=wp, bank=bank:
                      t.matmul(PS[bank][:, 0:tl],
                               wqb[wp][:, dc * 512 + oc4 * 128:
                                       dc * 512 + (oc4 + 1) * 128],
                               xn[:, dc * TPH + t0:dc * TPH + t0 + tl],
                               start=st, stop=sp),
                      reads=[("wqb", wp), "xn"], writes=[("PS", bank)])
                if r in (4, 5):
                    E("scalar", lambda s, t0=t0, tl=tl, qp=qp, bank=bank:
                      s.activation(qev[qp][:, t0:t0 + tl], PS[bank][:, 0:tl],
                                   AF.Silu),
                      reads=[("PS", bank)], writes=[("qev", qp)])
                else:
                    E("scalar", lambda s, t0=t0, tl=tl, qp=qp, bank=bank:
                      s.activation(qev[qp][:, t0:t0 + tl], PS[bank][:, 0:tl],
                                   AF.Copy),
                      reads=[("PS", bank)], writes=[("qev", qp)])
            if r in (4, 5):
                dst = d_silz[2 * kh + (r - 4)]
            else:
                dst = spill_dst(cc)
            dma("sync", dst, qev[qp][:], reads=[("qev", qp)],
                writes=[("spill", cc)])
    for bh in range(2):
        for (t0, tl) in T3S:
            for dc in range(NDC):
                st, sp = (dc == 0), (dc == NDC - 1)
                E("tensor", lambda t, dc=dc, t0=t0, tl=tl, st=st, sp=sp, bh=bh:
                  t.matmul(PS[0][0:32, 0:tl],
                           s_wba[:, dc * 64 + bh * 32:dc * 64 + bh * 32 + 32],
                           xn[:, dc * TPH + t0:dc * TPH + t0 + tl],
                           start=st, stop=sp),
                  reads=["wba", "xn"], writes=[("PS", 0)])
            E("vector", lambda v, t0=t0, tl=tl, bh=bh:
              v.tensor_copy((s_bb if bh == 0 else s_aa)[:, t0:t0 + tl],
                            PS[0][0:32, 0:tl]),
              reads=[("PS", 0)], writes=["ba"])

    em.phase_barrier()

    for cc8 in range(NCH):
        dma("sync", mxb[:].rearrange("p (c t) -> p c t", c=64),
            d_mixed[:, :, cc8 * C:cc8 * C + 131].transpose([1, 0, 2]),
            reads=[("spill", 0)], writes=["mxb"])
        for c64 in range(64):
            def fconv(v, c64=c64):
                o = cvo[:, c64 * 128:(c64 + 1) * 128]
                m0 = mxb[:, c64 * 131:c64 * 131 + 128]
                m1 = mxb[:, c64 * 131 + 1:c64 * 131 + 129]
                m2 = mxb[:, c64 * 131 + 2:c64 * 131 + 130]
                m3 = mxb[:, c64 * 131 + 3:c64 * 131 + 131]
                v.tensor_scalar(o, m0, s_convw[:, 4 * c64:4 * c64 + 1], None,
                                ALU.mult)
                v.scalar_tensor_tensor(o, m1, s_convw[:, 4 * c64 + 1:4 * c64 + 2],
                                       o, ALU.mult, ALU.add)
                v.scalar_tensor_tensor(o, m2, s_convw[:, 4 * c64 + 2:4 * c64 + 3],
                                       o, ALU.mult, ALU.add)
                return v.scalar_tensor_tensor(
                    o, m3, s_convw[:, 4 * c64 + 3:4 * c64 + 4], o,
                    ALU.mult, ALU.add)
            E("vector", fconv, reads=["mxb", "convw"], writes=[("cvo", c64)])
        for g4 in range(16):
            E("scalar", lambda s, g4=g4:
              s.activation(cvs[:, g4 * 512:(g4 + 1) * 512],
                           cvo[:, g4 * 512:(g4 + 1) * 512], AF.Silu),
              reads=[("cvo", 4 * g4), ("cvo", 4 * g4 + 1), ("cvo", 4 * g4 + 2),
                     ("cvo", 4 * g4 + 3)], writes=[("cvs", g4)])
        dma("sync", d_conv[:, :, cc8 * C:(cc8 + 1) * C].transpose([1, 0, 2]),
            cvs[:].rearrange("p (c t) -> p c t", c=64),
            reads=[("cvs", g) for g in range(16)], writes=[("convspill", cc8)])

    em.phase_barrier()

    br = s_bb[0:32, 3:3 + 1024]
    ar = s_aa[0:32, 3:3 + 1024]
    E("scalar", lambda s: s.activation(grow[:], br, AF.Exp, scale=-1.0),
      reads=["ba"], writes=["grow"])
    E("vector", lambda v: v.tensor_scalar(grow[:], grow[:], 1.0, None, ALU.add),
      writes=["grow"])
    E("scalar", lambda s: s.activation(grow[:], grow[:], AF.Ln), writes=["grow"])
    E("vector", lambda v: v.tensor_scalar(lnb[:], grow[:], -1.0, None, ALU.mult),
      reads=["grow"], writes=["lnbr"])
    E("scalar", lambda s: s.activation(grow[:], ar, AF.Exp, bias=s_dtb[:]),
      reads=["ba", "dtb"], writes=["grow"])
    E("vector", lambda v: v.tensor_scalar(grow[:], grow[:], 1.0, None, ALU.add),
      writes=["grow"])
    E("scalar", lambda s: s.activation(grow[:], grow[:], AF.Ln), writes=["grow"])
    E("scalar", lambda s: s.activation(q_t1[0:32, 0:1], s_alog[:], AF.Exp),
      reads=["alog"], writes=["aexp"])
    E("vector", lambda v: v.tensor_scalar(q_t2[0:32, 0:1], q_t1[0:32, 0:1], -1.0,
                                          None, ALU.mult),
      reads=["aexp"], writes=["negaexp"])
    E("scalar", lambda s: s.activation(grow2[:], grow[:], AF.Copy,
                                       scale=q_t2[0:32, 0:1]),
      reads=["grow", "negaexp"], writes=["grow2"])
    for cc8 in range(NCH):
        E("vector", lambda v, cc8=cc8:
          v.tensor_tensor_scan(G[:, cc8 * C:(cc8 + 1) * C],
                               grow2[:, cc8 * C:(cc8 + 1) * C],
                               grow2[:, cc8 * C:(cc8 + 1) * C],
                               0.0, ALU.add, ALU.bypass),
          reads=["grow2"], writes=[("Grow", cc8)])

    em.phase_barrier()
    if debug:
        dma("sync", dbg["dbg_rows"][:, 0:1024], G[:])
        dma("sync", dbg["dbg_rows"][:, 1024:2048], lnb[:])
        dma("sync", dbg["dbg_ba"][:, 0:TPH], s_bb[:])
        dma("sync", dbg["dbg_ba"][:, TPH:2 * TPH], s_aa[:])
        dma("sync", dbg["dbg_g"][:, 0:1024], grow2[:])
        dma("sync", dbg["dbg_g"][:, 1024:2048], grow[:])
        E("sync", lambda e: e.dma_start(out=dbg["dbg_g"][:, 2048:2049],
                                        in_=q_t1[0:32, 0:1],
                                        allow_slow_non_contiguous=True), dma=True)
        E("sync", lambda e: e.dma_start(out=dbg["dbg_g"][:, 2049:2050],
                                        in_=q_t2[0:32, 0:1],
                                        allow_slow_non_contiguous=True), dma=True)
        em.phase_barrier()

    _p2b(nc, em, locals())
    em.phase_barrier()
    if debug:
        dma("sync", dbg["dbg_chst"], chst[:])
        dma("sync", dbg["dbg_rsq"], rsq[:])
        dma("sync", dbg["dbg_cols"][:, 0:128], colsF[:])
        dma("sync", dbg["dbg_cols"][:, 128:160], cols2[:])
        dma("sync", dbg["dbg_lr"][:, 0:128], rqr[:])
        dma("sync", dbg["dbg_lr"][:, 128:256], rkr[:])
        dma("sync", dbg["dbg_lr"][:, 256:384], lnrq[:])
        dma("sync", dbg["dbg_lr"][:, 384:512], lnrk[:])
        dma("sync", dbg["dbg_mco"], Mco[0][:])
        dma("sync", dbg["dbg_bco"], bco[0][:])
        em.phase_barrier()
    _p3(nc, em, locals())
    em.phase_barrier()
    if debug:
        dma("sync", dbg["dbg_sin"], S_pp[0][:])
        em.phase_barrier()
    _p5(nc, em, locals())
    em.phase_barrier()
    if debug:
        dma("sync", dbg["dbg_yst"], yst[:])
        dma("sync", dbg["dbg_xg"], xgst[:])
        dma("sync", dbg["dbg_sz"], szb[1][:])
        dma("sync", dbg["dbg_r5"][:, 0:32], rsq5[:, 0:32])
        dma("sync", dbg["dbg_r5"][:, 32:64], nrm5[:, 0:32])
        em.phase_barrier()
    _p4(nc, em, locals())

    with nc.Block() as block:
        em.finalize(block)
    nc._em = em
    return nc


def _p2b(nc, em, L):
    E = em.emit
    g = lambda n: L[n]
    (PS, PSB, cvb, chst, sq_t, nrm2, rsq, rqr, rkr, lnrq, lnrk, growk, gmgc,
     mcrow, colsF, khcol, bgcol,
     bbcol, tcol, arow, gcI, ktok, vtok, khat, Mn_t, At_t, nbar,
     napp, mln, Lapp, Lnrm, dcur, negG, rhsb, zev, y0t, etmp, dmin, dexp,
     G, lnb, s_onesr, s_onesrb, s_maska, s_maskm, s_bdneg, s_bdl, s_idb,
     s_onescb, s_onescf, Mco, bco, d_conv, d_chain, q_i, q_t1, q_t2) = (
        g("PS"), g("PSB"), g("cvb"), g("chst"), g("sq_t"), g("nrm2"), g("rsq"),
        g("rqr"), g("rkr"), g("lnrq"), g("lnrk"), g("growk"), g("gmgc"),
        g("mcrow"), g("colsF"), g("khcol"), g("bgcol"),
        g("bbcol"), g("tcol"), g("arow"), g("gcI"),
        g("ktok"), g("vtok"), g("khat"), g("Mn_t"), g("At_t"), g("nbar"),
        g("napp"), g("mln"), g("Lapp"), g("Lnrm"), g("dcur"), g("negG"),
        g("rhsb"), g("zev"), g("y0t"), g("etmp"), g("dmin"), g("dexp"),
        g("G"), g("lnb"), g("s_onesr"), g("s_onesrb"), g("s_maska"),
        g("s_maskm"), g("s_bdneg"), g("s_bdl"), g("s_idb"), g("s_onescb"),
        g("s_onescf"), g("Mco"), g("bco"), g("d_conv"), g("d_chain"),
        g("q_i"), g("q_t1"), g("q_t2"))
    dgt, dgb, cols2, grqcol, gamcols, gcsel = (g("dgt"), g("dgb"), g("cols2"),
                                               g("grqcol"), g("gamcols"),
                                               g("gcsel"))
    s_onesf128, s_onesb128, s_e127 = (g("s_onesf128"), g("s_onesb128"),
                                      g("s_e127"))
    dma = lambda q, dst, src, **kw: E(
        q, lambda e, dst=dst, src=src: e.dma_start(out=dst, in_=src),
        dma=True, **kw)

    for h in range(NVH):
        E("vector", lambda v, h=h:
          v.tensor_copy(Mco[0][:, h * 128:(h + 1) * 128], s_idb[:]),
          reads=["idb"], writes=[("Mco", 0, h)])
    E("vector", lambda v: v.memset(bco[0][:], 0.0),
      writes=[("bco", 0, h2) for h2 in range(NVH)])

    for cc8 in range(NCH):
        dma("sync", cvb[:].rearrange("p (c t) -> p c t", c=64),
            d_conv[:, :, cc8 * C:(cc8 + 1) * C].transpose([1, 0, 2]),
            reads=[("convspill", cc8)], writes=["cvb"])
        tend = cc8 * C + C - 1
        Gsl = G[:, cc8 * C:(cc8 + 1) * C]
        lnbsl = lnb[:, cc8 * C:(cc8 + 1) * C]
        for grp in range(8):
            which, kh0 = divmod(grp, 4)
            toff = which * 2048 + kh0 * 512
            E("vector", lambda v, toff=toff:
              v.tensor_tensor(sq_t[:], cvb[:, toff:toff + 512],
                              cvb[:, toff:toff + 512], ALU.mult),
              reads=["cvb"], writes=["sq_t"])
            for k4 in range(4):
                kh = kh0 * 4 + k4
                for dup in range(2):
                    col = (0 if which == 0 else 32) + 2 * kh + dup
                    E("tensor", lambda t, col=col, k4=k4:
                      t.matmul(PS[0][:, col:col + 1],
                               sq_t[:, k4 * 128:(k4 + 1) * 128], s_onescb[:],
                               start=True, stop=True),
                      reads=["sq_t", "onescb"], writes=[("PS", 0)])
        E("vector", lambda v: v.tensor_scalar(nrm2[:], PS[0][:, 0:64], L2_EPS,
                                              None, ALU.add),
          reads=[("PS", 0)], writes=["nrm2"])
        E("vector", lambda v: _quake(v, rsq[:], nrm2[:], q_i[:, 0:64],
                                     q_t1[:, 0:64], q_t2[:, 0:64],
                                     L["q_t3"][:, 0:64]),
          reads=["nrm2"], writes=["rsq", "qscr"])
        E("vector", lambda v: v.tensor_scalar(rsq[:, 0:32], rsq[:, 0:32],
                                              1.0 / math.sqrt(DK), None,
                                              ALU.mult),
          writes=["rsq"])
        E("tensor", lambda t: t.transpose(PS[1][0:32, 0:128], rsq[:, 0:32],
                                          L["s_idf"][:]),
          reads=["rsq", "idf"], writes=[("PS", 1)])
        E("vector", lambda v: v.tensor_copy(rqr[:], PS[1][0:32, 0:128]),
          reads=[("PS", 1)], writes=["rqr"])
        E("tensor", lambda t: t.transpose(PS[1][0:32, 128:256], rsq[:, 32:64],
                                          L["s_idf"][:]),
          reads=["rsq", "idf"], writes=[("PS", 1)])
        E("vector", lambda v: v.tensor_copy(rkr[:], PS[1][0:32, 128:256]),
          reads=[("PS", 1)], writes=["rkr"])
        E("scalar", lambda s: s.activation(lnrq[:], rqr[:], AF.Ln),
          reads=["rqr"], writes=["lnrq"])
        E("scalar", lambda s: s.activation(lnrk[:], rkr[:], AF.Ln),
          reads=["rkr"], writes=["lnrk"])
        E("vector", lambda v, Gsl=Gsl:
          v.tensor_tensor(arow[:], Gsl, lnrq[:], ALU.add),
          reads=[("Grow", cc8), "lnrq"], writes=["arow"])
        E("vector", lambda v, Gsl=Gsl:
          v.tensor_tensor(growk[:], Gsl, lnrk[:], ALU.subtract),
          reads=[("Grow", cc8), "lnrk"], writes=["growk"])
        E("gpsimd", lambda g2, Gsl=Gsl, tend=tend:
          g2.tensor_scalar(gmgc[:], Gsl, G[:, tend:tend + 1], None,
                           ALU.subtract),
          reads=[("Grow", cc8)], writes=["gmgc"])
        E("vector", lambda v, Gsl=Gsl, lnbsl=lnbsl:
          v.tensor_tensor(mcrow[:], Gsl, lnbsl, ALU.add),
          reads=[("Grow", cc8), "lnbr"], writes=["mcrow"])
        E("vector", lambda v:
          v.tensor_tensor(mcrow[:], mcrow[:], lnrk[:], ALU.add),
          reads=["lnrk"], writes=["mcrow"])
        E("tensor", lambda t: t.transpose(PS[1][:, 0:32], growk[:],
                                          L["s_idf"][0:32, 0:32]),
          reads=["growk", "idf"], writes=[("PS", 1)])
        E("tensor", lambda t: t.transpose(PS[1][:, 32:64], gmgc[:],
                                          L["s_idf"][0:32, 0:32]),
          reads=["gmgc", "idf"], writes=[("PS", 1)])
        E("tensor", lambda t: t.transpose(PS[1][:, 64:96], mcrow[:],
                                          L["s_idf"][0:32, 0:32]),
          reads=["mcrow", "idf"], writes=[("PS", 1)])
        E("tensor", lambda t, Gsl=Gsl: t.transpose(PS[1][:, 96:128], Gsl,
                                                   L["s_idf"][0:32, 0:32]),
          reads=[("Grow", cc8), "idf"], writes=[("PS", 1)])
        E("tensor", lambda t: t.transpose(PS[1][:, 128:160], arow[:],
                                          L["s_idf"][0:32, 0:32]),
          reads=["arow", "idf"], writes=[("PS", 1)])
        E("scalar", lambda s: s.activation(colsF[:], PS[1][:, 0:128], AF.Copy),
          reads=[("PS", 1)], writes=["colsF"])
        E("scalar", lambda s: s.activation(cols2[:], PS[1][:, 128:160], AF.Copy),
          reads=[("PS", 1)], writes=["cols2"])
        E("scalar", lambda s: s.activation(grqcol[:], cols2[:], AF.Exp),
          reads=["cols2"], writes=["grqcol"])
        E("scalar", lambda s: s.activation(gamcols[:], colsF[:, 96:128], AF.Exp),
          reads=["colsF"], writes=["gamcols"])
        E("scalar", lambda s: s.activation(khcol[:], colsF[:, 32:64], AF.Exp,
                                           scale=-1.0),
          reads=["colsF"], writes=["khcol"])
        E("vector", lambda v: v.tensor_tensor(tcol[:], colsF[:, 64:96],
                                              colsF[:, 0:32], ALU.add),
          reads=["colsF"], writes=["tcol"])
        E("vector", lambda v: v.scalar_tensor_tensor(tcol[:], colsF[:, 96:128],
                                                     -2.0, tcol[:], ALU.mult,
                                                     ALU.add),
          reads=["colsF"], writes=["tcol"])
        E("scalar", lambda s: s.activation(bbcol[:], tcol[:], AF.Exp),
          reads=["tcol"], writes=["bbcol"])
        E("vector", lambda v: v.tensor_tensor(tcol[:], tcol[:], colsF[:, 96:128],
                                              ALU.add),
          reads=["colsF"], writes=["tcol"])
        E("scalar", lambda s: s.activation(bgcol[:], tcol[:], AF.Exp),
          reads=["tcol"], writes=["bgcol"])
        for kh in range(NKH):
            kp = kh % 2
            ksl = cvb[:, 2048 + kh * 128:2048 + (kh + 1) * 128]
            qsl = cvb[:, kh * 128:(kh + 1) * 128]
            E("tensor", lambda t, ksl=ksl, kp=kp:
              t.transpose(PSB[0][:, kp * 384:kp * 384 + 128], ksl,
                          L["s_idb"][:]),
              reads=["cvb", "idb"], writes=[("PSB", 0)])
            E("scalar", lambda s, kh=kh, kp=kp:
              s.activation(ktok[kp][:], PSB[0][:, kp * 384:kp * 384 + 128],
                           AF.Copy, scale=rsq[:, 32 + 2 * kh:33 + 2 * kh]),
              reads=[("PSB", 0), "rsq"], writes=[("ktok", kp)])
            for j in range(2):
                vsl = cvb[:, 4096 + (2 * kh + j) * 128:
                          4096 + (2 * kh + j + 1) * 128]
                E("tensor", lambda t, vsl=vsl, j=j, kp=kp:
                  t.transpose(PSB[0][:, kp * 384 + 128 * (j + 1):
                                      kp * 384 + 128 * (j + 2)], vsl,
                              L["s_idb"][:]),
                  reads=["cvb", "idb"], writes=[("PSB", 0)])
                E("scalar", lambda s, j=j, kp=kp:
                  s.activation(vtok[2 * kp + j][:],
                               PSB[0][:, kp * 384 + 128 * (j + 1):
                                       kp * 384 + 128 * (j + 2)], AF.Copy),
                  reads=[("PSB", 0)], writes=[("vtok", 2 * kp + j)])

            for hh in range(2):
                h = 2 * kh + hh
                p = hh
                A, B, Cz = PS[p], PS[2 + p], PS[4 + p]
                rA, rB, rC = ("PS", p), ("PS", 2 + p), ("PS", 4 + p)
                E("tensor", lambda t, ksl=ksl, A=A:
                  t.matmul(A[:, 0:128], ksl, ksl, start=True, stop=True),
                  reads=["cvb"], writes=[rA])
                E("tensor", lambda t, ksl=ksl, qsl=qsl, A=A:
                  t.matmul(A[:, 128:256], ksl, qsl, start=True, stop=True),
                  reads=["cvb"], writes=[rA])
                E("vector", lambda v, h=h, p=p:
                  v.tensor_scalar(dgt[p][0][:], L["s_idf"][:],
                                  colsF[:, h:h + 1], None, ALU.mult),
                  reads=["idf", "colsF"], writes=[("dgt", p, 0)])
                E("tensor", lambda t, p=p, A=A:
                  t.matmul(A[:, 256:384], s_onesf128[:], dgt[p][0][:],
                           start=True, stop=True),
                  reads=["onesf128", ("dgt", p, 0)], writes=[rA])
                E("vector", lambda v, h=h, p=p:
                  v.tensor_scalar(dgt[p][1][:], L["s_idf"][:],
                                  cols2[:, h:h + 1], None, ALU.mult),
                  reads=["idf", "cols2"], writes=[("dgt", p, 1)])
                E("tensor", lambda t, p=p, A=A:
                  t.matmul(A[:, 384:512], s_onesf128[:], dgt[p][1][:],
                           start=True, stop=True),
                  reads=["onesf128", ("dgt", p, 1)], writes=[rA])
                E("vector", lambda v, h=h, p=p, A=A:
                  v.scalar_tensor_tensor(dmin[p][:], A[:, 256:384],
                                         colsF[:, 64 + h:65 + h], s_maskm[:],
                                         ALU.subtract, ALU.add),
                  reads=[rA, "colsF", "maskm"], writes=[("dmin", p)])
                E("scalar", lambda s, p=p:
                  s.activation(dexp[p][:], dmin[p][:], AF.Exp, scale=-1.0),
                  reads=[("dmin", p)], writes=[("dexp", p)])
                E("vector", lambda v, p=p, A=A:
                  v.tensor_tensor(Mn_t[p][:], A[:, 0:128], dexp[p][:],
                                  ALU.mult),
                  reads=[rA, ("dexp", p)], writes=[("Mn", p)])
                E("vector", lambda v, h=h, p=p, A=A:
                  v.scalar_tensor_tensor(dmin[p][:], A[:, 384:512],
                                         colsF[:, h:h + 1], s_maska[:],
                                         ALU.subtract, ALU.add),
                  reads=[rA, "colsF", "maska"], writes=[("dmin", p)])
                E("scalar", lambda s, p=p:
                  s.activation(dexp[p][:], dmin[p][:], AF.Exp),
                  reads=[("dmin", p)], writes=[("dexp", p)])
                E("vector", lambda v, p=p, A=A:
                  v.tensor_tensor(At_t[p][:], A[:, 128:256], dexp[p][:],
                                  ALU.mult),
                  reads=[rA, ("dexp", p)], writes=[("At", p)])
                E("vector", lambda v, p=p:
                  v.tensor_tensor(nbar[p][:], Mn_t[p][:], s_bdneg[:], ALU.mult),
                  reads=[("Mn", p), "bdneg"], writes=[("nbar", p)])
                E("vector", lambda v, p=p:
                  v.tensor_tensor(mln[p][:], Mn_t[p][:], s_bdl[:], ALU.mult),
                  reads=[("Mn", p), "bdl"], writes=[("mln", p)])
                E("vector", lambda v, p=p: v.transpose(napp[p][:], nbar[p][:]),
                  reads=[("nbar", p)], writes=[("napp", p)])
                capp, cnrm = napp[p], nbar[p]
                rapp, rnrm = ("napp", p), ("nbar", p)
                for lv in range(4):
                    E("tensor", lambda t, capp=capp, cnrm=cnrm, B=B:
                      t.matmul(B[:, 0:128], cnrm[:], capp[:],
                               start=True, stop=True),
                      reads=[rapp, rnrm], writes=[rB])
                    E("tensor", lambda t, capp=capp, cnrm=cnrm, B=B:
                      t.matmul(B[:, 128:256], capp[:], cnrm[:],
                               start=True, stop=True),
                      reads=[rapp, rnrm], writes=[rB])
                    E("scalar", lambda s, lv=lv, p=p, B=B:
                      s.activation(Lapp[p][lv][:], B[:, 0:128], AF.Copy),
                      reads=[rB], writes=[("Lapp", p, lv)])
                    E("scalar", lambda s, lv=lv, p=p, B=B:
                      s.activation(Lnrm[p][lv][:], B[:, 128:256], AF.Copy),
                      reads=[rB], writes=[("Lnrm", p, lv)])
                    capp, cnrm = Lapp[p][lv], Lnrm[p][lv]
                    rapp, rnrm = ("Lapp", p, lv), ("Lnrm", p, lv)
                E("vector", lambda v, p=p:
                  v.tensor_tensor(dcur[p][0][:], napp[p][:], s_idb[:], ALU.add),
                  reads=[("napp", p), "idb"], writes=[("dcur", p, 0)])
                dp = 0
                for lv in range(4):
                    E("tensor", lambda t, lv=lv, dp=dp, p=p, B=B:
                      t.matmul(B[:, 256:384], Lnrm[p][lv][:], dcur[p][dp][:],
                               start=True, stop=False),
                      reads=[("Lnrm", p, lv), ("dcur", p, dp)], writes=[rB])
                    E("tensor", lambda t, dp=dp, p=p, B=B:
                      t.matmul(B[:, 256:384], s_idb[:], dcur[p][dp][:],
                               start=False, stop=True),
                      reads=["idb", ("dcur", p, dp)], writes=[rB])
                    E("scalar", lambda s, dp=dp, p=p, B=B:
                      s.activation(dcur[p][1 - dp][:], B[:, 256:384], AF.Copy),
                      reads=[rB], writes=[("dcur", p, 1 - dp)])
                    dp = 1 - dp
                dapp, rdapp = dcur[p][dp], ("dcur", p, dp)
                E("tensor", lambda t, dapp=dapp, p=p, B=B:
                  t.matmul(B[:, 256:384], mln[p][:], dapp[:],
                           start=True, stop=True),
                  reads=[("mln", p), rdapp], writes=[rB])
                E("vector", lambda v, p=p, B=B:
                  v.tensor_scalar(negG[p][:], B[:, 256:384], -1.0, None,
                                  ALU.mult),
                  reads=[rB], writes=[("negG", p)])
                E("vector", lambda v, hh=hh, h=h, p=p, kp=kp:
                  v.tensor_scalar(rhsb[p][:, 0:128], vtok[2 * kp + hh][:],
                                  bbcol[:, h:h + 1], None, ALU.mult),
                  reads=[("vtok", 2 * kp + hh), "bbcol"], writes=[("rhsb", p)])
                E("vector", lambda v, h=h, p=p, kp=kp:
                  v.tensor_scalar(rhsb[p][:, 128:256], ktok[kp][:],
                                  bgcol[:, h:h + 1], None, ALU.mult),
                  reads=[("ktok", kp), "bgcol"], writes=[("rhsb", p)])
                E("tensor", lambda t, dapp=dapp, p=p, Cz=Cz:
                  t.matmul(Cz[:, 0:256], dapp[:], rhsb[p][:],
                           start=True, stop=True),
                  reads=[rdapp, ("rhsb", p)], writes=[rC])
                E("scalar", lambda s, p=p, Cz=Cz:
                  s.activation(y0t[p][:], Cz[:, 0:256], AF.Copy),
                  reads=[rC], writes=[("y0t", p)])
                zcur, rz = y0t[p], ("y0t", p)
                for it in range(3):
                    sl_z = slice(256, 512) if it % 2 == 0 else slice(0, 256)
                    E("tensor", lambda t, zcur=zcur, p=p, Cz=Cz, sl_z=sl_z:
                      t.matmul(Cz[:, sl_z], negG[p][:], zcur[:],
                               start=True, stop=False),
                      reads=[("negG", p), rz], writes=[rC])
                    E("tensor", lambda t, p=p, Cz=Cz, sl_z=sl_z:
                      t.matmul(Cz[:, sl_z], s_idb[:], y0t[p][:],
                               start=False, stop=True),
                      reads=["idb", ("y0t", p)], writes=[rC])
                    E("scalar", lambda s, it=it, p=p, Cz=Cz, sl_z=sl_z:
                      s.activation(zev[p][it % 2][:], Cz[:, sl_z], AF.Copy),
                      reads=[rC], writes=[("zev", p, it % 2)])
                    zcur, rz = zev[p][it % 2], ("zev", p, it % 2)
                Xt, rX = zcur, rz
                E("vector", lambda v, h=h, p=p, kp=kp:
                  v.tensor_scalar(khat[p][:], ktok[kp][:], khcol[:, h:h + 1],
                                  None, ALU.mult),
                  reads=[("ktok", kp), "khcol"], writes=[("khat", p)])
                E("vector", lambda v, h=h, p=p:
                  v.tensor_scalar(dgb[p][:], s_idb[:], grqcol[:, h:h + 1],
                                  None, ALU.mult),
                  reads=["idb", "grqcol"], writes=[("dgb", p)])
                E("tensor", lambda t, p=p, A=A:
                  t.matmul(A[:, 256:384], s_onesb128[:], dgb[p][:],
                           start=True, stop=True),
                  reads=["onesb128", ("dgb", p)], writes=[rA])
                E("tensor", lambda t, Xt=Xt, p=p, B=B:
                  t.matmul(B[:, 384:512], Xt[:, 128:256], At_t[p][:],
                           start=True, stop=True),
                  reads=[rX, ("At", p)], writes=[rB])
                E("vector", lambda v, qsl=qsl, p=p, A=A:
                  v.tensor_tensor(etmp[p][:], qsl, A[:, 256:384], ALU.mult),
                  reads=["cvb", rA], writes=[("etmp", p)])
                E("vector", lambda v, h=h, p=p, B=B:
                  v.tensor_tensor(chst[:, h * 512:h * 512 + 128], etmp[p][:],
                                  B[:, 384:512], ALU.subtract),
                  reads=[("etmp", p), rB], writes=[("chE", h)])
                E("tensor", lambda t, Xt=Xt, p=p, B=B:
                  t.matmul(B[:, 0:128], Xt[:, 0:128], At_t[p][:],
                           start=True, stop=True),
                  reads=[rX, ("At", p)], writes=[rB])
                E("scalar", lambda s, h=h, p=p, B=B:
                  s.activation(chst[:, h * 512 + 128:h * 512 + 256],
                               B[:, 0:128], AF.Copy),
                  reads=[rB], writes=[("chO", h)])
                E("vector", lambda v, h=h, p=p:
                  v.tensor_scalar(gcsel[p][:], s_e127[:], gamcols[:, h:h + 1],
                                  None, ALU.mult),
                  reads=["e127", "gamcols"], writes=[("gcsel", p)])
                E("tensor", lambda t, p=p, B=B:
                  t.matmul(B[:, 256:257], s_onesf128[:], gcsel[p][:],
                           start=True, stop=True),
                  reads=["onesf128", ("gcsel", p)], writes=[rB])
                E("vector", lambda v, p=p, B=B:
                  v.tensor_scalar(gcI[p][:], s_idb[:], B[:, 256:257], None,
                                  ALU.mult),
                  reads=["idb", rB], writes=[("gcI", p)])
                E("tensor", lambda t, Xt=Xt, p=p, B=B:
                  t.matmul(B[:, 128:256], Xt[:, 128:256], khat[p][:],
                           start=True, stop=True),
                  reads=[rX, ("khat", p)], writes=[rB])
                E("vector", lambda v, h=h, p=p, B=B:
                  v.scalar_tensor_tensor(chst[:, h * 512 + 256:h * 512 + 384],
                                         B[:, 128:256], -1.0, gcI[p][:],
                                         ALU.mult, ALU.add),
                  reads=[rB, ("gcI", p)], writes=[("chM", h)])
                E("tensor", lambda t, Xt=Xt, p=p, B=B:
                  t.matmul(B[:, 128:256], khat[p][:], Xt[:, 0:128],
                           start=True, stop=True),
                  reads=[("khat", p), rX], writes=[rB])
                E("scalar", lambda s, h=h, p=p, B=B:
                  s.activation(chst[:, h * 512 + 384:h * 512 + 512],
                               B[:, 128:256], AF.Copy),
                  reads=[rB], writes=[("chB", h)])
                cp = cc8 % 2
                E("tensor", lambda t, h=h, cp=cp, B=B:
                  t.matmul(B[:, 128:256],
                           chst[:, h * 512 + 256:h * 512 + 384],
                           Mco[cp][:, h * 128:(h + 1) * 128],
                           start=True, stop=True),
                  reads=[("chM", h), ("Mco", cp, h)], writes=[rB])
                E("scalar", lambda s, h=h, cp=cp, B=B:
                  s.activation(Mco[1 - cp][:, h * 128:(h + 1) * 128],
                               B[:, 128:256], AF.Copy),
                  reads=[rB], writes=[("Mco", 1 - cp, h)])
                E("tensor", lambda t, h=h, cp=cp, B=B:
                  t.matmul(B[:, 128:256],
                           chst[:, h * 512 + 256:h * 512 + 384],
                           bco[cp][:, h * 128:(h + 1) * 128],
                           start=True, stop=False),
                  reads=[("chM", h), ("bco", cp, h)], writes=[rB])
                E("tensor", lambda t, h=h, B=B:
                  t.matmul(B[:, 128:256], s_idb[:],
                           chst[:, h * 512 + 384:h * 512 + 512],
                           start=False, stop=True),
                  reads=["idb", ("chB", h)], writes=[rB])
                E("scalar", lambda s, h=h, cp=cp, B=B:
                  s.activation(bco[1 - cp][:, h * 128:(h + 1) * 128],
                               B[:, 128:256], AF.Copy),
                  reads=[rB], writes=[("bco", 1 - cp, h)])
        dma("sync", d_chain[cc8], chst[:],
            reads=[("chE", h2) for h2 in range(NVH)] +
                  [("chO", h2) for h2 in range(NVH)] +
                  [("chM", h2) for h2 in range(NVH)] +
                  [("chB", h2) for h2 in range(NVH)],
            writes=[("chainspill", cc8)])


def _p3(nc, em, L):
    E = em.emit
    PS, PSB = L["PS"], L["PSB"]
    Mco, bco, S_pp, Sacc, gbuf = L["Mco"], L["bco"], L["S_pp"], L["Sacc"], L["gbuf"]
    s_idb, s_oh = L["s_idb"], L["s_oh"]
    d_ccin, d_ccout = L["d_ccin"], L["d_ccout"]
    dma = lambda q, dst, src, **kw: E(
        q, lambda e, dst=dst, src=src: e.dma_start(out=dst, in_=src),
        dma=True, **kw)
    pf = NCH % 2
    for h in range(NVH):
        sl = PSB[0][:, (h % 8) * 128:(h % 8 + 1) * 128]
        E("tensor", lambda t, h=h, sl=sl:
          t.transpose(sl, Mco[pf][:, h * 128:(h + 1) * 128], s_idb[:]),
          reads=[("Mco", pf, h), "idb"], writes=[("PSB", 0)])
        E("vector", lambda v, h=h, sl=sl:
          v.tensor_copy(gbuf[0][:, h * 128:(h + 1) * 128], sl),
          reads=[("PSB", 0)], writes=[("mgT", h)])
    dma("sync", d_ccin[:, 0:4096], gbuf[0][:, 0:4096],
        reads=[("mgT", h) for h in range(NVH)], writes=["ccin"])
    dma("sync", d_ccin[:, 4096:8192], bco[pf][:],
        reads=[("bco", pf, h) for h in range(NVH)], writes=["ccin2"])
    cc_tok = E("gpsimd", lambda g2:
               g2.collective_compute(
                   "AllGather", ALU.bypass,
                   replica_groups=[list(range(NCORE))],
                   ins=[d_ccin[:]], outs=[d_ccout[:]]),
               reads=["ccin", "ccin2"], writes=["ccout"])
    E("vector", lambda v: v.memset(S_pp[0][:], 0.0),
      writes=[("Spre", 0, h) for h in range(NVH)])
    E("vector", lambda v: v.memset(Sacc[0][:], 0.0), writes=[("Sacc", 0)],
      deps=[cc_tok])
    for g in range(NCORE):
        p = g % 2
        dma("sync", gbuf[p][:], d_ccout[g], reads=["ccout"],
            writes=[("gbuf", p)])
        E("vector", lambda v, p=p, g=g:
          v.scalar_tensor_tensor(Sacc[1 - p][:], S_pp[p][:], s_oh[:, g:g + 1],
                                 Sacc[p][:], ALU.mult, ALU.add),
          reads=[("Spre", p, h) for h in range(NVH)] + [("Sacc", p), "onehot"],
          writes=[("Sacc", 1 - p)])
        for h in range(NVH):
            E("tensor", lambda t, p=p, h=h:
              t.matmul(PS[h % 4][:, 0:128], gbuf[p][:, h * 128:(h + 1) * 128],
                       S_pp[p][:, h * 128:(h + 1) * 128],
                       start=True, stop=False),
              reads=[("gbuf", p), ("Spre", p, h)], writes=[("PS", h % 4)])
            E("tensor", lambda t, p=p, h=h:
              t.matmul(PS[h % 4][:, 0:128], s_idb[:],
                       gbuf[p][:, 4096 + h * 128:4096 + (h + 1) * 128],
                       start=False, stop=True),
              reads=["idb", ("gbuf", p)], writes=[("PS", h % 4)])
            E("vector", lambda v, p=p, h=h:
              v.tensor_copy(S_pp[1 - p][:, h * 128:(h + 1) * 128],
                            PS[h % 4][:, 0:128]),
              reads=[("PS", h % 4)], writes=[("Spre", 1 - p, h)])
    E("vector", lambda v: v.tensor_copy(S_pp[0][:], Sacc[0][:]),
      reads=[("Sacc", 0)], writes=[("Sst", 0, h) for h in range(NVH)])


def _p5(nc, em, L):
    E = em.emit
    PS, PSB = L["PS"], L["PSB"]
    S_pp, chb, szb, xgst, yst = L["S_pp"], L["chb"], L["szb"], L["xgst"], L["yst"]
    sqt5 = L["sqt5"]
    wob, astg, rmsr, rsq5, nrm5 = (L["wob"], L["astg"], L["rmsr"], L["rsq5"],
                                   L["nrm5"])
    sq_t, s_idb, s_onescb, s_onesr, s_normw = (L["sq_t"], L["s_idb"],
                                               L["s_onescb"], L["s_onesr"],
                                               L["s_normw"])
    s_idf = L["s_idf"]
    s_onesb128, p5dg = L["s_onesb128"], L["p5dg"]
    q_i, q_t1, q_t2, etmp = L["q_i"], L["q_t1"], L["q_t2"], L["etmp"]
    d_chain, d_silz, d_wo, d_attn = (L["d_chain"], L["d_silz"], L["d_wo"],
                                     L["d_attn"])
    dma = lambda q, dst, src, **kw: E(
        q, lambda e, dst=dst, src=src: e.dma_start(out=dst, in_=src),
        dma=True, **kw)
    for cc8 in range(NCH):
        p = cc8 % 2
        dma("sync", chb[p][:], d_chain[cc8], reads=[("chainspill", cc8)],
            writes=[("chb", p)])
        dma("sync", szb[p][:].rearrange("p (c t) -> p c t", c=32),
            d_silz[:, :, 3 + cc8 * C:3 + (cc8 + 1) * C].transpose([1, 0, 2]),
            reads=[("spill", 4)], writes=[("szb", p)])
        for h in range(NVH):
            hsl = slice(h * 128, (h + 1) * 128)
            hp = h % 2
            Po, Ps_ = PS[hp], PS[2 + hp]
            ro, rs = ("PS", hp), ("PS", 2 + hp)
            E("tensor", lambda t, p=p, h=h, hsl=hsl, Po=Po:
              t.matmul(Po[:, 0:128], S_pp[p][:, hsl],
                       chb[p][:, h * 512:h * 512 + 128], start=True, stop=False),
              reads=[("Sst", p, h), ("chb", p)], writes=[ro])
            E("tensor", lambda t, p=p, h=h, Po=Po:
              t.matmul(Po[:, 0:128], s_idb[:],
                       chb[p][:, h * 512 + 128:h * 512 + 256],
                       start=False, stop=True),
              reads=["idb", ("chb", p)], writes=[ro])
            E("tensor", lambda t, p=p, h=h, hsl=hsl, Ps_=Ps_:
              t.matmul(Ps_[:, 0:128], chb[p][:, h * 512 + 256:h * 512 + 384],
                       S_pp[p][:, hsl], start=True, stop=False),
              reads=[("chb", p), ("Sst", p, h)], writes=[rs])
            E("tensor", lambda t, p=p, h=h, Ps_=Ps_:
              t.matmul(Ps_[:, 0:128], s_idb[:],
                       chb[p][:, h * 512 + 384:h * 512 + 512],
                       start=False, stop=True),
              reads=["idb", ("chb", p)], writes=[rs])
            E("scalar", lambda s, p=p, h=h, hsl=hsl, Ps_=Ps_:
              s.activation(S_pp[1 - p][:, hsl], Ps_[:, 0:128], AF.Copy),
              reads=[rs], writes=[("Sst", 1 - p, h)])
            E("vector", lambda v, p=p, h=h, hsl=hsl, Po=Po:
              v.tensor_tensor(xgst[:, hsl], Po[:, 0:128], szb[p][:, hsl],
                              ALU.mult),
              reads=[ro, ("szb", p)], writes=[("xg", h)])
            E("vector", lambda v, h=h, hp=hp, hsl=hsl:
              v.tensor_tensor(sqt5[hp][:], xgst[:, hsl], xgst[:, hsl],
                              ALU.mult),
              reads=[("xg", h)], writes=[("sqt5", hp)])
            E("tensor", lambda t, h=h, hp=hp:
              t.matmul(PS[4][:, h:h + 1], sqt5[hp][:], s_onescb[:],
                       start=True, stop=True),
              reads=[("sqt5", hp), "onescb"], writes=[("PS", 4)])
        E("vector", lambda v:
          v.tensor_scalar(nrm5[:], PS[4][:, 0:64], 128.0 * RMS_EPS, None,
                          ALU.add),
          reads=[("PS", 4)], writes=["nrm5"])
        E("vector", lambda v:
          _quake(v, rsq5[:], nrm5[:], q_i[:, 0:64], q_t1[:, 0:64],
                 q_t2[:, 0:64], L["q_t3"][:, 0:64],
                 final_scale=math.sqrt(128.0)),
          reads=["nrm5"], writes=["rsq5", "qscr"])
        for h in range(NVH):
            hsl = slice(h * 128, (h + 1) * 128)
            hp = h % 2
            E("scalar", lambda s, h=h, hp=hp:
              s.activation(p5dg[hp][:], s_idb[:], AF.Copy,
                           scale=rsq5[:, h:h + 1]),
              reads=["idb", "rsq5"], writes=[("p5dg", hp)])
            E("tensor", lambda t, hp=hp:
              t.matmul(PS[5][:, hp * 256:hp * 256 + 128], s_onesb128[:],
                       p5dg[hp][:], start=True, stop=True),
              reads=["onesb128", ("p5dg", hp)], writes=[("PS", 5)])
            E("vector", lambda v, h=h, hp=hp, hsl=hsl:
              v.tensor_tensor(q_t1[:, 0:128] if hp == 0 else q_t2[:, 0:128],
                              xgst[:, hsl], PS[5][:, hp * 256:hp * 256 + 128],
                              ALU.mult),
              reads=[("xg", h), ("PS", 5)], writes=[("qscr5", hp)])
            ysl = slice(h * 512 + (cc8 % 4) * 128, h * 512 + (cc8 % 4 + 1) * 128)
            E("vector", lambda v, ysl=ysl, hp=hp:
              v.tensor_scalar(yst[:, ysl],
                              q_t1[:, 0:128] if hp == 0 else q_t2[:, 0:128],
                              s_normw[:], None, ALU.mult),
              reads=[("qscr5", hp), "normw"], writes=[("yst", h, cc8 % 4)])
        if cc8 % 4 == 3:
            half = cc8 // 4
            for dc in range(NDC):
                for hh in range(2):
                    wp = (2 * dc + hh) % 2
                    dma("gpsimd",
                        wob[wp][:].rearrange("p (c t) -> p c t", c=16),
                        d_wo[dc, hh * 16:(hh + 1) * 16].transpose([1, 0, 2]),
                        writes=[("wob", wp)])
                    for hv in range(hh * 16, hh * 16 + 16):
                        st, sp = (hv == 0), (hv == NVH - 1)
                        E("tensor", lambda t, hv=hv, st=st, sp=sp, wp=wp:
                          t.matmul(PS[4][:, 0:512],
                                   wob[wp][:, (hv % 16) * 128:
                                           (hv % 16 + 1) * 128],
                                   yst[:, hv * 512:(hv + 1) * 512],
                                   start=st, stop=sp),
                          reads=[("wob", wp)] +
                                [("yst", hv, j) for j in range(4)],
                          writes=[("PS", 4)])
                E("scalar", lambda s: s.activation(astg[:], PS[4][:, 0:512],
                                                   AF.Copy),
                  reads=[("PS", 4)], writes=["astg"])
                dma("sync", d_attn[dc, :, half * 512:(half + 1) * 512], astg[:],
                    reads=["astg"], writes=[("attnspill", dc, half)])


def _p4(nc, em, L):
    E = em.emit
    PS = L["PS"]
    hsl_t, w1b, w3b, w2b, gsb = L["hsl"], L["w1b"], L["w3b"], L["w2b"], L["gsb"]
    xrs, atb, outst = L["xrs"], L["atb"], L["outst"]
    xn, s_h128 = L["xn"], L["s_h128"]
    d_w1, d_w3, d_w2, d_x, d_attn, d_out = (L["d_w1"], L["d_w3"], L["d_w2"],
                                            L["d_x"], L["d_attn"], L["d_out"])
    dma = lambda q, dst, src, **kw: E(
        q, lambda e, dst=dst, src=src: e.dma_start(out=dst, in_=src),
        dma=True, **kw)
    out_toks = []
    for tg in range(2):
        toff = 3 + tg * 512
        slot_of = list(range(64))
        free = [64, 65]
        for hc in range(64):
            wp = hc % 2
            dma("gpsimd", w1b[wp][:].rearrange("p (dc t) -> p dc t", dc=NDC),
                d_w1[hc].transpose([1, 0, 2]), writes=[("w1b", wp)])
            dma("gpsimd", w3b[wp][:].rearrange("p (dc t) -> p dc t", dc=NDC),
                d_w3[hc].transpose([1, 0, 2]), writes=[("w3b", wp)])
            for dc in range(NDC):
                st, sp = (dc == 0), (dc == NDC - 1)
                E("tensor", lambda t, wp=wp, dc=dc, st=st, sp=sp, toff=toff:
                  t.matmul(PS[0][:, 0:512], w1b[wp][:, dc * 128:(dc + 1) * 128],
                           xn[:, dc * TPH + toff:dc * TPH + toff + 512],
                           start=st, stop=sp),
                  reads=[("w1b", wp), "xn"], writes=[("PS", 0)])
                E("tensor", lambda t, wp=wp, dc=dc, st=st, sp=sp, toff=toff:
                  t.matmul(PS[1][:, 0:512], w3b[wp][:, dc * 128:(dc + 1) * 128],
                           xn[:, dc * TPH + toff:dc * TPH + toff + 512],
                           start=st, stop=sp),
                  reads=[("w3b", wp), "xn"], writes=[("PS", 1)])
            E("scalar", lambda s: s.activation(gsb[:], PS[0][:, 0:512], AF.Silu),
              reads=[("PS", 0)], writes=["gsb"])
            sl = slot_of[hc]
            E("vector", lambda v, sl=sl:
              v.tensor_tensor(hsl_t[sl][:], PS[1][:, 0:512], gsb[:], ALU.mult),
              reads=[("PS", 1), "gsb"], writes=[("hs", sl)])
        step = 1
        while step < 64:
            nf = []
            base = 0
            while base < 64:
                for off in range(step):
                    a, b = base + off, base + off + step
                    sa, sb = slot_of[a], slot_of[b]
                    d0, d1 = free[0], free[1]
                    E("vector", lambda v, sa=sa, sb=sb, d0=d0:
                      v.tensor_tensor(hsl_t[d0][:], hsl_t[sa][:], hsl_t[sb][:],
                                      ALU.add),
                      reads=[("hs", sa), ("hs", sb)], writes=[("hs", d0)])
                    E("vector", lambda v, sa=sa, sb=sb, d1=d1:
                      v.tensor_tensor(hsl_t[d1][:], hsl_t[sa][:], hsl_t[sb][:],
                                      ALU.subtract),
                      reads=[("hs", sa), ("hs", sb)], writes=[("hs", d1)])
                    slot_of[a], slot_of[b] = d0, d1
                    free = [sa, sb]
                base += 2 * step
            step *= 2
        for hc in range(64):
            sl = slot_of[hc]
            E("tensor", lambda t, sl=sl:
              t.matmul(PS[2 + (sl % 2)][:, 0:512], s_h128[:], hsl_t[sl][:],
                       start=True, stop=True),
              reads=[("hs", sl), "h128"], writes=[("PS", 2 + (sl % 2))])
            E("vector", lambda v, sl=sl:
              v.tensor_scalar(hsl_t[sl][:], PS[2 + (sl % 2)][:, 0:512],
                              1.0 / math.sqrt(HID), None, ALU.mult),
              reads=[("PS", 2 + (sl % 2))], writes=[("hs", sl)])
        for dc in range(NDC):
            wp = dc % 2
            dma("gpsimd", w2b[wp][:].rearrange("p (c t) -> p c t", c=64),
                d_w2[dc].transpose([1, 0, 2]), writes=[("w2b", wp)])
            dma("sync", xrs[wp][:], d_x[dc * 128:(dc + 1) * 128,
                                        3 + tg * 512:3 + tg * 512 + 512],
                writes=[("xrs", wp)])
            dma("sync", atb[wp][:], d_attn[dc, :, tg * 512:(tg + 1) * 512],
                reads=[("attnspill", dc, tg)], writes=[("atb", wp)])
            for hc in range(64):
                st, sp = (hc == 0), (hc == 63)
                sl = slot_of[hc]
                E("tensor", lambda t, wp=wp, hc=hc, sl=sl, st=st, sp=sp:
                  t.matmul(PS[4][:, 0:512], w2b[wp][:, hc * 128:(hc + 1) * 128],
                           hsl_t[sl][:], start=st, stop=sp),
                  reads=[("w2b", wp), ("hs", sl)], writes=[("PS", 4)])
            E("vector", lambda v, wp=wp:
              v.tensor_tensor(outst[wp][:], PS[4][:, 0:512], xrs[wp][:],
                              ALU.add),
              reads=[("PS", 4), ("xrs", wp)], writes=[("outst", wp)])
            E("vector", lambda v, wp=wp:
              v.tensor_tensor(outst[wp][:], outst[wp][:], atb[wp][:], ALU.add),
              reads=[("atb", wp)], writes=[("outst", wp)])
            tk = dma("sync", d_out[dc * 128:(dc + 1) * 128,
                                   tg * 512:(tg + 1) * 512], outst[wp][:],
                     reads=[("outst", wp)], writes=[("outdone", dc, tg)])
            out_toks.append(tk)
    E("gpsimd", lambda g: g.engine_nop(), deps=out_toks)


def kernel(**inputs):
    per_core = _prep_host(inputs)
    nc = _build()
    res = run_bass_kernel_spmd(nc, per_core, core_ids=list(range(NCORE)))
    outs = [np.asarray(res.results[cid]["out_feat"]) for cid in range(NCORE)]
    full = np.concatenate([o.T for o in outs], axis=0)
    return np.ascontiguousarray(full.reshape(1, NCORE * TPC, DM)).astype(np.float32)
